# version 03
# baseline (speedup 1.0000x reference)
import numpy as np
from contextlib import ExitStack

import concourse.bass as bass
import concourse.mybir as mybir
from concourse.ap import AP as APc
from concourse.bass_utils import run_bass_kernel_spmd

F32 = mybir.dt.float32
BF16 = mybir.dt.bfloat16
I32 = mybir.dt.int32
ALU = mybir.AluOpType
AF = mybir.ActivationFunctionType
AX = mybir.AxisListType
ET = mybir.EngineType
ALL_ENG = [ET.PE, ET.DVE, ET.Activation, ET.Pool, ET.SP]

D = 4096
TOK = 2048
NK = D // 128
NTT = TOK // 512
NCORES = 8
T = 4096
EPS = 1e-5
THR = 0.01
INVTEMP = 10.0


def _tok_rows(core):
    if core % 2 == 0:
        return np.r_[0:1024, 3072:4096]
    return np.r_[1024:3072]


def _prep_oc(w):
    no, nk = w.shape[0] // 128, w.shape[1] // 128
    return np.ascontiguousarray(
        w.reshape(no, 128, nk, 128).transpose(0, 3, 2, 1).astype(np.float32))


def _prep_co(w):
    nk, no = w.shape[0] // 128, w.shape[1] // 128
    return np.ascontiguousarray(
        w.reshape(nk, 128, no, 128).transpose(2, 1, 0, 3).astype(np.float32))


def _col_chunks(b):
    return np.ascontiguousarray(b.reshape(-1, 128).T.astype(np.float32))



class Prog:

    def __init__(self, name):
        self.nc = bass.Bass("TRN2", target_bir_lowering=False, debug=False,
                            name=name)
        self.es = ExitStack()
        self.sems = {}
        self.cnt = {}
        self.inputs = {}
        self.outputs = {}
        self._names = {}

    def sem(self, name):
        if name not in self.sems:
            h = self.es.enter_context(self.nc.semaphore(name))
            self.sems[name] = h
            self.cnt[name] = 0
        return self.sems[name]

    def inc(self, instr, name, n=1):
        self.sem(name)
        self.cnt[name] += n
        instr.then_inc(self.sems[name], n)
        return self.cnt[name]

    def wait(self, eng, name, val=None):
        self.sem(name)
        eng.wait_ge(self.sems[name], self.cnt[name] if val is None else val)

    def dma(self, eng, out, in_, sname, wait=True, **kw):
        self.inc(eng.dma_start(out=out, in_=in_, **kw), sname, 16)
        if wait:
            self.wait(eng, sname)

    def barrier(self):
        self.nc.all_engine_barrier()

    def din(self, name, shape, dtype=F32):
        t = self.nc.dram_tensor(name, list(shape), dtype, kind="ExternalInput")
        self.inputs[name] = [int(s) for s in shape]
        return t.ap()

    def dout(self, name, shape, dtype=F32):
        t = self.nc.dram_tensor(name, list(shape), dtype, kind="ExternalOutput")
        self.outputs[name] = [int(s) for s in shape]
        return t.ap()

    def dscr(self, name, shape, dtype=F32, shared=False):
        t = self.nc.dram_tensor(name, list(shape), dtype, kind="Internal",
                                addr_space="Shared" if shared else "Local")
        return t.ap()

    def sb(self, name, shape, dtype=F32, es=None):
        n = self._names.get(name, 0)
        self._names[name] = n + 1
        if n:
            name = f"{name}_{n}"
        return (es or self.es).enter_context(
            self.nc.sbuf_tensor(name, list(shape), dtype))



def emit_ln_stats(p, bufs, x_hbm, tag="", n_tok=TOK):
    nc = p.nc
    sync, scalar, tensor, vector = nc.sync, nc.scalar, nc.tensor, nc.vector
    nts = n_tok // 512
    xf32 = bufs["xf32"]
    sq = bufs["sq"]
    ones_col = bufs["ones_col"]
    onesrow = bufs["onesrow"]
    rsb, bb = bufs["rsb"], bufs["bb"]
    srow, qrow, trow = bufs["srow"], bufs["qrow"], bufs["trow"]

    with nc.psum_tensor(f"ps_s{tag}", [1, n_tok], F32) as ps_s, \
         nc.psum_tensor(f"ps_q{tag}", [1, n_tok], F32) as ps_q:
        for k in range(NK):
            buf = xf32[k % 2]
            p.wait(sync, "T")
            p.dma(sync, buf[:, :], x_hbm[k * 128:(k + 1) * 128, :], "S",
                  wait=False)
            p.wait(scalar, "S")
            p.wait(scalar, "T")
            p.inc(scalar.activation(out=sq[:, :], in_=buf[:, :],
                                    func=AF.Square), "A", 1)
            p.wait(tensor, "A")
            for tt in range(nts):
                sl = slice(tt * 512, (tt + 1) * 512)
                tensor.matmul(ps_s[0:1, sl], ones_col[:, 0:1], buf[:, sl],
                              start=(k == 0), stop=(k == NK - 1))
                mm = tensor.matmul(ps_q[0:1, sl], ones_col[:, 0:1], sq[:, sl],
                                   start=(k == 0), stop=(k == NK - 1))
            p.inc(mm, "T", 1)

        p.wait(vector, "T")
        vector.tensor_scalar_mul(srow[:, :], ps_s[0:1, :], 1.0 / D)
        p.inc(vector.tensor_scalar_mul(qrow[:, :], ps_q[0:1, :], 1.0 / D),
              "V", 1)
        p.wait(vector, "V")
        p.inc(vector.tensor_tensor(trow[:, :], srow[:, :], srow[:, :],
                                   ALU.mult), "V", 1)
        p.wait(vector, "V")
        p.inc(vector.tensor_sub(qrow[:, :], qrow[:, :], trow[:, :]), "V", 1)
        p.wait(vector, "V")
        p.inc(vector.tensor_scalar_add(qrow[:, :], qrow[:, :], float(EPS)),
              "V", 1)
        p.wait(scalar, "V")
        p.inc(scalar.activation(out=trow[:, :], in_=qrow[:, :], func=AF.Ln),
              "A", 1)
        p.wait(scalar, "A")
        p.inc(scalar.activation(out=qrow[:, :], in_=trow[:, :], func=AF.Exp,
                                scale=-0.5), "A", 1)
        p.wait(vector, "A")
        p.inc(vector.tensor_tensor(srow[:, :], srow[:, :], qrow[:, :],
                                   ALU.mult), "V", 1)
        p.wait(vector, "V")
        p.inc(vector.tensor_scalar_mul(srow[:, :], srow[:, :], -1.0), "V", 1)

    with nc.psum_tensor(f"ps_bA{tag}", [128, n_tok], F32) as ps_bA, \
         nc.psum_tensor(f"ps_bB{tag}", [128, n_tok], F32) as ps_bB:
        p.wait(tensor, "V")
        for tt in range(nts):
            sl = slice(tt * 512, (tt + 1) * 512)
            tensor.matmul(ps_bA[:, sl], onesrow[0:1, :], qrow[0:1, sl],
                          start=True, stop=True)
            mm = tensor.matmul(ps_bB[:, sl], onesrow[0:1, :], srow[0:1, sl],
                               start=True, stop=True)
        p.inc(mm, "T", 1)
        p.wait(vector, "T")
        vector.tensor_copy(rsb[:, :], ps_bA[:, :])
        p.inc(vector.tensor_copy(bb[:, :], ps_bB[:, :]), "V", 1)

    p.barrier()


def emit_ln_norm(p, bufs, x_hbm, xln_sb, consume_f32=None, n_tok=TOK):
    nc = p.nc
    sync, vector = nc.sync, nc.vector
    xf32, sq = bufs["xf32"], bufs["sq"]
    rsb, bb = bufs["rsb"], bufs["bb"]
    for k in range(NK):
        buf = xf32[k % 2]
        p.wait(sync, "V")
        p.dma(sync, buf[:, :], x_hbm[k * 128:(k + 1) * 128, :], "S",
              wait=False)
        p.wait(vector, "S")
        p.wait(vector, "T")
        vector.tensor_tensor(sq[:, :], buf[:, :], rsb[:, :], ALU.mult)
        p.inc(vector.tensor_tensor(xln_sb[:, k * n_tok:(k + 1) * n_tok],
                                   sq[:, :], bb[:, :], ALU.add), "V", 1)
        if consume_f32 is not None:
            p.wait(vector, "V")
            p.inc(vector.tensor_tensor(sq[:, :], sq[:, :], bb[:, :],
                                       ALU.add), "V", 1)
            consume_f32(k)
    p.barrier()


def emit_dots(p, bufs, ps_dots, k):
    nc = p.nc
    tensor = nc.tensor
    sq = bufs["sq"]
    centT_sb = bufs["centT_sb"]
    p.wait(tensor, "V")
    for j in range(16):
        mm = tensor.matmul(
            ps_dots[:, j * 64:(j + 1) * 64],
            sq[:, j * 128:(j + 1) * 128],
            centT_sb[:, k * 64:(k + 1) * 64],
            start=(k == 0 and j in (0, 8)), stop=(k == NK - 1))
    p.inc(mm, "T", 1)


def emit_gate(p, bufs, ps_dots, partials_out, qs_out, tag=""):
    nc = p.nc
    sync, scalar, tensor, vector = nc.sync, nc.scalar, nc.tensor, nc.vector
    e, hits = bufs["e64"], bufs["hits"]
    mx, rowsum = bufs["mx"], bufs["rowsum"]
    qsc = bufs["qsc"]
    ones_col, eye = bufs["ones_col"], bufs["eye_f32"]
    cnt_sb, qsr_sb = bufs["cnt_sb"], bufs["qsr"]

    with nc.psum_tensor(f"ps_cnt{tag}", [1, 64], F32) as ps_cnt:
        for j in range(16):
            reg = ps_dots[:, j * 64:(j + 1) * 64]
            p.wait(vector, "T")
            p.wait(vector, "A")
            p.inc(vector.reduce_max(mx[:, :], reg, axis=AX.X, negate=True),
                  "V", 1)
            p.wait(vector, "V")
            p.inc(vector.tensor_scalar_mul(mx[:, :], mx[:, :], INVTEMP),
                  "V", 1)
            p.wait(scalar, "V")
            p.inc(scalar.activation(out=e[:, :], in_=reg, func=AF.Exp,
                                    bias=mx[:, 0:1], scale=INVTEMP,
                                    accum_out=rowsum[:, :]), "A", 1)
            p.wait(vector, "A")
            p.inc(vector.tensor_scalar_mul(rowsum[:, :], rowsum[:, :], THR),
                  "V", 1)
            p.wait(vector, "V")
            p.inc(vector.tensor_scalar(hits[:, :], e[:, :], rowsum[:, 0:1],
                                       None, ALU.is_gt), "V", 1)
            p.wait(vector, "V")
            p.inc(vector.reduce_max(qsc[:, j:j + 1], hits[:, :], axis=AX.X),
                  "V", 1)
            p.wait(tensor, "V")
            p.inc(tensor.matmul(ps_cnt[0:1, :], ones_col[:, 0:1], hits[:, :],
                                start=(j == 0), stop=(j == 15)), "T", 1)
        p.wait(vector, "T")
        p.inc(vector.tensor_copy(cnt_sb[:, :], ps_cnt[0:1, :]), "V", 1)
        p.wait(sync, "V")
        p.dma(sync, partials_out, cnt_sb[:, :], "S")

    with nc.psum_tensor(f"ps_qt{tag}", [128, 128], F32) as ps_qt:
        p.wait(tensor, "V")
        p.inc(tensor.transpose(ps_qt[0:16, :], qsc[:, :], eye[:, :]), "T", 1)
        p.wait(vector, "T")
        p.inc(vector.tensor_copy(qsr_sb[:, :], ps_qt[0:16, :]), "V", 1)
        p.wait(sync, "V")
        p.dma(sync, qs_out.rearrange("a (b c) -> (a b) c", b=16),
              qsr_sb[:, :], "S")
    p.barrier()


def emit_linear(p, bufs, xsrc_sb, strips_hbm, n_oc, out_cb, n_tok=TOK,
                nk=NK, tag=""):
    nc = p.nc
    g, tensor = nc.gpsimd, nc.tensor
    nts = n_tok // 512
    sbufs = bufs["strips"]
    ev = "EV" + tag
    p.sem(ev)
    with nc.psum_tensor(f"lin_p0{tag}", [128, 512], F32) as p0, \
         nc.psum_tensor(f"lin_p1{tag}", [128, 512], F32) as p1:
        pss = [p0, p1]
        p.dma(g, sbufs[0][:, :], strips_hbm[0], "G", wait=False)
        gtarget = [p.cnt["G"]]
        tload = [None, None]
        step = 0
        for oc in range(n_oc):
            if oc + 1 < n_oc:
                if tload[(oc + 1) % 2] is not None:
                    g.wait_ge(p.sems["T"], tload[(oc + 1) % 2])
                p.dma(g, sbufs[(oc + 1) % 2][:, :], strips_hbm[oc + 1], "G",
                      wait=False)
                gtarget.append(p.cnt["G"])
            sbw = sbufs[oc % 2]
            tensor.wait_ge(p.sems["G"], gtarget[oc])
            for tt in range(nts):
                psu = pss[step % 2]
                if step >= 2:
                    tensor.wait_ge(p.sems[ev], step - 1)
                for k in range(nk):
                    mm = tensor.matmul(
                        psu[:, :],
                        sbw[:, k * 128:(k + 1) * 128],
                        xsrc_sb[:, k * n_tok + tt * 512:
                                k * n_tok + tt * 512 + 512],
                        start=(k == 0), stop=(k == nk - 1))
                p.inc(mm, "T", 1)
                out_cb(oc, tt, psu, step)
                step += 1
            tload[oc % 2] = p.cnt["T"]
    p.barrier()



def build_L1():
    p = Prog("mk_L1")
    nc = p.nc
    sync, scalar, tensor, vector, g = (nc.sync, nc.scalar, nc.tensor,
                                       nc.vector, nc.gpsimd)

    x_hbm = p.din("xT", (D, TOK))
    w0f = p.din("w0f", (32, 128, NK * 128))
    b0f_in = p.din("b0f", (128, 32))
    centT_in = p.din("centT", (NK, 128, 64))
    eye_in = p.din("eye", (128, 128))
    onesrow_in = p.din("onesrow", (1, 128))
    onescol_in = p.din("onescol", (128, 1))

    partials_out = p.dout("partials", (1, 64))
    qs_out = p.dout("qs", (1, TOK))
    dense_out = p.dout("dense", (D, TOK))

    bufs = {}
    bufs["xln"] = p.sb("xln", (128, NK * TOK), BF16)
    bufs["xf32"] = [p.sb("xf32a", (128, TOK)), p.sb("xf32b", (128, TOK))]
    bufs["sq"] = p.sb("sq", (128, TOK))
    bufs["rsb"] = p.sb("rsb", (128, TOK))
    bufs["bb"] = p.sb("bb", (128, TOK))
    bufs["ones_col"] = p.sb("ones_col", (128, 1))
    bufs["onesrow"] = p.sb("onesrow_sb", (1, 128))
    bufs["eye_f32"] = p.sb("eye_sb", (128, 128))
    bufs["b0f_sb"] = p.sb("b0f_sb", (128, 32))

    with p.es, nc.body("main", valid_engines=ALL_ENG):
        p.dma(sync, bufs["ones_col"][:, :], onescol_in, "S", wait=False)
        p.dma(sync, bufs["onesrow"][:, :], onesrow_in, "S", wait=False)
        p.dma(sync, bufs["eye_f32"][:, :], eye_in, "S", wait=False)
        p.dma(sync, bufs["b0f_sb"][:, :], b0f_in, "S", wait=False)

        with ExitStack() as es2:
            for nm, shp, dt in (("srow", (1, TOK), F32), ("qrow", (1, TOK), F32),
                                ("trow", (1, TOK), F32),
                                ("centT_sb", (128, NK * 64), F32),
                                ("e64", (128, 64), F32), ("hits", (128, 64), F32),
                                ("mx", (128, 1), F32), ("rowsum", (128, 1), F32),
                                ("qsc", (128, 16), F32), ("cnt_sb", (1, 64), F32),
                                ("qsr", (16, 128), F32)):
                bufs[nm] = p.sb(nm, shp, dt, es=es2)
            p.dma(sync,
                  bufs["centT_sb"][:, :].rearrange("p (k j) -> p k j", k=NK),
                  centT_in.rearrange("k p j -> p k j"), "S", wait=False)
            p.wait(sync, "S")
            p.barrier()

            emit_ln_stats(p, bufs, x_hbm, tag="1")
            with nc.psum_tensor("ps_dots", [128, 1024], F32) as ps_dots:
                emit_ln_norm(p, bufs, x_hbm, bufs["xln"],
                             consume_f32=lambda k: emit_dots(p, bufs, ps_dots, k))
                emit_gate(p, bufs, ps_dots, partials_out, qs_out, tag="1")

        with ExitStack() as es3:
            bufs["strips"] = [p.sb("stripA", (128, NK * 128), BF16, es=es3),
                              p.sb("stripB", (128, NK * 128), BF16, es=es3)]
            bufs["dense_ev"] = [p.sb("dense_ev0", (128, 512), F32, es=es3),
                                p.sb("dense_ev1", (128, 512), F32, es=es3)]
            estore = {}

            def dense_cb(oc, tt, psu, step):
                evb = bufs["dense_ev"][step % 2]
                p.wait(vector, "T")
                if step >= 2:
                    p.wait(vector, "E", estore[step - 2])
                v = vector.tensor_scalar(evb[:, :], psu[:, :],
                                         bufs["b0f_sb"][:, oc:oc + 1], None,
                                         ALU.add)
                p.inc(v, "EVd1", 1)
                p.wait(scalar, "EVd1")
                p.inc(scalar.dma_start(
                    out=dense_out[oc * 128:(oc + 1) * 128,
                                  tt * 512:(tt + 1) * 512], in_=evb[:, :]),
                    "E", 16)
                estore[step] = p.cnt["E"]

            p.sem("EVd1")
            emit_linear(p, bufs, bufs["xln"], w0f, 32, dense_cb, tag="d1")

            p.wait(sync, "E")
            p.wait(sync, "S")
    return p




PADW = 2052
SEG = [1, 1027]
MAGIC = 0x5A17


def _sap(handle, offset, dims):
    free = int(np.prod(handle.shape[1:]))
    return APc(handle, offset, [[free, int(handle.shape[0])]] + dims)


def emit_csel(p, bufs, partials_in, qs_in, assign_in, tag=""):
    nc = p.nc
    sync, scalar, tensor, vector = nc.sync, nc.scalar, nc.tensor, nc.vector
    parts = bufs["parts_sb"]
    csel = bufs["csel_col"]
    ass_i = bufs["ass_i"]
    ass_f = bufs["ass_f"]
    iota = bufs["iota_sb"]
    oh = bufs["oh_sb"]
    rs_half = bufs["rs_half"]
    qs_row = bufs["qs_row"]
    qs_b = bufs["QS_sb"]
    onesrow = bufs["onesrow"]
    ones_col = bufs["ones_col"]

    p.dma(sync, parts[:, :], partials_in, "S", wait=False)
    p.dma(sync, ass_i[:, :], assign_in, "S", wait=False)
    p.dma(sync, qs_row[:, :], qs_in, "S", wait=False)
    p.wait(sync, "S")
    p.barrier()
    with nc.psum_tensor(f"ps_cs{tag}", [64, 1], F32) as ps_cs, \
         nc.psum_tensor(f"ps_oh{tag}", [64, 512], F32) as ps_oh, \
         nc.psum_tensor(f"ps_rs{tag}", [128, 32], F32) as ps_rs, \
         nc.psum_tensor(f"ps_qb{tag}", [128, TOK], F32) as ps_qb:
        p.inc(tensor.matmul(ps_cs[:, :], parts[:, :], ones_col[0:8, 0:1],
                            start=True, stop=True), "T", 1)
        p.wait(vector, "T")
        p.inc(vector.tensor_scalar(csel[:, :], ps_cs[:, :], 0.0, None,
                                   ALU.is_gt), "V", 1)
        p.inc(vector.tensor_copy(ass_f[:, :], ass_i[:, :]), "V", 1)
        for gg in range(D // 512):
            sl = slice(gg * 512, (gg + 1) * 512)
            p.wait(tensor, "V")
            p.inc(tensor.matmul(ps_oh[:, :], onesrow[0:1, 0:64],
                                ass_f[0:1, sl], start=True, stop=True),
                  "T", 1)
            p.wait(vector, "T")
            p.inc(vector.tensor_scalar(oh[:, sl], ps_oh[:, :],
                                       iota[:, 0:1], None, ALU.is_equal),
                  "V", 1)
        p.wait(tensor, "V")
        for oc in range(32):
            mm = tensor.matmul(ps_rs[:, oc:oc + 1],
                               oh[:, oc * 128:(oc + 1) * 128],
                               csel[:, 0:1], start=(oc == 0), stop=(oc == 31))
        p.inc(mm, "T", 1)
        p.wait(vector, "T")
        p.inc(vector.tensor_scalar_mul(rs_half[:, :], ps_rs[:, :], 0.5),
              "V", 1)
        p.wait(tensor, "V")
        for tt in range(NTT):
            sl = slice(tt * 512, (tt + 1) * 512)
            mm = tensor.matmul(ps_qb[:, sl], onesrow[0:1, :],
                               qs_row[0:1, sl], start=True, stop=True)
        p.inc(mm, "T", 1)
        p.wait(vector, "T")
        p.inc(vector.tensor_copy(qs_b[:, :], ps_qb[:, :]), "V", 1)
    p.barrier()


def emit_maskaccum(p, bufs, dense_in, x_state, tag=""):
    nc = p.nc
    sync, vector, g = nc.sync, nc.vector, nc.gpsimd
    rs_half, qs_b = bufs["rs_half"], bufs["QS_sb"]
    dbuf = bufs["xf32"]
    tbuf = bufs["sq"]
    for oc in range(NK):
        buf = dbuf[oc % 2]
        p.wait(sync, "G")
        p.dma(sync, buf[:, :], dense_in[oc * 128:(oc + 1) * 128, :], "S",
              wait=False)
        p.wait(vector, "S")
        p.wait(vector, "G")
        vector.tensor_scalar(tbuf[:, :], buf[:, :],
                             rs_half[:, oc:oc + 1], None, ALU.mult)
        p.inc(vector.tensor_tensor(tbuf[:, :], tbuf[:, :], qs_b[:, :],
                                   ALU.mult), "V", 1)
        p.wait(g, "V")
        p.dma(g, x_state[oc * 128:(oc + 1) * 128, :], tbuf[:, :], "G",
              wait=False, accum_op=ALU.add)
    p.wait(g, "G")
    p.barrier()


def emit_ln_norm_padded(p, bufs, x_hbm, xlnp, n_tok=TOK):
    nc = p.nc
    sync, vector = nc.sync, nc.vector
    xf32, sq = bufs["xf32"], bufs["sq"]
    rsb, bb = bufs["rsb"], bufs["bb"]
    for k in range(NK):
        buf = xf32[k % 2]
        p.wait(sync, "V")
        p.dma(sync, buf[:, :], x_hbm[k * 128:(k + 1) * 128, :], "S",
              wait=False)
        p.wait(vector, "S")
        vector.tensor_tensor(sq[:, :], buf[:, :], rsb[:, :], ALU.mult)
        out_ap = _sap(xlnp, k * PADW + 1, [[1026, 2], [1, 1024]])
        in_ap = _sap(sq, 0, [[1024, 2], [1, 1024]])
        in2_ap = _sap(bb, 0, [[1024, 2], [1, 1024]])
        p.inc(vector.tensor_tensor(out_ap, in_ap, in2_ap, ALU.add), "V", 1)
    p.barrier()


def emit_halo(p, bufs, xlnp, shA, shB, flA, flB, stamp_in):
    nc = p.nc
    g = nc.gpsimd
    zcol = bufs["zcol_bf"]
    stamp_sb = bufs["stamp_sb"]
    p.inc(g.memset(zcol[:, :], 0.0), "V", 1)
    p.dma(g, stamp_sb[:, :], stamp_in, "G")
    p.wait(g, "V")

    loc = [0, 1023, 1024, 2047]
    lcol = [1, 1024, 1027, 2050]
    me = g.partition_id()
    parity = g.to_reg(me % 2)

    def writes(sh, fl):
        for i, c in enumerate(loc):
            for k in range(NK):
                src = _sap(xlnp, k * PADW + lcol[i], [[1, 1]])
                dst = APc(sh.tensor, i * D + k * 128, [[1, 128], [1, 1]])
                p.cnt["G"] += 16
                g.dma_start(out=dst, in_=src).then_inc(p.sems["G"], 16)
        for k in range(NK):
            dst = APc(sh.tensor, 4 * D + k * 128, [[1, 128], [1, 1]])
            p.cnt["G"] += 16
            g.dma_start(out=dst, in_=zcol[:, 0:1]).then_inc(p.sems["G"], 16)
        g.wait_ge(p.sems["G"], p.cnt["G"])
        p.cnt["G"] += 16
        g.dma_start(out=APc(fl.tensor, 0, [[1, 1], [1, 1]]),
                    in_=stamp_sb[0:1, 0:1]).then_inc(p.sems["G"], 16)
        g.wait_ge(p.sems["G"], p.cnt["G"])

    def poll(fl):
        preg = nc.alloc_register(ET.Pool, p.nc.make_name("poll", True))
        creg = nc.alloc_register(ET.Pool, p.nc.make_name("pcond", True))
        g.reg_mov(creg, 1)
        with g.While(creg):
            g.reg_load(preg, APc(fl.tensor, 0, [[1, 1], [1, 1]]))
            g.reg_alu(creg, preg, MAGIC, ALU.not_equal)

    def reads(srcs):
        halocol = [0, 1025, 1026, 2051]
        for i, (sh, slot) in enumerate(srcs):
            for k in range(NK):
                src = APc(sh.tensor, slot * D + k * 128, [[1, 128], [1, 1]])
                dst = _sap(xlnp, k * PADW + halocol[i], [[1, 1]])
                p.cnt["G"] += 16
                g.dma_start(out=dst, in_=src).then_inc(p.sems["G"], 16)
        g.wait_ge(p.sems["G"], p.cnt["G"])

    gc0 = p.cnt["G"]
    with g.If_eq(parity, 0):
        writes(shA, flA)
        poll(flB)
        reads([(shA, 4), (shB, 0), (shB, 3), (shA, 4)])
    gc_after = p.cnt["G"]
    with g.Else():
        p.cnt["G"] = gc0
        writes(shB, flB)
        poll(flA)
        reads([(shA, 1), (shB, 2), (shB, 1), (shA, 2)])
        assert p.cnt["G"] == gc_after, "halo branch sem counts diverge"
    p.cnt["G"] = gc_after
    p.barrier()


def emit_conv(p, bufs, xlnp, cw_in, qkv_scr):
    nc = p.nc
    g, tensor, vector, scalar = nc.gpsimd, nc.tensor, nc.vector, nc.scalar
    strips = bufs["cstrips"]
    cb_sb = bufs["cb_sb"]
    ev = bufs["conv_ev"]
    base = [1, 513, 1027, 1539]
    n_oc = 24
    p.sem("EVcv")
    estore = {}
    with nc.psum_tensor("cv_p0", [128, 512], F32) as p0, \
         nc.psum_tensor("cv_p1", [128, 512], F32) as p1:
        pss = [p0, p1]
        step = 0
        for oc in range(n_oc):
            p.wait(g, "T")
            for t in range(3):
                p.dma(g, strips[0][t][:, :], cw_in[t, oc], "G", wait=False)
            sb3 = strips[0]
            p.wait(tensor, "G")
            for tt in range(NTT):
                psu = pss[step % 2]
                if step >= 2:
                    tensor.wait_ge(p.sems["EVcv"], step - 1)
                first = True
                for t in range(3):
                    for k in range(NK):
                        mm = tensor.matmul(
                            psu[:, :], sb3[t][:, k * 128:(k + 1) * 128],
                            _sap(xlnp, k * PADW + base[tt] + t - 1,
                                 [[1, 512]]),
                            start=first, stop=(t == 2 and k == NK - 1))
                        first = False
                p.inc(mm, "T", 1)
                evb = ev[step % 2]
                p.wait(vector, "T")
                if step >= 2:
                    p.wait(vector, "E", estore[step - 2])
                v = vector.tensor_scalar(evb[:, :], psu[:, :],
                                         cb_sb[:, oc:oc + 1], None, ALU.add)
                p.inc(v, "EVcv", 1)
                p.wait(scalar, "EVcv")
                p.inc(scalar.dma_start(
                    out=qkv_scr[oc * 128:(oc + 1) * 128,
                                tt * 512:(tt + 1) * 512],
                    in_=evb[:, :]), "E", 16)
                estore[step] = p.cnt["E"]
                step += 1
    p.wait(scalar, "E")
    p.barrier()


def emit_l0attn_blocks(p, bufs, qkv_scr, x_state):
    nc = p.nc
    sync, scalar, tensor, vector, g = (nc.sync, nc.scalar, nc.tensor,
                                       nc.vector, nc.gpsimd)
    wq_sb, wk_sb = bufs["wq_sb"], bufs["wk_sb"]
    wv_sb = bufs["wv_sb"]
    wo_sb = bufs["wo_sb"]
    proj_sb = bufs["proj_sb"]
    bq_sb, bk_sb = bufs["bq_sb"], bufs["bk_sb"]
    bvb_sb = bufs["bvb_sb"]
    bo_sb, pb_sb = bufs["bo_sb"], bufs["pb_sb"]
    tril_sb = bufs["tril_sb"]
    eyeb = bufs["eye_bf"]
    qsb = bufs["q_sb"]; ksb = bufs["k_sb"]; vsb = bufs["v_sb"]
    qh_sb, kh_sb = bufs["qh_sb"], bufs["kh_sb"]
    vh_sb = bufs["vh_sb"]
    sm_sb = bufs["sm_sb"]
    e_sb = bufs["e_sb"]
    et_sb = bufs["et_sb"]
    o_sb = bufs["o_sb"]
    oT_sb = bufs["oT_sb"]
    o2_sb = bufs["o2_sb"]
    mx1, rs1 = bufs["mx"], bufs["rowsum"]
    yev = bufs["yev"]

    SC = 0.125
    for blk in range(16):
        p.wait(sync, "T")
        for ch in range(8):
            for dstbuf, row0 in ((qsb, 0), (ksb, 1024), (vsb, 2048)):
                p.dma(sync, dstbuf[:, ch * 128:(ch + 1) * 128],
                      qkv_scr[row0 + ch * 128:row0 + (ch + 1) * 128,
                              blk * 128:(blk + 1) * 128], "S", wait=False)
        p.wait(sync, "S")
        p.barrier()
        with nc.psum_tensor(f"qh_ps{blk}", [128, 128], F32) as qh_ps, \
             nc.psum_tensor(f"kh_ps{blk}", [128, 128], F32) as kh_ps, \
             nc.psum_tensor(f"vh_ps{blk}", [128, 1024], F32) as vh_ps:
            for hp in range(8):
                for c in range(8):
                    tensor.matmul(qh_ps[:, :],
                                  wq_sb[:, (hp * 8 + c) * 128:
                                        (hp * 8 + c) * 128 + 128],
                                  qsb[:, c * 128:(c + 1) * 128],
                                  start=(c == 0), stop=(c == 7))
                    mm = tensor.matmul(kh_ps[:, :],
                                       wk_sb[:, (hp * 8 + c) * 128:
                                             (hp * 8 + c) * 128 + 128],
                                       ksb[:, c * 128:(c + 1) * 128],
                                       start=(c == 0), stop=(c == 7))
                p.inc(mm, "T", 1)
                p.wait(vector, "T")
                vector.tensor_scalar(qh_sb[:, hp * 128:(hp + 1) * 128],
                                     qh_ps[:, :], bq_sb[:, hp:hp + 1], SC,
                                     ALU.add, ALU.mult)
                p.inc(vector.tensor_scalar(
                    kh_sb[:, hp * 128:(hp + 1) * 128], kh_ps[:, :],
                    bk_sb[:, hp:hp + 1], None, ALU.add), "V", 1)
                p.wait(tensor, "V")
            for c in range(8):
                for nn in range(2):
                    mm = tensor.matmul(
                        vh_ps[:, nn * 512:(nn + 1) * 512],
                        vsb[:, c * 128:(c + 1) * 128],
                        wv_sb[:, c * 1024 + nn * 512:c * 1024 + nn * 512 + 512],
                        start=(c == 0), stop=(c == 7))
            p.inc(mm, "T", 1)
            p.wait(vector, "T")
            p.inc(vector.tensor_tensor(vh_sb[:, :], vh_ps[:, :],
                                       bvb_sb[:, :], ALU.add), "V", 1)
            p.wait(tensor, "V")
        with nc.psum_tensor(f"s_ps{blk}", [128, 128], F32) as s_ps, \
             nc.psum_tensor(f"et_ps{blk}", [128, 128], BF16) as et_ps, \
             nc.psum_tensor(f"pv_ps{blk}", [128, 64], F32) as pv_ps:
            for hh in range(16):
                hp, hi = hh // 2, hh % 2
                lq = qh_sb[hi * 64:hi * 64 + 64, hp * 128:(hp + 1) * 128]
                lk = kh_sb[hi * 64:hi * 64 + 64, hp * 128:(hp + 1) * 128]
                p.inc(tensor.matmul(s_ps[:, :], lq, lk, start=True,
                                    stop=True), "T", 1)
                p.wait(vector, "T")
                p.inc(vector.tensor_tensor(sm_sb[:, :], s_ps[:, :],
                                           tril_sb[:, :], ALU.add), "V", 1)
                p.wait(vector, "V")
                p.inc(vector.reduce_max(mx1[:, :], sm_sb[:, :], axis=AX.X,
                                        negate=True), "V", 1)
                p.wait(scalar, "V")
                p.inc(scalar.activation(out=e_sb[:, :], in_=sm_sb[:, :],
                                        func=AF.Exp, bias=mx1[:, 0:1],
                                        scale=1.0, accum_out=rs1[:, :]),
                      "A", 1)
                p.wait(vector, "A")
                p.inc(vector.reciprocal(rs1[:, :], rs1[:, :]), "V", 1)
                p.wait(tensor, "A")
                p.inc(tensor.transpose(et_ps[:, :], e_sb[:, :], eyeb[:, :]),
                      "T", 1)
                p.wait(vector, "T")
                p.inc(vector.tensor_copy(et_sb[:, :], et_ps[:, :]), "V", 1)
                p.wait(tensor, "V")
                p.inc(tensor.matmul(pv_ps[:, :], et_sb[:, :],
                                    vh_sb[:, hh * 64:(hh + 1) * 64],
                                    start=True, stop=True), "T", 1)
                p.wait(vector, "T")
                p.inc(vector.tensor_scalar(o_sb[:, hh * 64:(hh + 1) * 64],
                                           pv_ps[:, :], rs1[:, 0:1], None,
                                           ALU.mult), "V", 1)
                p.wait(tensor, "V")
        with nc.psum_tensor(f"ot_ps{blk}", [128, 128], BF16) as ot_ps:
            for c in range(8):
                p.wait(tensor, "V")
                p.inc(tensor.transpose(ot_ps[:, :],
                                       o_sb[:, c * 128:(c + 1) * 128],
                                       eyeb[:, :]), "T", 1)
                p.wait(vector, "T")
                p.inc(vector.tensor_copy(oT_sb[:, c * 128:(c + 1) * 128],
                                         ot_ps[:, :]), "V", 1)
        with nc.psum_tensor(f"o2_ps{blk}", [128, 128], F32) as o2_ps:
            for wc in range(8):
                p.wait(tensor, "V")
                for c in range(8):
                    mm = tensor.matmul(o2_ps[:, :],
                                       wo_sb[:, (wc * 8 + c) * 128:
                                             (wc * 8 + c) * 128 + 128],
                                       oT_sb[:, c * 128:(c + 1) * 128],
                                       start=(c == 0), stop=(c == 7))
                p.inc(mm, "T", 1)
                p.wait(vector, "T")
                p.inc(vector.tensor_scalar(o2_sb[:, wc * 128:(wc + 1) * 128],
                                           o2_ps[:, :], bo_sb[:, wc:wc + 1],
                                           None, ALU.add), "V", 1)
        with nc.psum_tensor(f"y_ps0{blk}", [128, 128], F32) as y0, \
             nc.psum_tensor(f"y_ps1{blk}", [128, 128], F32) as y1:
            ys = [y0, y1]
            for oc in range(32):
                psu = ys[oc % 2]
                if oc >= 2:
                    p.wait(tensor, "Yev")
                p.wait(tensor, "V")
                for c in range(8):
                    mm = tensor.matmul(psu[:, :],
                                       proj_sb[:, (oc * 8 + c) * 128:
                                               (oc * 8 + c) * 128 + 128],
                                       o2_sb[:, c * 128:(c + 1) * 128],
                                       start=(c == 0), stop=(c == 7))
                p.inc(mm, "T", 1)
                p.wait(vector, "T")
                p.wait(vector, "G")
                v = vector.tensor_scalar(yev[:, :], psu[:, :],
                                         pb_sb[:, oc:oc + 1], None, ALU.add)
                p.inc(v, "Yev", 1)
                p.wait(g, "Yev")
                p.dma(g, x_state[oc * 128:(oc + 1) * 128,
                                 blk * 128:(blk + 1) * 128],
                      yev[:, :], "G", wait=True, accum_op=ALU.add)
    p.wait(g, "G")
    p.barrier()


def build_L2():
    p = Prog("mk_L2")
    nc = p.nc
    sync, scalar, tensor, vector, g = (nc.sync, nc.scalar, nc.tensor,
                                       nc.vector, nc.gpsimd)
    p.sem("Yev")

    x_in = p.din("xT", (D, TOK))
    parts_in = p.din("partials1", (8, 64))
    qs_in = p.din("qs1", (1, TOK))
    dense_in = p.din("dense1", (D, TOK))
    assign_in = p.din("assign", (1, D), I32)
    iota_in = p.din("iota64", (64, 1))
    eye_in = p.din("eye", (128, 128))
    onesrow_in = p.din("onesrow", (1, 128))
    onescol_in = p.din("onescol", (128, 1))
    cw_in = p.din("cw", (3, 24, 128, NK * 128))
    cb_in = p.din("cb", (128, 24))
    wq_in = p.din("wq0", (8, 128, 1024))
    wk_in = p.din("wk0", (8, 128, 1024))
    wv_in = p.din("wv0r", (8, 128, 1024))
    wo_in = p.din("wo0", (8, 128, 1024))
    proj_in = p.din("proj0", (32, 128, 1024))
    bq_in = p.din("bq0", (128, 8))
    bk_in = p.din("bk0", (128, 8))
    bvb_in = p.din("bv0b", (128, 1024))
    bo_in = p.din("bo0", (128, 8))
    pb_in = p.din("pb0", (128, 32))
    tril_in = p.din("trilm", (128, 128))
    w0f = p.din("w0f", (32, 128, NK * 128))
    b0f_in = p.din("b0f", (128, 32))
    centT_in = p.din("centT", (NK, 128, 64))
    stamp_in = p.din("stamp", (1, 1), I32)

    x_state = p.dout("x2", (D, TOK))
    partials_out = p.dout("partials3", (1, 64))
    qs_out = p.dout("qs3", (1, TOK))
    dense_out = p.dout("dense3", (D, TOK))

    qkv_scr = p.dscr("qkv_scr", (3072, TOK), BF16)
    shA = p.dscr("sh_haloA", (5, D), BF16, shared=True)
    shB = p.dscr("sh_haloB", (5, D), BF16, shared=True)
    flA = p.dscr("sh_flagA", (1, 4), I32, shared=True)
    flB = p.dscr("sh_flagB", (1, 4), I32, shared=True)

    import os
    STAGE = int(os.environ.get("L2STAGE", "3"))
    DBG = os.environ.get("L2DBG", "") == "1"
    if DBG:
        dbg_qkv = p.dout("dbg_qkv", (3072, TOK), F32)

    bufs = {}
    bufs["xf32"] = [p.sb("xf32a", (128, TOK)), p.sb("xf32b", (128, TOK))]
    bufs["sq"] = p.sb("sq", (128, TOK))
    bufs["rsb"] = p.sb("rsb", (128, TOK))
    bufs["bb"] = p.sb("bb", (128, TOK))
    bufs["ones_col"] = p.sb("ones_col", (128, 1))
    bufs["onesrow"] = p.sb("onesrow_sb", (1, 128))
    bufs["eye_f32"] = p.sb("eye_sb", (128, 128))
    bufs["rs_half"] = p.sb("rs_half", (128, 32))
    bufs["b0f_sb"] = p.sb("b0f_sb", (128, 32))

    with p.es, nc.body("main", valid_engines=ALL_ENG):
        p.dma(sync, bufs["ones_col"][:, :], onescol_in, "S", wait=False)
        p.dma(sync, bufs["onesrow"][:, :], onesrow_in, "S", wait=False)
        p.dma(sync, bufs["eye_f32"][:, :], eye_in, "S", wait=False)
        p.dma(sync, bufs["b0f_sb"][:, :], b0f_in, "S", wait=False)
        p.dma(sync, x_state, x_in, "S", wait=False)
        p.wait(sync, "S")
        p.barrier()

        with ExitStack() as esc:
            for nm, shp, dt in (("parts_sb", (8, 64), F32),
                                ("csel_col", (64, 1), F32),
                                ("ass_i", (1, D), I32), ("ass_f", (1, D), F32),
                                ("iota_sb", (64, 1), F32),
                                ("oh_sb", (64, D), F32),
                                ("qs_row", (1, TOK), F32),
                                ("QS_sb", (128, TOK), F32)):
                bufs[nm] = p.sb(nm, shp, dt, es=esc)
            p.dma(sync, bufs["iota_sb"][:, :], iota_in, "S", wait=False)
            emit_csel(p, bufs, parts_in, qs_in, assign_in, tag="c1")
            emit_maskaccum(p, bufs, dense_in, x_state, tag="m1")

        if STAGE >= 2:
            with ExitStack() as esl:
                xlnp = p.sb("xlnp", (128, NK * PADW), BF16, es=esl)
                with ExitStack() as esr:
                    for nm, shp, dt in (("srow", (1, TOK), F32),
                                        ("qrow", (1, TOK), F32),
                                        ("trow", (1, TOK), F32)):
                        bufs[nm] = p.sb(nm, shp, dt, es=esr)
                    emit_ln_stats(p, bufs, x_state, tag="2")
                emit_ln_norm_padded(p, bufs, x_state, xlnp)
                with ExitStack() as esh:
                    bufs["zcol_bf"] = p.sb("zcol_bf", (128, 1), BF16, es=esh)
                    bufs["stamp_sb"] = p.sb("stamp_sb", (1, 1), I32, es=esh)
                    emit_halo(p, bufs, xlnp, shA, shB, flA, flB, stamp_in)
                with ExitStack() as esv:
                    bufs["cstrips"] = [
                        [p.sb(f"cst0_{t}", (128, NK * 128), BF16, es=esv)
                         for t in range(3)]]
                    bufs["cb_sb"] = p.sb("cb_sb", (128, 24), F32, es=esv)
                    bufs["conv_ev"] = [p.sb("cev0", (128, 512), BF16, es=esv),
                                       p.sb("cev1", (128, 512), BF16, es=esv)]
                    p.dma(sync, bufs["cb_sb"][:, :], cb_in, "S")
                    emit_conv(p, bufs, xlnp, cw_in, qkv_scr)
                    if DBG:
                        for r in range(24):
                            p.dma(g, dbg_qkv[r * 128:(r + 1) * 128, :],
                                  qkv_scr[r * 128:(r + 1) * 128, :], "G",
                                  wait=False)
                        p.wait(g, "G")
                        p.barrier()
            with ExitStack() as esa:
                for nm, shp, dt, src in (
                        ("wq_sb", (128, 8 * 1024), BF16, wq_in),
                        ("wk_sb", (128, 8 * 1024), BF16, wk_in),
                        ("wv_sb", (128, 8 * 1024), BF16, wv_in),
                        ("wo_sb", (128, 8 * 1024), BF16, wo_in),
                        ("proj_sb", (128, 32 * 1024), BF16, proj_in)):
                    bufs[nm] = p.sb(nm, shp, dt, es=esa)
                    p.dma(g, bufs[nm][:, :].rearrange(
                        "p (a b) -> p a b", b=src.shape[-1]),
                        src.rearrange("a p b -> p a b"), "G", wait=False)
                for nm, shp, src in (("bq_sb", (128, 8), bq_in),
                                     ("bk_sb", (128, 8), bk_in),
                                     ("bvb_sb", (128, 1024), bvb_in),
                                     ("bo_sb", (128, 8), bo_in),
                                     ("pb_sb", (128, 32), pb_in),
                                     ("tril_sb", (128, 128), tril_in)):
                    bufs[nm] = p.sb(nm, shp, F32, es=esa)
                    p.dma(sync, bufs[nm][:, :], src, "S", wait=False)
                bufs["eye_bf"] = p.sb("eye_bf", (128, 128), BF16, es=esa)
                p.dma(g, bufs["eye_bf"][:, :], eye_in, "G", wait=False)
                for nm, shp, dt in (
                        ("q_sb", (128, 1024), BF16), ("k_sb", (128, 1024), BF16),
                        ("v_sb", (128, 1024), BF16),
                        ("qh_sb", (128, 8 * 128), BF16),
                        ("kh_sb", (128, 8 * 128), BF16),
                        ("vh_sb", (128, 1024), BF16),
                        ("sm_sb", (128, 128), F32), ("e_sb", (128, 128), BF16),
                        ("et_sb", (128, 128), BF16),
                        ("o_sb", (128, 1024), BF16),
                        ("oT_sb", (128, 8 * 128), BF16),
                        ("o2_sb", (128, 8 * 128), BF16),
                        ("mx", (128, 1), F32), ("rowsum", (128, 1), F32),
                        ("yev", (128, 128), F32)):
                    bufs[nm] = p.sb(nm, shp, dt, es=esa)
                p.wait(sync, "S")
                p.wait(g, "G")
                p.barrier()
                emit_l0attn_blocks(p, bufs, qkv_scr, x_state)

        if STAGE >= 3:
            with ExitStack() as es2:
                bufs["xln"] = p.sb("xln", (128, NK * TOK), BF16, es=es2)
                with ExitStack() as es2b:
                    for nm, shp, dt in (("srow", (1, TOK), F32),
                                        ("qrow", (1, TOK), F32),
                                        ("trow", (1, TOK), F32)):
                        bufs[nm] = p.sb(nm, shp, dt, es=es2b)
                    emit_ln_stats(p, bufs, x_state, tag="3")
                with ExitStack() as es2c:
                    for nm, shp, dt in (("centT_sb", (128, NK * 64), F32),
                                        ("e64", (128, 64), F32),
                                        ("hits", (128, 64), F32),
                                        ("mx", (128, 1), F32),
                                        ("rowsum", (128, 1), F32),
                                        ("qsc", (128, 16), F32),
                                        ("cnt_sb", (1, 64), F32),
                                        ("qsr", (16, 128), F32)):
                        bufs[nm] = p.sb(nm, shp, dt, es=es2c)
                    p.dma(sync,
                          bufs["centT_sb"][:, :].rearrange(
                              "p (k j) -> p k j", k=NK),
                          centT_in.rearrange("k p j -> p k j"), "S",
                          wait=False)
                    p.wait(sync, "S")
                    p.barrier()
                    with nc.psum_tensor("ps_dots3", [128, 1024],
                                        F32) as ps_dots:
                        emit_ln_norm(p, bufs, x_state, bufs["xln"],
                                     consume_f32=lambda k: emit_dots(
                                         p, bufs, ps_dots, k))
                        emit_gate(p, bufs, ps_dots, partials_out, qs_out,
                                  tag="3")
                with ExitStack() as es3:
                    bufs["strips"] = [
                        p.sb("stripA", (128, NK * 128), BF16, es=es3),
                        p.sb("stripB", (128, NK * 128), BF16, es=es3)]
                    bufs["dense_ev"] = [
                        p.sb("dense_ev0", (128, 512), F32, es=es3),
                        p.sb("dense_ev1", (128, 512), F32, es=es3)]
                    estore = {}

                    def dense_cb(oc, tt, psu, step):
                        evb = bufs["dense_ev"][step % 2]
                        p.wait(vector, "T")
                        if step >= 2:
                            p.wait(vector, "E", estore[step - 2])
                        v = vector.tensor_scalar(evb[:, :], psu[:, :],
                                                 bufs["b0f_sb"][:, oc:oc + 1],
                                                 None, ALU.add)
                        p.inc(v, "EVd3", 1)
                        p.wait(scalar, "EVd3")
                        p.inc(scalar.dma_start(
                            out=dense_out[oc * 128:(oc + 1) * 128,
                                          tt * 512:(tt + 1) * 512],
                            in_=evb[:, :]), "E", 16)
                        estore[step] = p.cnt["E"]

                    p.sem("EVd3")
                    emit_linear(p, bufs, bufs["xln"], w0f, 32, dense_cb,
                                tag="d3")

        p.wait(sync, "E")
        p.wait(sync, "S")
        p.wait(sync, "G")
    return p




MAGIC3 = 0x5A18


def emit_l1ffn(p, bufs, x_state, w12_in, b12_in, w3_in, b3_in, x4_scr,
               tag=""):
    nc = p.nc
    sync, scalar, tensor, vector, g = (nc.sync, nc.scalar, nc.tensor,
                                       nc.vector, nc.gpsimd)
    with ExitStack() as esf:
        xln = p.sb("xlnF" + tag, (128, NK * TOK), BF16, es=esf)
        with ExitStack() as esr:
            for nm in ("srow", "qrow", "trow"):
                bufs[nm] = p.sb(nm + "F" + tag, (1, TOK), F32, es=esr)
            emit_ln_stats(p, bufs, x_state, tag="F" + tag)
        emit_ln_norm(p, bufs, x_state, xln)
        with ExitStack() as esw:
            st1 = p.sb("w12sA" + tag, (128, NK * 128), BF16, es=esw)
            st2 = p.sb("w12sB" + tag, (128, NK * 128), BF16, es=esw)
            b12_sb = p.sb("b12_sb" + tag, (128, 64), F32, es=esw)
            t1 = p.sb("t1_" + tag, (128, 512), F32, es=esw)
            x4ev = [p.sb(f"x4ev{i}_" + tag, (128, 512), BF16, es=esw)
                    for i in range(2)]
            p.dma(sync, b12_sb[:, :], b12_in, "S")
            ev = "EVf" + tag
            p.sem(ev)
            estore = {}
            step = 0
            tstore = {}
            with nc.psum_tensor("f1psa" + tag, [128, 512], F32) as ps1a, \
                 nc.psum_tensor("f2psa" + tag, [128, 512], F32) as ps2a, \
                 nc.psum_tensor("f1psb" + tag, [128, 512], F32) as ps1b, \
                 nc.psum_tensor("f2psb" + tag, [128, 512], F32) as ps2b:
                t1b = x4ev
                for j in range(32):
                    g.wait_ge(p.sems["T"], tstore.get(j - 1, 0))
                    p.dma(g, st1[:, :], w12_in[j], "G", wait=False)
                    p.dma(g, st2[:, :], w12_in[j + 32], "G", wait=False)
                    p.wait(tensor, "G")
                    for tt in range(NTT):
                        ps1 = ps1a if step % 2 == 0 else ps1b
                        ps2 = ps2a if step % 2 == 0 else ps2b
                        if step >= 2:
                            tensor.wait_ge(p.sems[ev], step - 1)
                        for k in range(NK):
                            sl = slice(k * TOK + tt * 512,
                                       k * TOK + tt * 512 + 512)
                            tensor.matmul(ps1[:, :],
                                          st1[:, k * 128:(k + 1) * 128],
                                          xln[:, sl], start=(k == 0),
                                          stop=(k == NK - 1))
                            mm = tensor.matmul(ps2[:, :],
                                               st2[:, k * 128:(k + 1) * 128],
                                               xln[:, sl], start=(k == 0),
                                               stop=(k == NK - 1))
                        p.inc(mm, "T", 1)
                        scalar.wait_ge(p.sems["T"], p.cnt["T"])
                        p.inc(scalar.activation(
                            out=t1[:, :], in_=ps1[:, :], func=AF.Silu,
                            bias=b12_sb[:, j:j + 1], scale=1.0), "A", 1)
                        vector.wait_ge(p.sems["A"], p.cnt["A"])
                        if step >= 2:
                            p.wait(vector, "E", estore[step - 2])
                        evb = x4ev[step % 2]
                        v = vector.scalar_tensor_tensor(
                            evb[:, :], ps2[:, :], b12_sb[:, j + 32:j + 33],
                            t1[:, :], ALU.add, ALU.mult)
                        p.inc(v, ev, 1)
                        scalar.wait_ge(p.sems[ev], p.cnt[ev])
                        p.inc(scalar.dma_start(
                            out=x4_scr[j * 128:(j + 1) * 128,
                                       tt * 512:(tt + 1) * 512],
                            in_=evb[:, :]), "E", 16)
                        estore[step] = p.cnt["E"]
                        step += 1
                    tstore[j] = p.cnt["T"]
            p.wait(scalar, "E")
            p.barrier()
    with ExitStack() as esw3:
        x4r = p.sb("x4r" + tag, (128, NK * TOK), BF16, es=esw3)
        b3_sb = p.sb("b3_sb" + tag, (128, 32), F32, es=esw3)
        bufs["strips"] = [p.sb("w3sA" + tag, (128, NK * 128), BF16, es=esw3),
                          p.sb("w3sB" + tag, (128, NK * 128), BF16, es=esw3)]
        yev = [p.sb(f"yf{i}_" + tag, (128, 512), F32, es=esw3)
               for i in range(2)]
        p.dma(sync, b3_sb[:, :], b3_in, "S", wait=False)
        for k in range(NK):
            p.dma(g, x4r[:, k * TOK:(k + 1) * TOK],
                  x4_scr[k * 128:(k + 1) * 128, :], "G", wait=False)
        p.wait(sync, "S")
        p.wait(g, "G")
        p.barrier()
        gstore = {}
        evn = "EVw3" + tag
        p.sem(evn)

        def w3_cb(oc, tt, psu, step):
            evb = yev[step % 2]
            p.wait(vector, "T")
            if step >= 2:
                p.wait(vector, "G", gstore[step - 2])
            v = vector.tensor_scalar(evb[:, :], psu[:, :],
                                     b3_sb[:, oc:oc + 1], 0.5,
                                     ALU.add, ALU.mult)
            p.inc(v, evn, 1)
            p.wait(g, evn)
            p.inc(g.dma_start(out=x_state[oc * 128:(oc + 1) * 128,
                                          tt * 512:(tt + 1) * 512],
                              in_=evb[:, :], accum_op=ALU.add), "G", 16)
            gstore[step] = p.cnt["G"]

        emit_linear2(p, bufs, x4r, w3_in, 32, w3_cb, free_sem=evn,
                     tag="w3" + tag)
        p.wait(g, "G")
    p.barrier()


def emit_linear2(p, bufs, xsrc_sb, strips_hbm, n_oc, out_cb, free_sem,
                 n_tok=TOK, nk=NK, tag=""):
    nc = p.nc
    g, tensor = nc.gpsimd, nc.tensor
    nts = n_tok // 512
    sbufs = bufs["strips"]
    with nc.psum_tensor(f"lin_p0{tag}", [128, 512], F32) as p0, \
         nc.psum_tensor(f"lin_p1{tag}", [128, 512], F32) as p1:
        pss = [p0, p1]
        p.dma(g, sbufs[0][:, :], strips_hbm[0], "G", wait=False)
        gtarget = [p.cnt["G"]]
        tload = [None, None]
        step = 0
        for oc in range(n_oc):
            if oc + 1 < n_oc:
                if tload[(oc + 1) % 2] is not None:
                    g.wait_ge(p.sems["T"], tload[(oc + 1) % 2])
                p.dma(g, sbufs[(oc + 1) % 2][:, :], strips_hbm[oc + 1], "G",
                      wait=False)
                gtarget.append(p.cnt["G"])
            sbw = sbufs[oc % 2]
            tensor.wait_ge(p.sems["G"], gtarget[oc])
            for tt in range(nts):
                psu = pss[step % 2]
                if step >= 2:
                    tensor.wait_ge(p.sems[free_sem], step - 1)
                for k in range(nk):
                    mm = tensor.matmul(
                        psu[:, :], sbw[:, k * 128:(k + 1) * 128],
                        xsrc_sb[:, k * n_tok + tt * 512:
                                k * n_tok + tt * 512 + 512],
                        start=(k == 0), stop=(k == nk - 1))
                p.inc(mm, "T", 1)
                out_cb(oc, tt, psu, step)
                step += 1
            tload[oc % 2] = p.cnt["T"]
    p.barrier()


def _gq(parity, qb):
    return (qb if qb < 8 else 16 + qb) if parity == 0 else 8 + qb


def _own(gg):
    if gg < 8:
        return 0, gg
    if gg >= 24:
        return 0, gg - 16
    return 1, gg - 8


def emit_l1attn(p, bufs, x_state, wq_in, wk_in, wv_in, wo_in, cos_in,
                sin_in, mask_in, stamp_in, shKA, shKB, shVA, shVB, flA, flB,
                q_scr, myK, myV, att_scr):
    nc = p.nc
    sync, scalar, tensor, vector, g = (nc.sync, nc.scalar, nc.tensor,
                                       nc.vector, nc.gpsimd)
    SC = 1.0 / np.sqrt(128.0)

    with ExitStack() as esa:
        xln = p.sb("xlnA", (128, NK * TOK), BF16, es=esa)
        with ExitStack() as esr:
            for nm in ("srow", "qrow", "trow"):
                bufs[nm] = p.sb(nm + "A1", (1, TOK), F32, es=esr)
            emit_ln_stats(p, bufs, x_state, tag="A1")
        emit_ln_norm(p, bufs, x_state, xln)
        with ExitStack() as esw:
            cos_sb = p.sb("cos_sb", (128, TOK), F32, es=esw)
            sin_sb = p.sb("sin_sb", (128, TOK), F32, es=esw)
            stw = p.sb("l1aw", (128, NK * 128), BF16, es=esw)
            rta = p.sb("rta", (64, 512), F32, es=esw)
            rtb = p.sb("rtb", (64, 512), F32, es=esw)
            rev = [p.sb(f"l1rev{i}", (128, 512), BF16, es=esw)
                   for i in range(2)]
            p.dma(sync, cos_sb[:, :], cos_in, "S", wait=False)
            p.dma(sync, sin_sb[:, :], sin_in, "S", wait=False)
            p.wait(sync, "S")
            p.barrier()
            estore = {}
            step = 0
            for which, w_in, dst, do_rot in (("K", wk_in, myK, True),
                                             ("Q", wq_in, q_scr, True),
                                             ("V", wv_in, myV, False)):
                with nc.psum_tensor(f"l1p{which}", [128, 512], F32) as psA:
                    for oc in range(32):
                        p.wait(g, "T")
                        p.dma(g, stw[:, :], w_in[oc], "G", wait=False)
                        p.wait(tensor, "G")
                        for tt in range(NTT):
                            p.wait(tensor, "V")
                            for k in range(NK):
                                sl = slice(k * TOK + tt * 512,
                                           k * TOK + tt * 512 + 512)
                                mm = tensor.matmul(
                                    psA[:, :],
                                    stw[:, k * 128:(k + 1) * 128],
                                    xln[:, sl], start=(k == 0),
                                    stop=(k == NK - 1))
                            p.inc(mm, "T", 1)
                            vector.wait_ge(p.sems["T"], p.cnt["T"])
                            if step >= 2:
                                p.wait(vector, "E", estore[step - 2])
                            evb = rev[step % 2]
                            sl2 = slice(tt * 512, (tt + 1) * 512)
                            if do_rot:
                                pe, po = psA[0:64, :], psA[64:128, :]
                                ce = cos_sb[0:64, sl2]
                                se = sin_sb[0:64, sl2]
                                vector.tensor_tensor(rta[:, :], pe, ce,
                                                     ALU.mult)
                                vector.tensor_tensor(rtb[:, :], po, se,
                                                     ALU.mult)
                                vector.tensor_sub(evb[0:64, :], rta[:, :],
                                                  rtb[:, :])
                                vector.tensor_tensor(rta[:, :], pe, se,
                                                     ALU.mult)
                                vector.tensor_tensor(rtb[:, :], po, ce,
                                                     ALU.mult)
                                v = vector.tensor_add(evb[64:128, :],
                                                      rta[:, :], rtb[:, :])
                            else:
                                v = vector.tensor_copy(evb[:, :], psA[:, :])
                            p.inc(v, "V", 1)
                            scalar.wait_ge(p.sems["V"], p.cnt["V"])
                            p.inc(scalar.dma_start(
                                out=dst[oc * 128:(oc + 1) * 128,
                                        tt * 512:(tt + 1) * 512],
                                in_=evb[:, :]), "E", 16)
                            estore[step] = p.cnt["E"]
                            step += 1
            p.wait(scalar, "E")
            p.barrier()

    stamp_sb = bufs["stamp_sb"]
    p.dma(g, stamp_sb[:, :], stamp_in, "G")
    me = g.partition_id()
    parity = g.to_reg(me % 2)
    gc0 = p.cnt["G"]

    def shr_copy(dstK, dstV, fl):
        p.cnt["G"] += 16
        g.dma_start(out=dstK, in_=myK).then_inc(p.sems["G"], 16)
        p.cnt["G"] += 16
        g.dma_start(out=dstV, in_=myV).then_inc(p.sems["G"], 16)
        g.wait_ge(p.sems["G"], p.cnt["G"])
        p.cnt["G"] += 16
        g.dma_start(out=APc(fl.tensor, 0, [[1, 1], [1, 1]]),
                    in_=stamp_sb[0:1, 0:1]).then_inc(p.sems["G"], 16)
        g.wait_ge(p.sems["G"], p.cnt["G"])

    def poll3(fl):
        preg = nc.alloc_register(ET.Pool, nc.make_name("poll3", True))
        creg = nc.alloc_register(ET.Pool, nc.make_name("pcond3", True))
        g.reg_mov(creg, 1)
        with g.While(creg):
            g.reg_load(preg, APc(fl.tensor, 0, [[1, 1], [1, 1]]))
            g.reg_alu(creg, preg, MAGIC3, ALU.not_equal)

    with g.If_eq(parity, 0):
        shr_copy(shKA, shVA, flA)
        poll3(flB)
    gc_after = p.cnt["G"]
    with g.Else():
        p.cnt["G"] = gc0
        shr_copy(shKB, shVB, flB)
        poll3(flA)
    p.cnt["G"] = gc_after
    p.barrier()

    my_par = None
    with ExitStack() as esb:
        mask_sb = p.sb("mask_sb", (128, 144 * 128), BF16, es=esb)
        ksA = p.sb("ksA", (128, TOK), BF16, es=esb)
        ksB = p.sb("ksB", (128, TOK), BF16, es=esb)
        vsA = p.sb("vsA", (128, TOK), BF16, es=esb)
        vsB = p.sb("vsB", (128, TOK), BF16, es=esb)
        qt = p.sb("qt", (128, 128), BF16, es=esb)
        ssb = p.sb("ssb", (128, 4096), F32, es=esb)
        estrip = p.sb("estrip", (128, 4096), BF16, es=esb)
        etstrip = p.sb("etstrip", (128, 4096), BF16, es=esb)
        eyeb = p.sb("eyeb3", (128, 128), BF16, es=esb)
        mxc = p.sb("mxc", (128, 1), F32, es=esb)
        rsc = p.sb("rsc", (128, 1), F32, es=esb)
        aev = p.sb("aev", (128, 128), BF16, es=esb)
        for mq in range(16):
            msl = slice(mq * 9, (mq + 1) * 9)
            p.dma(g, mask_sb[:, mq * 9 * 128:(mq + 1) * 9 * 128].rearrange(
                      "p (m k) -> p m k", m=9),
                  mask_in[msl].rearrange("m p k -> p m k"), "G", wait=False)
        p.dma(g, eyeb[:, :], bufs["eye_f32"][:, :] if False else
              bufs["eye_in_ap"], "G", wait=False)
        p.wait(g, "G")
        p.barrier()
        with nc.psum_tensor("s3ps0", [128, 512], F32) as sp0, \
             nc.psum_tensor("s3ps1", [128, 512], F32) as sp1, \
             nc.psum_tensor("et3ps", [128, 128], BF16) as etp, \
             nc.psum_tensor("pv3ps0", [128, 128], F32) as pvp0, \
             nc.psum_tensor("pv3ps1", [128, 128], F32) as pvp1:
            sps = [sp0, sp1]
            pvs = [pvp0, pvp1]
            pvst = 0
            for h in range(32):
                p.wait(sync, "T")
                p.dma(sync, ksA[:, :], shKA[h * 128:(h + 1) * 128, :], "S",
                      wait=False)
                p.dma(sync, ksB[:, :], shKB[h * 128:(h + 1) * 128, :], "S",
                      wait=False)
                for b in range(16):
                    p.dma(sync, vsA[:, b * 128:(b + 1) * 128],
                          shVA[h * 128:(h + 1) * 128,
                               b * 128:(b + 1) * 128], "S", wait=False,
                          transpose=True)
                    p.dma(sync, vsB[:, b * 128:(b + 1) * 128],
                          shVB[h * 128:(h + 1) * 128,
                               b * 128:(b + 1) * 128], "S", wait=False,
                          transpose=True)
                p.wait(tensor, "S")
                for qb in range(16):
                    lo = qb if qb < 8 else qb + 8
                    hi = qb + 8 if qb < 8 else qb + 16
                    nblk = hi + 1
                    p.wait(sync, "T")
                    p.dma(sync, qt[:, :],
                          q_scr[h * 128:(h + 1) * 128,
                                qb * 128:(qb + 1) * 128], "S", wait=False)
                    p.wait(tensor, "S")
                    runs = []
                    gg = 0
                    while gg < nblk:
                        half, lb = _own(gg)
                        run = [gg]
                        while (gg + 1 < nblk and
                               _own(gg + 1)[0] == half and
                               _own(gg + 1)[1] == _own(gg)[1] + 1):
                            gg += 1
                            run.append(gg)
                        runs.append((half, _own(run[0])[1], run))
                        gg += 1
                    spst = 0
                    for half, lb0, run in runs:
                        ks = ksA if half == 0 else ksB
                        for c0 in range(0, len(run), 4):
                            w = min(4, len(run) - c0) * 128
                            psu = sps[spst % 2]
                            if spst >= 2:
                                p.wait(tensor, "V")
                            mm = tensor.matmul(
                                psu[:, 0:w], qt[:, :],
                                ks[:, (lb0 + c0) * 128:(lb0 + c0) * 128 + w],
                                start=True, stop=True)
                            p.inc(mm, "T", 1)
                            p.wait(vector, "T")
                            gfirst = run[c0]
                            for ii in range(w // 128):
                                gcur = gfirst + ii
                                dstsl = slice(gcur * 128, gcur * 128 + 128)
                                srcsl = slice(ii * 128, ii * 128 + 128)
                                if gcur >= lo:
                                    mi = 9 * qb + (gcur - lo)
                                    v = vector.tensor_tensor(
                                        ssb[:, dstsl], psu[:, srcsl],
                                        mask_sb[:, mi * 128:(mi + 1) * 128],
                                        ALU.add)
                                else:
                                    v = vector.tensor_copy(ssb[:, dstsl],
                                                           psu[:, srcsl])
                            p.inc(v, "V", 1)
                            spst += 1
                    klen = nblk * 128
                    p.wait(vector, "V")
                    p.inc(vector.reduce_max(mxc[:, :], ssb[:, 0:klen],
                                            axis=AX.X, negate=True), "V", 1)
                    p.wait(vector, "V")
                    p.inc(vector.tensor_scalar_mul(mxc[:, :], mxc[:, :], SC),
                          "V", 1)
                    p.wait(scalar, "V")
                    p.inc(scalar.activation(out=estrip[:, 0:klen],
                                            in_=ssb[:, 0:klen], func=AF.Exp,
                                            bias=mxc[:, 0:1], scale=SC,
                                            accum_out=rsc[:, :]), "A", 1)
                    p.wait(vector, "A")
                    p.inc(vector.reciprocal(rsc[:, :], rsc[:, :]), "V", 1)
                    pvp = pvs[pvst % 2]
                    pvst += 1
                    for gg2 in range(nblk):
                        p.wait(tensor, "A" if gg2 == 0 else "V")
                        p.inc(tensor.transpose(
                            etp[:, :], estrip[:, gg2 * 128:(gg2 + 1) * 128],
                            eyeb[:, :]), "T", 1)
                        p.wait(vector, "T")
                        p.inc(vector.tensor_copy(
                            etstrip[:, gg2 * 128:(gg2 + 1) * 128],
                            etp[:, :]), "V", 1)
                    p.wait(tensor, "V")
                    for gg2 in range(nblk):
                        half, lb = _own(gg2)
                        vs = vsA if half == 0 else vsB
                        mm = tensor.matmul(
                            pvp[:, :],
                            etstrip[:, gg2 * 128:(gg2 + 1) * 128],
                            vs[:, lb * 128:(lb + 1) * 128],
                            start=(gg2 == 0), stop=(gg2 == nblk - 1))
                    p.inc(mm, "T", 1)
                    p.wait(vector, "T")
                    v = vector.tensor_scalar(aev[:, :], pvp[:, :],
                                             rsc[:, 0:1], None, ALU.mult)
                    p.inc(v, "V", 1)
                    p.wait(scalar, "V")
                    p.inc(scalar.dma_start(
                        out=att_scr[qb * 128:(qb + 1) * 128,
                                    h * 128:(h + 1) * 128],
                        in_=aev[:, :]), "E", 16)
        p.wait(scalar, "E")
        p.barrier()

    with ExitStack() as esc:
        attT = p.sb("attT", (128, NK * TOK), BF16, es=esc)
        bufs["strips"] = [p.sb("wosA", (128, NK * 128), BF16, es=esc),
                          p.sb("wosB", (128, NK * 128), BF16, es=esc)]
        yev = [p.sb(f"yo{i}", (128, 512), F32, es=esc) for i in range(2)]
        for c in range(NK):
            for tt in range(NTT):
                p.inc(scalar.dma_start(
                    out=attT[:, c * TOK + tt * 512:c * TOK + tt * 512 + 512],
                    in_=att_scr[tt * 512:(tt + 1) * 512,
                                c * 128:(c + 1) * 128],
                    transpose=True), "E", 16)
        p.wait(scalar, "E")
        p.barrier()
        gstore = {}
        p.sem("EVwo")

        def wo_cb(oc, tt, psu, step):
            evb = yev[step % 2]
            p.wait(vector, "T")
            if step >= 2:
                p.wait(vector, "G", gstore[step - 2])
            v = vector.tensor_copy(evb[:, :], psu[:, :])
            p.inc(v, "EVwo", 1)
            p.wait(g, "EVwo")
            p.inc(g.dma_start(out=x_state[oc * 128:(oc + 1) * 128,
                                          tt * 512:(tt + 1) * 512],
                              in_=evb[:, :], accum_op=ALU.add), "G", 16)
            gstore[step] = p.cnt["G"]

        emit_linear2(p, bufs, attT, wo_in, 32, wo_cb, free_sem="EVwo",
                     tag="wo")
        p.wait(g, "G")
    p.barrier()


def build_L3():
    p = Prog("mk_L3")
    nc = p.nc
    sync, scalar, tensor, vector, g = (nc.sync, nc.scalar, nc.tensor,
                                       nc.vector, nc.gpsimd)

    x_in = p.din("xT3", (D, TOK))
    parts_in = p.din("partials3", (8, 64))
    qs_in = p.din("qs3", (1, TOK))
    dense_in = p.din("dense3", (D, TOK))
    assign_in = p.din("assign", (1, D), I32)
    iota_in = p.din("iota64", (64, 1))
    eye_in = p.din("eye", (128, 128))
    onesrow_in = p.din("onesrow", (1, 128))
    onescol_in = p.din("onescol", (128, 1))
    w12_in = p.din("w12p", (64, 128, NK * 128))
    b12_in = p.din("b12c", (128, 64))
    w3_in = p.din("w3p", (32, 128, NK * 128))
    b3_in = p.din("b3c", (128, 32))
    wq_in = p.din("wqp", (32, 128, NK * 128))
    wk_in = p.din("wkp", (32, 128, NK * 128))
    wv_in = p.din("wvp", (32, 128, NK * 128))
    wo_in = p.din("wop", (32, 128, NK * 128))
    cos_in = p.din("cosb", (128, TOK))
    sin_in = p.din("sinb", (128, TOK))
    mask_in = p.din("maskb", (144, 128, 128), BF16)
    stamp_in = p.din("stamp3", (1, 1), I32)

    x_state = p.dout("xout", (D, TOK))

    x4_scr = p.dscr("x4_scr", (D, TOK), BF16)
    q_scr = p.dscr("q_scr", (D, TOK), BF16)
    myK = p.dscr("myK", (D, TOK), BF16)
    myV = p.dscr("myV", (D, TOK), BF16)
    att_scr = p.dscr("att_scr", (TOK, D), BF16)
    shKA = p.dscr("shKA", (D, TOK), BF16, shared=True)
    shKB = p.dscr("shKB", (D, TOK), BF16, shared=True)
    shVA = p.dscr("shVA", (D, TOK), BF16, shared=True)
    shVB = p.dscr("shVB", (D, TOK), BF16, shared=True)
    flA = p.dscr("flA3", (1, 4), I32, shared=True)
    flB = p.dscr("flB3", (1, 4), I32, shared=True)

    import os
    STAGE = int(os.environ.get("L3STAGE", "4"))

    bufs = {}
    bufs["xf32"] = [p.sb("xf32a", (128, TOK)), p.sb("xf32b", (128, TOK))]
    bufs["sq"] = p.sb("sq", (128, TOK))
    bufs["rsb"] = p.sb("rsb", (128, TOK))
    bufs["bb"] = p.sb("bb", (128, TOK))
    bufs["ones_col"] = p.sb("ones_col", (128, 1))
    bufs["onesrow"] = p.sb("onesrow_sb", (1, 128))
    bufs["rs_half"] = p.sb("rs_half", (128, 32))
    bufs["stamp_sb"] = p.sb("stamp_sb", (1, 1), I32)
    bufs["eye_in_ap"] = eye_in

    with p.es, nc.body("main", valid_engines=ALL_ENG):
        p.dma(sync, bufs["ones_col"][:, :], onescol_in, "S", wait=False)
        p.dma(sync, bufs["onesrow"][:, :], onesrow_in, "S", wait=False)
        p.dma(sync, x_state, x_in, "S", wait=False)
        p.wait(sync, "S")
        p.barrier()

        with ExitStack() as esc:
            for nm, shp, dt in (("parts_sb", (8, 64), F32),
                                ("csel_col", (64, 1), F32),
                                ("ass_i", (1, D), I32),
                                ("ass_f", (1, D), F32),
                                ("iota_sb", (64, 1), F32),
                                ("oh_sb", (64, D), F32),
                                ("qs_row", (1, TOK), F32),
                                ("QS_sb", (128, TOK), F32)):
                bufs[nm] = p.sb(nm, shp, dt, es=esc)
            p.dma(sync, bufs["iota_sb"][:, :], iota_in, "S", wait=False)
            emit_csel(p, bufs, parts_in, qs_in, assign_in, tag="c3")
            emit_maskaccum(p, bufs, dense_in, x_state, tag="m3")

        if STAGE >= 2:
            emit_l1ffn(p, bufs, x_state, w12_in, b12_in, w3_in, b3_in,
                       x4_scr, tag="f4")
        if STAGE >= 3:
            emit_l1attn(p, bufs, x_state, wq_in, wk_in, wv_in, wo_in,
                        cos_in, sin_in, mask_in, stamp_in, shKA, shKB,
                        shVA, shVB, flA, flB, q_scr, myK, myV, att_scr)
        if STAGE >= 4:
            emit_l1ffn(p, bufs, x_state, w12_in, b12_in, w3_in, b3_in,
                       x4_scr, tag="f6")

        p.wait(sync, "E")
        p.wait(sync, "S")
        p.wait(sync, "G")
        p.wait(sync, "V")
        p.wait(sync, "T")
    return p


def _host_prep3(inp, core):
    h = {}
    h["w12p"] = _prep_oc(np.asarray(inp["L1_ffn_w12"], np.float32))
    h["b12c"] = _col_chunks(np.asarray(inp["L1_ffn_b12"], np.float32))
    h["w3p"] = _prep_oc(np.asarray(inp["L1_ffn_w3"], np.float32))
    h["b3c"] = _col_chunks(np.asarray(inp["L1_ffn_b3"], np.float32))
    perm = np.concatenate([hh * 128 + np.r_[np.arange(0, 128, 2),
                                            np.arange(1, 128, 2)]
                           for hh in range(32)])
    wq = np.asarray(inp["L1_attn_wq"], np.float32)[perm]
    wk = np.asarray(inp["L1_attn_wk"], np.float32)[perm]
    h["wqp"] = _prep_oc(wq)
    h["wkp"] = _prep_oc(wk)
    h["wvp"] = _prep_oc(np.asarray(inp["L1_attn_wv"], np.float32))
    h["wop"] = _prep_oc(np.asarray(inp["L1_attn_wo"], np.float32))
    return h


def _host_core3(core):
    import ml_dtypes
    rows = _tok_rows(core)
    t = rows.astype(np.float64)
    cos = np.cos(t).astype(np.float32)
    sin = np.sin(t).astype(np.float32)
    cosb = np.ascontiguousarray(np.broadcast_to(cos[None, :], (128, TOK)))
    sinb = np.ascontiguousarray(np.broadcast_to(sin[None, :], (128, TOK)))
    parity = core % 2
    tril = np.tril(np.ones((128, 128), np.float32))
    m_none = np.zeros((128, 128), np.float32)
    m_tril = np.where(tril > 0, 0.0, -1e9).astype(np.float32)
    m_full = np.full((128, 128), -1e9, np.float32)
    tiles = np.zeros((144, 128, 128), np.float32)
    for qb in range(16):
        lo = qb if qb < 8 else qb + 8
        gq = _gq(parity, qb)
        for i in range(9):
            gcur = lo + i
            if gcur < gq:
                tiles[9 * qb + i] = m_none
            elif gcur == gq:
                tiles[9 * qb + i] = m_tril
            else:
                tiles[9 * qb + i] = m_full
    maskb = tiles.astype(ml_dtypes.bfloat16)
    return cosb, sinb, maskb



_HOST = {}


def _host_prep(inp):
    h = {}
    h["w0f"] = _prep_oc(np.asarray(inp["L0_ffn_weight"], np.float32))
    h["b0f"] = _col_chunks(np.asarray(inp["L0_ffn_bias"], np.float32))
    cent = np.asarray(inp["L0_ffn_centroids"], np.float32)
    h["centT"] = np.ascontiguousarray(cent.T.reshape(NK, 128, 64))
    h["assign"] = np.asarray(inp["L0_ffn_assignments"], np.int32).reshape(1, D)
    h["eye"] = np.eye(128, dtype=np.float32)
    h["onesrow"] = np.ones((1, 128), np.float32)
    h["onescol"] = np.ones((128, 1), np.float32)
    h["iota64"] = np.arange(64, dtype=np.float32).reshape(64, 1)
    cw = np.asarray(inp["L0_attn_conv_w"], np.float32)
    h["cw"] = np.stack([_prep_co(cw[t]) for t in range(3)])
    h["cb"] = _col_chunks(np.asarray(inp["L0_attn_conv_b"], np.float32))
    h["wq0"] = _prep_co(np.asarray(inp["L0_attn_wq"], np.float32).reshape(1024, 1024))
    h["wk0"] = _prep_co(np.asarray(inp["L0_attn_wk"], np.float32).reshape(1024, 1024))
    h["wv0r"] = np.ascontiguousarray(
        np.asarray(inp["L0_attn_wv"], np.float32).reshape(8, 128, 1024))
    h["wo0"] = _prep_co(np.asarray(inp["L0_attn_wo"], np.float32).reshape(1024, 1024))
    h["proj0"] = _prep_co(np.asarray(inp["L0_attn_proj_w"], np.float32))
    h["bq0"] = _col_chunks(np.asarray(inp["L0_attn_bq"], np.float32).reshape(-1))
    h["bk0"] = _col_chunks(np.asarray(inp["L0_attn_bk"], np.float32).reshape(-1))
    h["bv0b"] = np.ascontiguousarray(np.broadcast_to(
        np.asarray(inp["L0_attn_bv"], np.float32).reshape(1, 1024), (128, 1024)))
    h["bo0"] = _col_chunks(np.asarray(inp["L0_attn_bo"], np.float32))
    h["pb0"] = _col_chunks(np.asarray(inp["L0_attn_proj_b"], np.float32))
    tril = np.tril(np.ones((128, 128), bool))
    h["trilm"] = np.where(tril, 0.0, -1e9).astype(np.float32)
    return h


def kernel(**inp):
    x_full = np.asarray(inp["L0_attn_inputs"], np.float32)
    core_rows = [_tok_rows(c) for c in range(NCORES)]
    xT = [np.ascontiguousarray(x_full[c // 2][core_rows[c]].T)
          for c in range(NCORES)]
    h = _host_prep(inp)
    h3 = _host_prep3(inp, 0)
    percore = [_host_core3(c) for c in range(NCORES)]

    p1 = build_L1()
    in_maps = [{"xT": xT[c], "w0f": h["w0f"], "b0f": h["b0f"],
                "centT": h["centT"], "eye": h["eye"],
                "onesrow": h["onesrow"], "onescol": h["onescol"]}
               for c in range(NCORES)]
    r1 = run_bass_kernel_spmd(p1.nc, in_maps, core_ids=list(range(NCORES)))
    l1 = r1.results
    parts1 = np.concatenate([l1[c]["partials"] for c in range(NCORES)],
                            axis=0)

    p2 = build_L2()
    in_maps = [{"xT": xT[c], "partials1": parts1, "qs1": l1[c]["qs"],
                "dense1": l1[c]["dense"], "assign": h["assign"],
                "iota64": h["iota64"], "eye": h["eye"],
                "onesrow": h["onesrow"], "onescol": h["onescol"],
                "cw": h["cw"], "cb": h["cb"], "wq0": h["wq0"],
                "wk0": h["wk0"], "wv0r": h["wv0r"], "wo0": h["wo0"],
                "proj0": h["proj0"], "bq0": h["bq0"], "bk0": h["bk0"],
                "bv0b": h["bv0b"], "bo0": h["bo0"], "pb0": h["pb0"],
                "trilm": h["trilm"], "w0f": h["w0f"], "b0f": h["b0f"],
                "centT": h["centT"], "stamp": np.array([[MAGIC]], np.int32)}
               for c in range(NCORES)]
    r2 = run_bass_kernel_spmd(p2.nc, in_maps, core_ids=list(range(NCORES)))
    l2 = r2.results
    parts3 = np.concatenate([l2[c]["partials3"] for c in range(NCORES)],
                            axis=0)

    p3 = build_L3()
    in_maps = []
    for c in range(NCORES):
        cosb, sinb, maskb = percore[c]
        in_maps.append(
            {"xT3": l2[c]["x2"], "partials3": parts3, "qs3": l2[c]["qs3"],
             "dense3": l2[c]["dense3"], "assign": h["assign"],
             "iota64": h["iota64"], "eye": h["eye"],
             "onesrow": h["onesrow"], "onescol": h["onescol"],
             "w12p": h3["w12p"], "b12c": h3["b12c"], "w3p": h3["w3p"],
             "b3c": h3["b3c"], "wqp": h3["wqp"], "wkp": h3["wkp"],
             "wvp": h3["wvp"], "wop": h3["wop"], "cosb": cosb,
             "sinb": sinb, "maskb": maskb,
             "stamp3": np.array([[MAGIC3]], np.int32)})
    r3 = run_bass_kernel_spmd(p3.nc, in_maps, core_ids=list(range(NCORES)))

    out = np.empty((4, T, D), np.float32)
    for c in range(NCORES):
        out[c // 2][core_rows[c]] = r3.results[c]["xout"].T
    return out


# version 04
# speedup vs baseline: 1.0944x; 1.0944x over previous
import numpy as np
from contextlib import ExitStack

import concourse.bass as bass
import concourse.mybir as mybir
from concourse.ap import AP as APc
from concourse.bass_utils import run_bass_kernel_spmd

F32 = mybir.dt.float32
BF16 = mybir.dt.bfloat16
I32 = mybir.dt.int32
ALU = mybir.AluOpType
AF = mybir.ActivationFunctionType
AX = mybir.AxisListType
ET = mybir.EngineType
ALL_ENG = [ET.PE, ET.DVE, ET.Activation, ET.Pool, ET.SP]

D = 4096
TOK = 2048
NK = D // 128
NTT = TOK // 512
NCORES = 8
T = 4096
EPS = 1e-5
THR = 0.01
INVTEMP = 10.0


def _tok_rows(core):
    if core % 2 == 0:
        return np.r_[0:1024, 3072:4096]
    return np.r_[1024:3072]


def _prep_oc(w):
    no, nk = w.shape[0] // 128, w.shape[1] // 128
    return np.ascontiguousarray(
        w.reshape(no, 128, nk, 128).transpose(0, 3, 2, 1).astype(np.float32))


def _prep_co(w):
    nk, no = w.shape[0] // 128, w.shape[1] // 128
    return np.ascontiguousarray(
        w.reshape(nk, 128, no, 128).transpose(2, 1, 0, 3).astype(np.float32))


def _col_chunks(b):
    return np.ascontiguousarray(b.reshape(-1, 128).T.astype(np.float32))



class Prog:

    def __init__(self, name):
        self.nc = bass.Bass("TRN2", target_bir_lowering=False, debug=False,
                            name=name)
        self.es = ExitStack()
        self.sems = {}
        self.cnt = {}
        self.inputs = {}
        self.outputs = {}
        self._names = {}

    def sem(self, name):
        if name not in self.sems:
            h = self.es.enter_context(self.nc.semaphore(name))
            self.sems[name] = h
            self.cnt[name] = 0
        return self.sems[name]

    def inc(self, instr, name, n=1):
        self.sem(name)
        self.cnt[name] += n
        instr.then_inc(self.sems[name], n)
        return self.cnt[name]

    def wait(self, eng, name, val=None):
        self.sem(name)
        eng.wait_ge(self.sems[name], self.cnt[name] if val is None else val)

    def dma(self, eng, out, in_, sname, wait=True, **kw):
        self.inc(eng.dma_start(out=out, in_=in_, **kw), sname, 16)
        if wait:
            self.wait(eng, sname)

    def barrier(self):
        self.nc.all_engine_barrier()

    def din(self, name, shape, dtype=F32):
        t = self.nc.dram_tensor(name, list(shape), dtype, kind="ExternalInput")
        self.inputs[name] = [int(s) for s in shape]
        return t.ap()

    def dout(self, name, shape, dtype=F32):
        t = self.nc.dram_tensor(name, list(shape), dtype, kind="ExternalOutput")
        self.outputs[name] = [int(s) for s in shape]
        return t.ap()

    def dscr(self, name, shape, dtype=F32, shared=False):
        t = self.nc.dram_tensor(name, list(shape), dtype, kind="Internal",
                                addr_space="Shared" if shared else "Local")
        return t.ap()

    def sb(self, name, shape, dtype=F32, es=None):
        n = self._names.get(name, 0)
        self._names[name] = n + 1
        if n:
            name = f"{name}_{n}"
        return (es or self.es).enter_context(
            self.nc.sbuf_tensor(name, list(shape), dtype))



def emit_ln_stats(p, bufs, x_hbm, tag="", n_tok=TOK):
    nc = p.nc
    sync, scalar, tensor, vector = nc.sync, nc.scalar, nc.tensor, nc.vector
    nts = n_tok // 512
    xf32 = bufs["xf32"]
    sq = bufs["sq"]
    ones_col = bufs["ones_col"]
    onesrow = bufs["onesrow"]
    rsb, bb = bufs["rsb"], bufs["bb"]
    srow, qrow, trow = bufs["srow"], bufs["qrow"], bufs["trow"]

    with nc.psum_tensor(f"ps_s{tag}", [1, n_tok], F32) as ps_s, \
         nc.psum_tensor(f"ps_q{tag}", [1, n_tok], F32) as ps_q:
        for k in range(NK):
            buf = xf32[k % 2]
            p.wait(sync, "T")
            p.dma(sync, buf[:, :], x_hbm[k * 128:(k + 1) * 128, :], "S",
                  wait=False)
            p.wait(scalar, "S")
            p.wait(scalar, "T")
            p.inc(scalar.activation(out=sq[:, :], in_=buf[:, :],
                                    func=AF.Square), "A", 1)
            p.wait(tensor, "A")
            for tt in range(nts):
                sl = slice(tt * 512, (tt + 1) * 512)
                tensor.matmul(ps_s[0:1, sl], ones_col[:, 0:1], buf[:, sl],
                              start=(k == 0), stop=(k == NK - 1))
                mm = tensor.matmul(ps_q[0:1, sl], ones_col[:, 0:1], sq[:, sl],
                                   start=(k == 0), stop=(k == NK - 1))
            p.inc(mm, "T", 1)

        p.wait(vector, "T")
        vector.tensor_scalar_mul(srow[:, :], ps_s[0:1, :], 1.0 / D)
        p.inc(vector.tensor_scalar_mul(qrow[:, :], ps_q[0:1, :], 1.0 / D),
              "V", 1)
        p.wait(vector, "V")
        p.inc(vector.tensor_tensor(trow[:, :], srow[:, :], srow[:, :],
                                   ALU.mult), "V", 1)
        p.wait(vector, "V")
        p.inc(vector.tensor_sub(qrow[:, :], qrow[:, :], trow[:, :]), "V", 1)
        p.wait(vector, "V")
        p.inc(vector.tensor_scalar_add(qrow[:, :], qrow[:, :], float(EPS)),
              "V", 1)
        p.wait(scalar, "V")
        p.inc(scalar.activation(out=trow[:, :], in_=qrow[:, :], func=AF.Ln),
              "A", 1)
        p.wait(scalar, "A")
        p.inc(scalar.activation(out=qrow[:, :], in_=trow[:, :], func=AF.Exp,
                                scale=-0.5), "A", 1)
        p.wait(vector, "A")
        p.inc(vector.tensor_tensor(srow[:, :], srow[:, :], qrow[:, :],
                                   ALU.mult), "V", 1)
        p.wait(vector, "V")
        p.inc(vector.tensor_scalar_mul(srow[:, :], srow[:, :], -1.0), "V", 1)

    with nc.psum_tensor(f"ps_bA{tag}", [128, n_tok], F32) as ps_bA, \
         nc.psum_tensor(f"ps_bB{tag}", [128, n_tok], F32) as ps_bB:
        p.wait(tensor, "V")
        for tt in range(nts):
            sl = slice(tt * 512, (tt + 1) * 512)
            tensor.matmul(ps_bA[:, sl], onesrow[0:1, :], qrow[0:1, sl],
                          start=True, stop=True)
            mm = tensor.matmul(ps_bB[:, sl], onesrow[0:1, :], srow[0:1, sl],
                               start=True, stop=True)
        p.inc(mm, "T", 1)
        p.wait(vector, "T")
        vector.tensor_copy(rsb[:, :], ps_bA[:, :])
        p.inc(vector.tensor_copy(bb[:, :], ps_bB[:, :]), "V", 1)

    p.barrier()


def emit_ln_norm(p, bufs, x_hbm, xln_sb, consume_f32=None, n_tok=TOK):
    nc = p.nc
    sync, vector = nc.sync, nc.vector
    xf32, sq = bufs["xf32"], bufs["sq"]
    rsb, bb = bufs["rsb"], bufs["bb"]
    for k in range(NK):
        buf = xf32[k % 2]
        p.wait(sync, "V")
        p.dma(sync, buf[:, :], x_hbm[k * 128:(k + 1) * 128, :], "S",
              wait=False)
        p.wait(vector, "S")
        p.wait(vector, "T")
        vector.tensor_tensor(sq[:, :], buf[:, :], rsb[:, :], ALU.mult)
        p.inc(vector.tensor_tensor(xln_sb[:, k * n_tok:(k + 1) * n_tok],
                                   sq[:, :], bb[:, :], ALU.add), "V", 1)
        if consume_f32 is not None:
            p.wait(vector, "V")
            p.inc(vector.tensor_tensor(sq[:, :], sq[:, :], bb[:, :],
                                       ALU.add), "V", 1)
            consume_f32(k)
    p.barrier()


def emit_dots(p, bufs, ps_dots, k):
    nc = p.nc
    tensor = nc.tensor
    sq = bufs["sq"]
    centT_sb = bufs["centT_sb"]
    p.wait(tensor, "V")
    for j in range(16):
        mm = tensor.matmul(
            ps_dots[:, j * 64:(j + 1) * 64],
            sq[:, j * 128:(j + 1) * 128],
            centT_sb[:, k * 64:(k + 1) * 64],
            start=(k == 0 and j in (0, 8)), stop=(k == NK - 1))
    p.inc(mm, "T", 1)


def emit_gate(p, bufs, ps_dots, partials_out, qs_out, tag=""):
    nc = p.nc
    sync, scalar, tensor, vector = nc.sync, nc.scalar, nc.tensor, nc.vector
    e, hits = bufs["e64"], bufs["hits"]
    mx, rowsum = bufs["mx"], bufs["rowsum"]
    qsc = bufs["qsc"]
    ones_col, eye = bufs["ones_col"], bufs["eye_f32"]
    cnt_sb, qsr_sb = bufs["cnt_sb"], bufs["qsr"]

    with nc.psum_tensor(f"ps_cnt{tag}", [1, 64], F32) as ps_cnt:
        for j in range(16):
            reg = ps_dots[:, j * 64:(j + 1) * 64]
            p.wait(vector, "T")
            p.wait(vector, "A")
            p.inc(vector.reduce_max(mx[:, :], reg, axis=AX.X, negate=True),
                  "V", 1)
            p.wait(vector, "V")
            p.inc(vector.tensor_scalar_mul(mx[:, :], mx[:, :], INVTEMP),
                  "V", 1)
            p.wait(scalar, "V")
            p.inc(scalar.activation(out=e[:, :], in_=reg, func=AF.Exp,
                                    bias=mx[:, 0:1], scale=INVTEMP,
                                    accum_out=rowsum[:, :]), "A", 1)
            p.wait(vector, "A")
            p.inc(vector.tensor_scalar_mul(rowsum[:, :], rowsum[:, :], THR),
                  "V", 1)
            p.wait(vector, "V")
            p.inc(vector.tensor_scalar(hits[:, :], e[:, :], rowsum[:, 0:1],
                                       None, ALU.is_gt), "V", 1)
            p.wait(vector, "V")
            p.inc(vector.reduce_max(qsc[:, j:j + 1], hits[:, :], axis=AX.X),
                  "V", 1)
            p.wait(tensor, "V")
            p.inc(tensor.matmul(ps_cnt[0:1, :], ones_col[:, 0:1], hits[:, :],
                                start=(j == 0), stop=(j == 15)), "T", 1)
        p.wait(vector, "T")
        p.inc(vector.tensor_copy(cnt_sb[:, :], ps_cnt[0:1, :]), "V", 1)
        p.wait(sync, "V")
        p.dma(sync, partials_out, cnt_sb[:, :], "S")

    with nc.psum_tensor(f"ps_qt{tag}", [128, 128], F32) as ps_qt:
        p.wait(tensor, "V")
        p.inc(tensor.transpose(ps_qt[0:16, :], qsc[:, :], eye[:, :]), "T", 1)
        p.wait(vector, "T")
        p.inc(vector.tensor_copy(qsr_sb[:, :], ps_qt[0:16, :]), "V", 1)
        p.wait(sync, "V")
        p.dma(sync, qs_out.rearrange("a (b c) -> (a b) c", b=16),
              qsr_sb[:, :], "S")
    p.barrier()


def emit_linear(p, bufs, xsrc_sb, strips_hbm, n_oc, out_cb, n_tok=TOK,
                nk=NK, tag=""):
    nc = p.nc
    g, tensor = nc.gpsimd, nc.tensor
    nts = n_tok // 512
    sbufs = bufs["strips"]
    ev = "EV" + tag
    p.sem(ev)
    with nc.psum_tensor(f"lin_p0{tag}", [128, 512], F32) as p0, \
         nc.psum_tensor(f"lin_p1{tag}", [128, 512], F32) as p1:
        pss = [p0, p1]
        p.dma(g, sbufs[0][:, :], strips_hbm[0], "G", wait=False)
        gtarget = [p.cnt["G"]]
        tload = [None, None]
        step = 0
        for oc in range(n_oc):
            if oc + 1 < n_oc:
                if tload[(oc + 1) % 2] is not None:
                    g.wait_ge(p.sems["T"], tload[(oc + 1) % 2])
                p.dma(g, sbufs[(oc + 1) % 2][:, :], strips_hbm[oc + 1], "G",
                      wait=False)
                gtarget.append(p.cnt["G"])
            sbw = sbufs[oc % 2]
            tensor.wait_ge(p.sems["G"], gtarget[oc])
            for tt in range(nts):
                psu = pss[step % 2]
                if step >= 2:
                    tensor.wait_ge(p.sems[ev], step - 1)
                for k in range(nk):
                    mm = tensor.matmul(
                        psu[:, :],
                        sbw[:, k * 128:(k + 1) * 128],
                        xsrc_sb[:, k * n_tok + tt * 512:
                                k * n_tok + tt * 512 + 512],
                        start=(k == 0), stop=(k == nk - 1))
                p.inc(mm, "T", 1)
                out_cb(oc, tt, psu, step)
                step += 1
            tload[oc % 2] = p.cnt["T"]
    p.barrier()



def build_L1():
    p = Prog("mk_L1")
    nc = p.nc
    sync, scalar, tensor, vector, g = (nc.sync, nc.scalar, nc.tensor,
                                       nc.vector, nc.gpsimd)

    x_hbm = p.din("xT", (D, TOK))
    w0f = p.din("w0f", (32, 128, NK * 128))
    b0f_in = p.din("b0f", (128, 32))
    centT_in = p.din("centT", (NK, 128, 64))
    eye_in = p.din("eye", (128, 128))
    onesrow_in = p.din("onesrow", (1, 128))
    onescol_in = p.din("onescol", (128, 1))

    partials_out = p.dout("partials", (1, 64))
    qs_out = p.dout("qs", (1, TOK))
    dense_out = p.dout("dense", (D, TOK))

    bufs = {}
    bufs["xln"] = p.sb("xln", (128, NK * TOK), BF16)
    bufs["xf32"] = [p.sb("xf32a", (128, TOK)), p.sb("xf32b", (128, TOK))]
    bufs["sq"] = p.sb("sq", (128, TOK))
    bufs["rsb"] = p.sb("rsb", (128, TOK))
    bufs["bb"] = p.sb("bb", (128, TOK))
    bufs["ones_col"] = p.sb("ones_col", (128, 1))
    bufs["onesrow"] = p.sb("onesrow_sb", (1, 128))
    bufs["eye_f32"] = p.sb("eye_sb", (128, 128))
    bufs["b0f_sb"] = p.sb("b0f_sb", (128, 32))

    with p.es, nc.body("main", valid_engines=ALL_ENG):
        p.dma(sync, bufs["ones_col"][:, :], onescol_in, "S", wait=False)
        p.dma(sync, bufs["onesrow"][:, :], onesrow_in, "S", wait=False)
        p.dma(sync, bufs["eye_f32"][:, :], eye_in, "S", wait=False)
        p.dma(sync, bufs["b0f_sb"][:, :], b0f_in, "S", wait=False)

        with ExitStack() as es2:
            for nm, shp, dt in (("srow", (1, TOK), F32), ("qrow", (1, TOK), F32),
                                ("trow", (1, TOK), F32),
                                ("centT_sb", (128, NK * 64), F32),
                                ("e64", (128, 64), F32), ("hits", (128, 64), F32),
                                ("mx", (128, 1), F32), ("rowsum", (128, 1), F32),
                                ("qsc", (128, 16), F32), ("cnt_sb", (1, 64), F32),
                                ("qsr", (16, 128), F32)):
                bufs[nm] = p.sb(nm, shp, dt, es=es2)
            p.dma(sync,
                  bufs["centT_sb"][:, :].rearrange("p (k j) -> p k j", k=NK),
                  centT_in.rearrange("k p j -> p k j"), "S", wait=False)
            p.wait(sync, "S")
            p.barrier()

            emit_ln_stats(p, bufs, x_hbm, tag="1")
            with nc.psum_tensor("ps_dots", [128, 1024], F32) as ps_dots:
                emit_ln_norm(p, bufs, x_hbm, bufs["xln"],
                             consume_f32=lambda k: emit_dots(p, bufs, ps_dots, k))
                emit_gate(p, bufs, ps_dots, partials_out, qs_out, tag="1")

        with ExitStack() as es3:
            bufs["strips"] = [p.sb("stripA", (128, NK * 128), BF16, es=es3),
                              p.sb("stripB", (128, NK * 128), BF16, es=es3)]
            bufs["dense_ev"] = [p.sb("dense_ev0", (128, 512), F32, es=es3),
                                p.sb("dense_ev1", (128, 512), F32, es=es3)]
            estore = {}

            def dense_cb(oc, tt, psu, step):
                evb = bufs["dense_ev"][step % 2]
                p.wait(vector, "T")
                if step >= 2:
                    p.wait(vector, "E", estore[step - 2])
                v = vector.tensor_scalar(evb[:, :], psu[:, :],
                                         bufs["b0f_sb"][:, oc:oc + 1], None,
                                         ALU.add)
                p.inc(v, "EVd1", 1)
                p.wait(scalar, "EVd1")
                p.inc(scalar.dma_start(
                    out=dense_out[oc * 128:(oc + 1) * 128,
                                  tt * 512:(tt + 1) * 512], in_=evb[:, :]),
                    "E", 16)
                estore[step] = p.cnt["E"]

            p.sem("EVd1")
            emit_linear(p, bufs, bufs["xln"], w0f, 32, dense_cb, tag="d1")

            p.wait(sync, "E")
            p.wait(sync, "S")
    return p




PADW = 2052
SEG = [1, 1027]
MAGIC = 0x5A17


def _sap(handle, offset, dims):
    free = int(np.prod(handle.shape[1:]))
    return APc(handle, offset, [[free, int(handle.shape[0])]] + dims)


def emit_csel(p, bufs, partials_in, qs_in, assign_in, tag=""):
    nc = p.nc
    sync, scalar, tensor, vector = nc.sync, nc.scalar, nc.tensor, nc.vector
    parts = bufs["parts_sb"]
    csel = bufs["csel_col"]
    ass_i = bufs["ass_i"]
    ass_f = bufs["ass_f"]
    iota = bufs["iota_sb"]
    oh = bufs["oh_sb"]
    rs_half = bufs["rs_half"]
    qs_row = bufs["qs_row"]
    qs_b = bufs["QS_sb"]
    onesrow = bufs["onesrow"]
    ones_col = bufs["ones_col"]

    p.dma(sync, parts[:, :], partials_in, "S", wait=False)
    p.dma(sync, ass_i[:, :], assign_in, "S", wait=False)
    p.dma(sync, qs_row[:, :], qs_in, "S", wait=False)
    p.wait(sync, "S")
    p.barrier()
    with nc.psum_tensor(f"ps_cs{tag}", [64, 1], F32) as ps_cs, \
         nc.psum_tensor(f"ps_oh{tag}", [64, 512], F32) as ps_oh, \
         nc.psum_tensor(f"ps_rs{tag}", [128, 32], F32) as ps_rs, \
         nc.psum_tensor(f"ps_qb{tag}", [128, TOK], F32) as ps_qb:
        p.inc(tensor.matmul(ps_cs[:, :], parts[:, :], ones_col[0:8, 0:1],
                            start=True, stop=True), "T", 1)
        p.wait(vector, "T")
        p.inc(vector.tensor_scalar(csel[:, :], ps_cs[:, :], 0.0, None,
                                   ALU.is_gt), "V", 1)
        p.inc(vector.tensor_copy(ass_f[:, :], ass_i[:, :]), "V", 1)
        for gg in range(D // 512):
            sl = slice(gg * 512, (gg + 1) * 512)
            p.wait(tensor, "V")
            p.inc(tensor.matmul(ps_oh[:, :], onesrow[0:1, 0:64],
                                ass_f[0:1, sl], start=True, stop=True),
                  "T", 1)
            p.wait(vector, "T")
            p.inc(vector.tensor_scalar(oh[:, sl], ps_oh[:, :],
                                       iota[:, 0:1], None, ALU.is_equal),
                  "V", 1)
        p.wait(tensor, "V")
        for oc in range(32):
            mm = tensor.matmul(ps_rs[:, oc:oc + 1],
                               oh[:, oc * 128:(oc + 1) * 128],
                               csel[:, 0:1], start=(oc == 0), stop=(oc == 31))
        p.inc(mm, "T", 1)
        p.wait(vector, "T")
        p.inc(vector.tensor_scalar_mul(rs_half[:, :], ps_rs[:, :], 0.5),
              "V", 1)
        p.wait(tensor, "V")
        for tt in range(NTT):
            sl = slice(tt * 512, (tt + 1) * 512)
            mm = tensor.matmul(ps_qb[:, sl], onesrow[0:1, :],
                               qs_row[0:1, sl], start=True, stop=True)
        p.inc(mm, "T", 1)
        p.wait(vector, "T")
        p.inc(vector.tensor_copy(qs_b[:, :], ps_qb[:, :]), "V", 1)
    p.barrier()


def emit_maskaccum(p, bufs, dense_in, x_state, tag=""):
    nc = p.nc
    sync, vector, g = nc.sync, nc.vector, nc.gpsimd
    rs_half, qs_b = bufs["rs_half"], bufs["QS_sb"]
    dbuf = bufs["xf32"]
    tbuf = bufs["sq"]
    for oc in range(NK):
        buf = dbuf[oc % 2]
        p.wait(sync, "G")
        p.dma(sync, buf[:, :], dense_in[oc * 128:(oc + 1) * 128, :], "S",
              wait=False)
        p.wait(vector, "S")
        p.wait(vector, "G")
        vector.tensor_scalar(tbuf[:, :], buf[:, :],
                             rs_half[:, oc:oc + 1], None, ALU.mult)
        p.inc(vector.tensor_tensor(tbuf[:, :], tbuf[:, :], qs_b[:, :],
                                   ALU.mult), "V", 1)
        p.wait(g, "V")
        p.dma(g, x_state[oc * 128:(oc + 1) * 128, :], tbuf[:, :], "G",
              wait=False, accum_op=ALU.add)
    p.wait(g, "G")
    p.barrier()


def emit_ln_norm_padded(p, bufs, x_hbm, xlnp, n_tok=TOK):
    nc = p.nc
    sync, vector = nc.sync, nc.vector
    xf32, sq = bufs["xf32"], bufs["sq"]
    rsb, bb = bufs["rsb"], bufs["bb"]
    for k in range(NK):
        buf = xf32[k % 2]
        p.wait(sync, "V")
        p.dma(sync, buf[:, :], x_hbm[k * 128:(k + 1) * 128, :], "S",
              wait=False)
        p.wait(vector, "S")
        vector.tensor_tensor(sq[:, :], buf[:, :], rsb[:, :], ALU.mult)
        out_ap = _sap(xlnp, k * PADW + 1, [[1026, 2], [1, 1024]])
        in_ap = _sap(sq, 0, [[1024, 2], [1, 1024]])
        in2_ap = _sap(bb, 0, [[1024, 2], [1, 1024]])
        p.inc(vector.tensor_tensor(out_ap, in_ap, in2_ap, ALU.add), "V", 1)
    p.barrier()


def emit_halo(p, bufs, xlnp, shA, shB, flA, flB, stamp_in):
    nc = p.nc
    g = nc.gpsimd
    zcol = bufs["zcol_bf"]
    stamp_sb = bufs["stamp_sb"]
    p.inc(g.memset(zcol[:, :], 0.0), "V", 1)
    p.dma(g, stamp_sb[:, :], stamp_in, "G")
    p.wait(g, "V")

    loc = [0, 1023, 1024, 2047]
    lcol = [1, 1024, 1027, 2050]
    me = g.partition_id()
    parity = g.to_reg(me % 2)

    def writes(sh, fl):
        for i, c in enumerate(loc):
            for k in range(NK):
                src = _sap(xlnp, k * PADW + lcol[i], [[1, 1]])
                dst = APc(sh.tensor, i * D + k * 128, [[1, 128], [1, 1]])
                p.cnt["G"] += 16
                g.dma_start(out=dst, in_=src).then_inc(p.sems["G"], 16)
        for k in range(NK):
            dst = APc(sh.tensor, 4 * D + k * 128, [[1, 128], [1, 1]])
            p.cnt["G"] += 16
            g.dma_start(out=dst, in_=zcol[:, 0:1]).then_inc(p.sems["G"], 16)
        g.wait_ge(p.sems["G"], p.cnt["G"])
        p.cnt["G"] += 16
        g.dma_start(out=APc(fl.tensor, 0, [[1, 1], [1, 1]]),
                    in_=stamp_sb[0:1, 0:1]).then_inc(p.sems["G"], 16)
        g.wait_ge(p.sems["G"], p.cnt["G"])

    def poll(fl):
        preg = nc.alloc_register(ET.Pool, p.nc.make_name("poll", True))
        creg = nc.alloc_register(ET.Pool, p.nc.make_name("pcond", True))
        g.reg_mov(creg, 1)
        with g.While(creg):
            g.reg_load(preg, APc(fl.tensor, 0, [[1, 1], [1, 1]]))
            g.reg_alu(creg, preg, MAGIC, ALU.not_equal)

    def reads(srcs):
        halocol = [0, 1025, 1026, 2051]
        for i, (sh, slot) in enumerate(srcs):
            for k in range(NK):
                src = APc(sh.tensor, slot * D + k * 128, [[1, 128], [1, 1]])
                dst = _sap(xlnp, k * PADW + halocol[i], [[1, 1]])
                p.cnt["G"] += 16
                g.dma_start(out=dst, in_=src).then_inc(p.sems["G"], 16)
        g.wait_ge(p.sems["G"], p.cnt["G"])

    gc0 = p.cnt["G"]
    with g.If_eq(parity, 0):
        writes(shA, flA)
        poll(flB)
        reads([(shA, 4), (shB, 0), (shB, 3), (shA, 4)])
    gc_after = p.cnt["G"]
    with g.Else():
        p.cnt["G"] = gc0
        writes(shB, flB)
        poll(flA)
        reads([(shA, 1), (shB, 2), (shB, 1), (shA, 2)])
        assert p.cnt["G"] == gc_after, "halo branch sem counts diverge"
    p.cnt["G"] = gc_after
    p.barrier()


def emit_conv(p, bufs, xlnp, cw_in, qkv_scr):
    nc = p.nc
    g, tensor, vector, scalar = nc.gpsimd, nc.tensor, nc.vector, nc.scalar
    strips = bufs["cstrips"]
    cb_sb = bufs["cb_sb"]
    ev = bufs["conv_ev"]
    base = [1, 513, 1027, 1539]
    n_oc = 24
    p.sem("EVcv")
    estore = {}
    with nc.psum_tensor("cv_p0", [128, 512], F32) as p0, \
         nc.psum_tensor("cv_p1", [128, 512], F32) as p1:
        pss = [p0, p1]
        step = 0
        for t in range(3):
            p.dma(g, strips[0][t][:, :], cw_in[t, 0], "G", wait=False)
        gtargetC = [p.cnt["G"]]
        tstoreC = {}
        for oc in range(n_oc):
            if oc + 1 < n_oc:
                if oc >= 1:
                    g.wait_ge(p.sems["T"], tstoreC[oc - 1])
                for t in range(3):
                    p.dma(g, strips[(oc + 1) % 2][t][:, :], cw_in[t, oc + 1],
                          "G", wait=False)
                gtargetC.append(p.cnt["G"])
            sb3 = strips[oc % 2]
            tensor.wait_ge(p.sems["G"], gtargetC[oc])
            for tt in range(NTT):
                psu = pss[step % 2]
                if step >= 2:
                    tensor.wait_ge(p.sems["EVcv"], step - 1)
                first = True
                for t in range(3):
                    for k in range(NK):
                        mm = tensor.matmul(
                            psu[:, :], sb3[t][:, k * 128:(k + 1) * 128],
                            _sap(xlnp, k * PADW + base[tt] + t - 1,
                                 [[1, 512]]),
                            start=first, stop=(t == 2 and k == NK - 1))
                        first = False
                p.inc(mm, "T", 1)
                evb = ev[step % 2]
                p.wait(vector, "T")
                if step >= 2:
                    p.wait(vector, "E", estore[step - 2])
                v = vector.tensor_scalar(evb[:, :], psu[:, :],
                                         cb_sb[:, oc:oc + 1], None, ALU.add)
                p.inc(v, "EVcv", 1)
                p.wait(scalar, "EVcv")
                p.inc(scalar.dma_start(
                    out=qkv_scr[oc * 128:(oc + 1) * 128,
                                tt * 512:(tt + 1) * 512],
                    in_=evb[:, :]), "E", 16)
                estore[step] = p.cnt["E"]
                step += 1
            tstoreC[oc] = p.cnt["T"]
    p.wait(scalar, "E")
    p.barrier()


def emit_l0attn_blocks(p, bufs, qkv_scr, x_state):
    nc = p.nc
    sync, scalar, tensor, vector, g = (nc.sync, nc.scalar, nc.tensor,
                                       nc.vector, nc.gpsimd)
    wq_sb, wk_sb = bufs["wq_sb"], bufs["wk_sb"]
    wv_sb = bufs["wv_sb"]
    wo_sb = bufs["wo_sb"]
    proj_sb = bufs["proj_sb"]
    bq_sb, bk_sb = bufs["bq_sb"], bufs["bk_sb"]
    bvb_sb = bufs["bvb_sb"]
    bo_sb, pb_sb = bufs["bo_sb"], bufs["pb_sb"]
    tril_sb = bufs["tril_sb"]
    eyeb = bufs["eye_bf"]
    qsb = bufs["q_sb"]; ksb = bufs["k_sb"]; vsb = bufs["v_sb"]
    qh_sb, kh_sb = bufs["qh_sb"], bufs["kh_sb"]
    vh_sb = bufs["vh_sb"]
    sm_sb = bufs["sm_sb"]
    e_sb = bufs["e_sb"]
    et_sb = bufs["et_sb"]
    o_sb = bufs["o_sb"]
    oT_sb = bufs["oT_sb"]
    o2_sb = bufs["o2_sb"]
    mx1, rs1 = bufs["mx"], bufs["rowsum"]
    yev = bufs["yev"]

    SC = 0.125
    for blk in range(16):
        p.wait(sync, "T")
        for ch in range(8):
            for dstbuf, row0 in ((qsb, 0), (ksb, 1024), (vsb, 2048)):
                p.dma(sync, dstbuf[:, ch * 128:(ch + 1) * 128],
                      qkv_scr[row0 + ch * 128:row0 + (ch + 1) * 128,
                              blk * 128:(blk + 1) * 128], "S", wait=False)
        p.wait(sync, "S")
        p.barrier()
        with nc.psum_tensor(f"qh_ps{blk}", [128, 128], F32) as qh_ps, \
             nc.psum_tensor(f"kh_ps{blk}", [128, 128], F32) as kh_ps, \
             nc.psum_tensor(f"vh_ps{blk}", [128, 1024], F32) as vh_ps:
            for hp in range(8):
                for c in range(8):
                    tensor.matmul(qh_ps[:, :],
                                  wq_sb[:, (hp * 8 + c) * 128:
                                        (hp * 8 + c) * 128 + 128],
                                  qsb[:, c * 128:(c + 1) * 128],
                                  start=(c == 0), stop=(c == 7))
                    mm = tensor.matmul(kh_ps[:, :],
                                       wk_sb[:, (hp * 8 + c) * 128:
                                             (hp * 8 + c) * 128 + 128],
                                       ksb[:, c * 128:(c + 1) * 128],
                                       start=(c == 0), stop=(c == 7))
                p.inc(mm, "T", 1)
                p.wait(vector, "T")
                vector.tensor_scalar(qh_sb[:, hp * 128:(hp + 1) * 128],
                                     qh_ps[:, :], bq_sb[:, hp:hp + 1], SC,
                                     ALU.add, ALU.mult)
                p.inc(vector.tensor_scalar(
                    kh_sb[:, hp * 128:(hp + 1) * 128], kh_ps[:, :],
                    bk_sb[:, hp:hp + 1], None, ALU.add), "V", 1)
                p.wait(tensor, "V")
            for c in range(8):
                for nn in range(2):
                    mm = tensor.matmul(
                        vh_ps[:, nn * 512:(nn + 1) * 512],
                        vsb[:, c * 128:(c + 1) * 128],
                        wv_sb[:, c * 1024 + nn * 512:c * 1024 + nn * 512 + 512],
                        start=(c == 0), stop=(c == 7))
            p.inc(mm, "T", 1)
            p.wait(vector, "T")
            p.inc(vector.tensor_tensor(vh_sb[:, :], vh_ps[:, :],
                                       bvb_sb[:, :], ALU.add), "V", 1)
            p.wait(tensor, "V")
        with nc.psum_tensor(f"s_ps{blk}", [128, 128], F32) as s_ps, \
             nc.psum_tensor(f"et_ps{blk}", [128, 128], BF16) as et_ps, \
             nc.psum_tensor(f"pv_ps{blk}", [128, 64], F32) as pv_ps:
            for hh in range(16):
                hp, hi = hh // 2, hh % 2
                lq = qh_sb[hi * 64:hi * 64 + 64, hp * 128:(hp + 1) * 128]
                lk = kh_sb[hi * 64:hi * 64 + 64, hp * 128:(hp + 1) * 128]
                p.inc(tensor.matmul(s_ps[:, :], lq, lk, start=True,
                                    stop=True), "T", 1)
                p.wait(vector, "T")
                p.inc(vector.tensor_tensor(sm_sb[:, :], s_ps[:, :],
                                           tril_sb[:, :], ALU.add), "V", 1)
                p.wait(vector, "V")
                p.inc(vector.reduce_max(mx1[:, :], sm_sb[:, :], axis=AX.X,
                                        negate=True), "V", 1)
                p.wait(scalar, "V")
                p.inc(scalar.activation(out=e_sb[:, :], in_=sm_sb[:, :],
                                        func=AF.Exp, bias=mx1[:, 0:1],
                                        scale=1.0, accum_out=rs1[:, :]),
                      "A", 1)
                p.wait(vector, "A")
                p.inc(vector.reciprocal(rs1[:, :], rs1[:, :]), "V", 1)
                p.wait(tensor, "A")
                p.inc(tensor.transpose(et_ps[:, :], e_sb[:, :], eyeb[:, :]),
                      "T", 1)
                p.wait(vector, "T")
                p.inc(vector.tensor_copy(et_sb[:, :], et_ps[:, :]), "V", 1)
                p.wait(tensor, "V")
                p.inc(tensor.matmul(pv_ps[:, :], et_sb[:, :],
                                    vh_sb[:, hh * 64:(hh + 1) * 64],
                                    start=True, stop=True), "T", 1)
                p.wait(vector, "T")
                p.inc(vector.tensor_scalar(o_sb[:, hh * 64:(hh + 1) * 64],
                                           pv_ps[:, :], rs1[:, 0:1], None,
                                           ALU.mult), "V", 1)
                p.wait(tensor, "V")
        with nc.psum_tensor(f"ot_ps{blk}", [128, 128], BF16) as ot_ps:
            for c in range(8):
                p.wait(tensor, "V")
                p.inc(tensor.transpose(ot_ps[:, :],
                                       o_sb[:, c * 128:(c + 1) * 128],
                                       eyeb[:, :]), "T", 1)
                p.wait(vector, "T")
                p.inc(vector.tensor_copy(oT_sb[:, c * 128:(c + 1) * 128],
                                         ot_ps[:, :]), "V", 1)
        with nc.psum_tensor(f"o2_ps{blk}", [128, 128], F32) as o2_ps:
            for wc in range(8):
                p.wait(tensor, "V")
                for c in range(8):
                    mm = tensor.matmul(o2_ps[:, :],
                                       wo_sb[:, (wc * 8 + c) * 128:
                                             (wc * 8 + c) * 128 + 128],
                                       oT_sb[:, c * 128:(c + 1) * 128],
                                       start=(c == 0), stop=(c == 7))
                p.inc(mm, "T", 1)
                p.wait(vector, "T")
                p.inc(vector.tensor_scalar(o2_sb[:, wc * 128:(wc + 1) * 128],
                                           o2_ps[:, :], bo_sb[:, wc:wc + 1],
                                           None, ALU.add), "V", 1)
        with nc.psum_tensor(f"y_ps0{blk}", [128, 128], F32) as y0, \
             nc.psum_tensor(f"y_ps1{blk}", [128, 128], F32) as y1:
            ys = [y0, y1]
            for oc in range(32):
                psu = ys[oc % 2]
                if oc >= 2:
                    p.wait(tensor, "Yev")
                p.wait(tensor, "V")
                for c in range(8):
                    mm = tensor.matmul(psu[:, :],
                                       proj_sb[:, (oc * 8 + c) * 128:
                                               (oc * 8 + c) * 128 + 128],
                                       o2_sb[:, c * 128:(c + 1) * 128],
                                       start=(c == 0), stop=(c == 7))
                p.inc(mm, "T", 1)
                p.wait(vector, "T")
                p.wait(vector, "G")
                v = vector.tensor_scalar(yev[:, :], psu[:, :],
                                         pb_sb[:, oc:oc + 1], None, ALU.add)
                p.inc(v, "Yev", 1)
                p.wait(g, "Yev")
                p.dma(g, x_state[oc * 128:(oc + 1) * 128,
                                 blk * 128:(blk + 1) * 128],
                      yev[:, :], "G", wait=True, accum_op=ALU.add)
    p.wait(g, "G")
    p.barrier()


def build_L2():
    p = Prog("mk_L2")
    nc = p.nc
    sync, scalar, tensor, vector, g = (nc.sync, nc.scalar, nc.tensor,
                                       nc.vector, nc.gpsimd)
    p.sem("Yev")

    x_in = p.din("xT", (D, TOK))
    parts_in = p.din("partials1", (8, 64))
    qs_in = p.din("qs1", (1, TOK))
    dense_in = p.din("dense1", (D, TOK))
    assign_in = p.din("assign", (1, D), I32)
    iota_in = p.din("iota64", (64, 1))
    eye_in = p.din("eye", (128, 128))
    onesrow_in = p.din("onesrow", (1, 128))
    onescol_in = p.din("onescol", (128, 1))
    cw_in = p.din("cw", (3, 24, 128, NK * 128))
    cb_in = p.din("cb", (128, 24))
    wq_in = p.din("wq0", (8, 128, 1024))
    wk_in = p.din("wk0", (8, 128, 1024))
    wv_in = p.din("wv0r", (8, 128, 1024))
    wo_in = p.din("wo0", (8, 128, 1024))
    proj_in = p.din("proj0", (32, 128, 1024))
    bq_in = p.din("bq0", (128, 8))
    bk_in = p.din("bk0", (128, 8))
    bvb_in = p.din("bv0b", (128, 1024))
    bo_in = p.din("bo0", (128, 8))
    pb_in = p.din("pb0", (128, 32))
    tril_in = p.din("trilm", (128, 128))
    w0f = p.din("w0f", (32, 128, NK * 128))
    b0f_in = p.din("b0f", (128, 32))
    centT_in = p.din("centT", (NK, 128, 64))
    stamp_in = p.din("stamp", (1, 1), I32)

    x_state = p.dout("x2", (D, TOK))
    partials_out = p.dout("partials3", (1, 64))
    qs_out = p.dout("qs3", (1, TOK))
    dense_out = p.dout("dense3", (D, TOK))

    qkv_scr = p.dscr("qkv_scr", (3072, TOK), BF16)
    shA = p.dscr("sh_haloA", (5, D), BF16, shared=True)
    shB = p.dscr("sh_haloB", (5, D), BF16, shared=True)
    flA = p.dscr("sh_flagA", (1, 4), I32, shared=True)
    flB = p.dscr("sh_flagB", (1, 4), I32, shared=True)

    import os
    STAGE = int(os.environ.get("L2STAGE", "3"))
    DBG = os.environ.get("L2DBG", "") == "1"
    if DBG:
        dbg_qkv = p.dout("dbg_qkv", (3072, TOK), F32)

    bufs = {}
    bufs["xf32"] = [p.sb("xf32a", (128, TOK)), p.sb("xf32b", (128, TOK))]
    bufs["sq"] = p.sb("sq", (128, TOK))
    bufs["ones_col"] = p.sb("ones_col", (128, 1))
    bufs["onesrow"] = p.sb("onesrow_sb", (1, 128))
    bufs["eye_f32"] = p.sb("eye_sb", (128, 128))
    bufs["rs_half"] = p.sb("rs_half", (128, 32))
    bufs["b0f_sb"] = p.sb("b0f_sb", (128, 32))

    with p.es, nc.body("main", valid_engines=ALL_ENG):
        p.dma(sync, bufs["ones_col"][:, :], onescol_in, "S", wait=False)
        p.dma(sync, bufs["onesrow"][:, :], onesrow_in, "S", wait=False)
        p.dma(sync, bufs["eye_f32"][:, :], eye_in, "S", wait=False)
        p.dma(sync, bufs["b0f_sb"][:, :], b0f_in, "S", wait=False)
        p.dma(sync, x_state, x_in, "S", wait=False)
        p.wait(sync, "S")
        p.barrier()

        with ExitStack() as esc:
            for nm, shp, dt in (("parts_sb", (8, 64), F32),
                                ("csel_col", (64, 1), F32),
                                ("ass_i", (1, D), I32), ("ass_f", (1, D), F32),
                                ("iota_sb", (64, 1), F32),
                                ("oh_sb", (64, D), F32),
                                ("qs_row", (1, TOK), F32),
                                ("QS_sb", (128, TOK), F32)):
                bufs[nm] = p.sb(nm, shp, dt, es=esc)
            p.dma(sync, bufs["iota_sb"][:, :], iota_in, "S", wait=False)
            emit_csel(p, bufs, parts_in, qs_in, assign_in, tag="c1")
            emit_maskaccum(p, bufs, dense_in, x_state, tag="m1")

        if STAGE >= 2:
            with ExitStack() as esl:
                xlnp = p.sb("xlnp", (128, NK * PADW), BF16, es=esl)
                with ExitStack() as esln2:
                    bufs["rsb"] = p.sb("rsb2", (128, TOK), F32, es=esln2)
                    bufs["bb"] = p.sb("bb2", (128, TOK), F32, es=esln2)
                    with ExitStack() as esr:
                        for nm, shp, dt in (("srow", (1, TOK), F32),
                                            ("qrow", (1, TOK), F32),
                                            ("trow", (1, TOK), F32)):
                            bufs[nm] = p.sb(nm, shp, dt, es=esr)
                        emit_ln_stats(p, bufs, x_state, tag="2")
                    emit_ln_norm_padded(p, bufs, x_state, xlnp)
                with ExitStack() as esh:
                    bufs["zcol_bf"] = p.sb("zcol_bf", (128, 1), BF16, es=esh)
                    bufs["stamp_sb"] = p.sb("stamp_sb", (1, 1), I32, es=esh)
                    emit_halo(p, bufs, xlnp, shA, shB, flA, flB, stamp_in)
                with ExitStack() as esv:
                    bufs["cstrips"] = [
                        [p.sb(f"cst{b}_{t}", (128, NK * 128), BF16, es=esv)
                         for t in range(3)] for b in range(2)]
                    bufs["cb_sb"] = p.sb("cb_sb", (128, 24), F32, es=esv)
                    bufs["conv_ev"] = [p.sb("cev0", (128, 512), BF16, es=esv),
                                       p.sb("cev1", (128, 512), BF16, es=esv)]
                    p.dma(sync, bufs["cb_sb"][:, :], cb_in, "S")
                    emit_conv(p, bufs, xlnp, cw_in, qkv_scr)
                    if DBG:
                        for r in range(24):
                            p.dma(g, dbg_qkv[r * 128:(r + 1) * 128, :],
                                  qkv_scr[r * 128:(r + 1) * 128, :], "G",
                                  wait=False)
                        p.wait(g, "G")
                        p.barrier()
            with ExitStack() as esa:
                for nm, shp, dt, src in (
                        ("wq_sb", (128, 8 * 1024), BF16, wq_in),
                        ("wk_sb", (128, 8 * 1024), BF16, wk_in),
                        ("wv_sb", (128, 8 * 1024), BF16, wv_in),
                        ("wo_sb", (128, 8 * 1024), BF16, wo_in),
                        ("proj_sb", (128, 32 * 1024), BF16, proj_in)):
                    bufs[nm] = p.sb(nm, shp, dt, es=esa)
                    p.dma(g, bufs[nm][:, :].rearrange(
                        "p (a b) -> p a b", b=src.shape[-1]),
                        src.rearrange("a p b -> p a b"), "G", wait=False)
                for nm, shp, src in (("bq_sb", (128, 8), bq_in),
                                     ("bk_sb", (128, 8), bk_in),
                                     ("bvb_sb", (128, 1024), bvb_in),
                                     ("bo_sb", (128, 8), bo_in),
                                     ("pb_sb", (128, 32), pb_in),
                                     ("tril_sb", (128, 128), tril_in)):
                    bufs[nm] = p.sb(nm, shp, F32, es=esa)
                    p.dma(sync, bufs[nm][:, :], src, "S", wait=False)
                bufs["eye_bf"] = p.sb("eye_bf", (128, 128), BF16, es=esa)
                p.dma(g, bufs["eye_bf"][:, :], eye_in, "G", wait=False)
                for nm, shp, dt in (
                        ("q_sb", (128, 1024), BF16), ("k_sb", (128, 1024), BF16),
                        ("v_sb", (128, 1024), BF16),
                        ("qh_sb", (128, 8 * 128), BF16),
                        ("kh_sb", (128, 8 * 128), BF16),
                        ("vh_sb", (128, 1024), BF16),
                        ("sm_sb", (128, 128), F32), ("e_sb", (128, 128), BF16),
                        ("et_sb", (128, 128), BF16),
                        ("o_sb", (128, 1024), BF16),
                        ("oT_sb", (128, 8 * 128), BF16),
                        ("o2_sb", (128, 8 * 128), BF16),
                        ("mx", (128, 1), F32), ("rowsum", (128, 1), F32),
                        ("yev", (128, 128), F32)):
                    bufs[nm] = p.sb(nm, shp, dt, es=esa)
                p.wait(sync, "S")
                p.wait(g, "G")
                p.barrier()
                emit_l0attn_blocks(p, bufs, qkv_scr, x_state)

        if STAGE >= 3:
            with ExitStack() as es2:
                bufs["xln"] = p.sb("xln", (128, NK * TOK), BF16, es=es2)
                bufs["rsb"] = p.sb("rsb3", (128, TOK), F32, es=es2)
                bufs["bb"] = p.sb("bb3", (128, TOK), F32, es=es2)
                with ExitStack() as es2b:
                    for nm, shp, dt in (("srow", (1, TOK), F32),
                                        ("qrow", (1, TOK), F32),
                                        ("trow", (1, TOK), F32)):
                        bufs[nm] = p.sb(nm, shp, dt, es=es2b)
                    emit_ln_stats(p, bufs, x_state, tag="3")
                with ExitStack() as es2c:
                    for nm, shp, dt in (("centT_sb", (128, NK * 64), F32),
                                        ("e64", (128, 64), F32),
                                        ("hits", (128, 64), F32),
                                        ("mx", (128, 1), F32),
                                        ("rowsum", (128, 1), F32),
                                        ("qsc", (128, 16), F32),
                                        ("cnt_sb", (1, 64), F32),
                                        ("qsr", (16, 128), F32)):
                        bufs[nm] = p.sb(nm, shp, dt, es=es2c)
                    p.dma(sync,
                          bufs["centT_sb"][:, :].rearrange(
                              "p (k j) -> p k j", k=NK),
                          centT_in.rearrange("k p j -> p k j"), "S",
                          wait=False)
                    p.wait(sync, "S")
                    p.barrier()
                    with nc.psum_tensor("ps_dots3", [128, 1024],
                                        F32) as ps_dots:
                        emit_ln_norm(p, bufs, x_state, bufs["xln"],
                                     consume_f32=lambda k: emit_dots(
                                         p, bufs, ps_dots, k))
                        emit_gate(p, bufs, ps_dots, partials_out, qs_out,
                                  tag="3")
                with ExitStack() as es3:
                    bufs["strips"] = [
                        p.sb("stripA", (128, NK * 128), BF16, es=es3),
                        p.sb("stripB", (128, NK * 128), BF16, es=es3)]
                    bufs["dense_ev"] = [
                        p.sb("dense_ev0", (128, 512), F32, es=es3),
                        p.sb("dense_ev1", (128, 512), F32, es=es3)]
                    estore = {}

                    def dense_cb(oc, tt, psu, step):
                        evb = bufs["dense_ev"][step % 2]
                        p.wait(vector, "T")
                        if step >= 2:
                            p.wait(vector, "E", estore[step - 2])
                        v = vector.tensor_scalar(evb[:, :], psu[:, :],
                                                 bufs["b0f_sb"][:, oc:oc + 1],
                                                 None, ALU.add)
                        p.inc(v, "EVd3", 1)
                        p.wait(scalar, "EVd3")
                        p.inc(scalar.dma_start(
                            out=dense_out[oc * 128:(oc + 1) * 128,
                                          tt * 512:(tt + 1) * 512],
                            in_=evb[:, :]), "E", 16)
                        estore[step] = p.cnt["E"]

                    p.sem("EVd3")
                    emit_linear(p, bufs, bufs["xln"], w0f, 32, dense_cb,
                                tag="d3")

        p.wait(sync, "E")
        p.wait(sync, "S")
        p.wait(sync, "G")
    return p




MAGIC3 = 0x5A18


def emit_l1ffn(p, bufs, x_state, w12_in, b12_in, w3_in, b3_in, x4_scr,
               tag=""):
    nc = p.nc
    sync, scalar, tensor, vector, g = (nc.sync, nc.scalar, nc.tensor,
                                       nc.vector, nc.gpsimd)
    with ExitStack() as esf:
        xln = p.sb("xlnF" + tag, (128, NK * TOK), BF16, es=esf)
        with ExitStack() as esr:
            for nm in ("srow", "qrow", "trow"):
                bufs[nm] = p.sb(nm + "F" + tag, (1, TOK), F32, es=esr)
            emit_ln_stats(p, bufs, x_state, tag="F" + tag)
        emit_ln_norm(p, bufs, x_state, xln)
        with ExitStack() as esw:
            st1b = [p.sb("w12sA%d" % i + tag, (128, NK * 128), BF16,
                         es=esw) for i in range(2)]
            st2b = [p.sb("w12sB%d" % i + tag, (128, NK * 128), BF16,
                         es=esw) for i in range(2)]
            b12_sb = p.sb("b12_sb" + tag, (128, 64), F32, es=esw)
            t1 = p.sb("t1_" + tag, (128, 512), F32, es=esw)
            x4ev = [p.sb(f"x4ev{i}_" + tag, (128, 512), BF16, es=esw)
                    for i in range(2)]
            p.dma(sync, b12_sb[:, :], b12_in, "S")
            ev = "EVf" + tag
            p.sem(ev)
            estore = {}
            step = 0
            tstore = {}
            with nc.psum_tensor("f1psa" + tag, [128, 512], F32) as ps1a, \
                 nc.psum_tensor("f2psa" + tag, [128, 512], F32) as ps2a, \
                 nc.psum_tensor("f1psb" + tag, [128, 512], F32) as ps1b, \
                 nc.psum_tensor("f2psb" + tag, [128, 512], F32) as ps2b:
                p.dma(g, st1b[0][:, :], w12_in[0], "G", wait=False)
                p.dma(g, st2b[0][:, :], w12_in[32], "G", wait=False)
                gtarget = [p.cnt["G"]]
                for j in range(32):
                    if j + 1 < 32:
                        if j >= 1:
                            g.wait_ge(p.sems["T"], tstore[j - 1])
                        p.dma(g, st1b[(j + 1) % 2][:, :], w12_in[j + 1],
                              "G", wait=False)
                        p.dma(g, st2b[(j + 1) % 2][:, :], w12_in[j + 33],
                              "G", wait=False)
                        gtarget.append(p.cnt["G"])
                    st1, st2 = st1b[j % 2], st2b[j % 2]
                    tensor.wait_ge(p.sems["G"], gtarget[j])
                    for tt in range(NTT):
                        ps1 = ps1a if step % 2 == 0 else ps1b
                        ps2 = ps2a if step % 2 == 0 else ps2b
                        if step >= 2:
                            tensor.wait_ge(p.sems[ev], step - 1)
                        for k in range(NK):
                            sl = slice(k * TOK + tt * 512,
                                       k * TOK + tt * 512 + 512)
                            tensor.matmul(ps1[:, :],
                                          st1[:, k * 128:(k + 1) * 128],
                                          xln[:, sl], start=(k == 0),
                                          stop=(k == NK - 1))
                            mm = tensor.matmul(ps2[:, :],
                                               st2[:, k * 128:(k + 1) * 128],
                                               xln[:, sl], start=(k == 0),
                                               stop=(k == NK - 1))
                        p.inc(mm, "T", 1)
                        scalar.wait_ge(p.sems["T"], p.cnt["T"])
                        p.inc(scalar.activation(
                            out=t1[:, :], in_=ps1[:, :], func=AF.Silu,
                            bias=b12_sb[:, j:j + 1], scale=1.0), "A", 1)
                        vector.wait_ge(p.sems["A"], p.cnt["A"])
                        if step >= 2:
                            p.wait(vector, "E", estore[step - 2])
                        evb = x4ev[step % 2]
                        v = vector.scalar_tensor_tensor(
                            evb[:, :], ps2[:, :], b12_sb[:, j + 32:j + 33],
                            t1[:, :], ALU.add, ALU.mult)
                        p.inc(v, ev, 1)
                        scalar.wait_ge(p.sems[ev], p.cnt[ev])
                        p.inc(scalar.dma_start(
                            out=x4_scr[j * 128:(j + 1) * 128,
                                       tt * 512:(tt + 1) * 512],
                            in_=evb[:, :]), "E", 16)
                        estore[step] = p.cnt["E"]
                        step += 1
                    tstore[j] = p.cnt["T"]
            p.wait(scalar, "E")
            p.barrier()
    with ExitStack() as esw3:
        x4r = p.sb("x4r" + tag, (128, NK * TOK), BF16, es=esw3)
        b3_sb = p.sb("b3_sb" + tag, (128, 32), F32, es=esw3)
        bufs["strips"] = [p.sb("w3sA" + tag, (128, NK * 128), BF16, es=esw3),
                          p.sb("w3sB" + tag, (128, NK * 128), BF16, es=esw3)]
        yev = [p.sb(f"yf{i}_" + tag, (128, 512), F32, es=esw3)
               for i in range(2)]
        p.dma(sync, b3_sb[:, :], b3_in, "S", wait=False)
        for k in range(NK):
            p.dma(g, x4r[:, k * TOK:(k + 1) * TOK],
                  x4_scr[k * 128:(k + 1) * 128, :], "G", wait=False)
        p.wait(sync, "S")
        p.wait(g, "G")
        p.barrier()
        gstore = {}
        evn = "EVw3" + tag
        p.sem(evn)

        def w3_cb(oc, tt, psu, step):
            evb = yev[step % 2]
            p.wait(vector, "T")
            if step >= 2:
                p.wait(vector, "G", gstore[step - 2])
            v = vector.tensor_scalar(evb[:, :], psu[:, :],
                                     b3_sb[:, oc:oc + 1], 0.5,
                                     ALU.add, ALU.mult)
            p.inc(v, evn, 1)
            p.wait(g, evn)
            p.inc(g.dma_start(out=x_state[oc * 128:(oc + 1) * 128,
                                          tt * 512:(tt + 1) * 512],
                              in_=evb[:, :], accum_op=ALU.add), "G", 16)
            gstore[step] = p.cnt["G"]

        emit_linear2(p, bufs, x4r, w3_in, 32, w3_cb, free_sem=evn,
                     tag="w3" + tag)
        p.wait(g, "G")
    p.barrier()


def emit_linear2(p, bufs, xsrc_sb, strips_hbm, n_oc, out_cb, free_sem,
                 n_tok=TOK, nk=NK, tag=""):
    nc = p.nc
    g, tensor = nc.gpsimd, nc.tensor
    nts = n_tok // 512
    sbufs = bufs["strips"]
    with nc.psum_tensor(f"lin_p0{tag}", [128, 512], F32) as p0, \
         nc.psum_tensor(f"lin_p1{tag}", [128, 512], F32) as p1:
        pss = [p0, p1]
        p.dma(g, sbufs[0][:, :], strips_hbm[0], "G", wait=False)
        gtarget = [p.cnt["G"]]
        tload = [None, None]
        step = 0
        for oc in range(n_oc):
            if oc + 1 < n_oc:
                if tload[(oc + 1) % 2] is not None:
                    g.wait_ge(p.sems["T"], tload[(oc + 1) % 2])
                p.dma(g, sbufs[(oc + 1) % 2][:, :], strips_hbm[oc + 1], "G",
                      wait=False)
                gtarget.append(p.cnt["G"])
            sbw = sbufs[oc % 2]
            tensor.wait_ge(p.sems["G"], gtarget[oc])
            for tt in range(nts):
                psu = pss[step % 2]
                if step >= 2:
                    tensor.wait_ge(p.sems[free_sem], step - 1)
                for k in range(nk):
                    mm = tensor.matmul(
                        psu[:, :], sbw[:, k * 128:(k + 1) * 128],
                        xsrc_sb[:, k * n_tok + tt * 512:
                                k * n_tok + tt * 512 + 512],
                        start=(k == 0), stop=(k == nk - 1))
                p.inc(mm, "T", 1)
                out_cb(oc, tt, psu, step)
                step += 1
            tload[oc % 2] = p.cnt["T"]
    p.barrier()


def _gq(parity, qb):
    return (qb if qb < 8 else 16 + qb) if parity == 0 else 8 + qb


def _own(gg):
    if gg < 8:
        return 0, gg
    if gg >= 24:
        return 0, gg - 16
    return 1, gg - 8


def emit_l1attn(p, bufs, x_state, wq_in, wk_in, wv_in, wo_in, cos_in,
                sin_in, mask_in, stamp_in, shKA, shKB, shVA, shVB, flA, flB,
                q_scr, myK, myV, att_scr):
    nc = p.nc
    sync, scalar, tensor, vector, g = (nc.sync, nc.scalar, nc.tensor,
                                       nc.vector, nc.gpsimd)
    SC = 1.0 / np.sqrt(128.0)

    with ExitStack() as esa:
        xln = p.sb("xlnA", (128, NK * TOK), BF16, es=esa)
        with ExitStack() as esr:
            for nm in ("srow", "qrow", "trow"):
                bufs[nm] = p.sb(nm + "A1", (1, TOK), F32, es=esr)
            emit_ln_stats(p, bufs, x_state, tag="A1")
        emit_ln_norm(p, bufs, x_state, xln)
        with ExitStack() as esw:
            cos_sb = p.sb("cos_sb", (128, TOK), F32, es=esw)
            sin_sb = p.sb("sin_sb", (128, TOK), F32, es=esw)
            stwb = [p.sb("l1aw%d" % i, (128, NK * 128), BF16, es=esw)
                    for i in range(2)]
            rta = p.sb("rta", (64, 512), F32, es=esw)
            rtb = p.sb("rtb", (64, 512), F32, es=esw)
            rev = [p.sb(f"l1rev{i}", (128, 512), BF16, es=esw)
                   for i in range(2)]
            p.dma(sync, cos_sb[:, :], cos_in, "S", wait=False)
            p.dma(sync, sin_sb[:, :], sin_in, "S", wait=False)
            p.wait(sync, "S")
            p.barrier()
            estore = {}
            vstore = {}
            step = 0
            seq = [("K", wk_in, myK, True), ("Q", wq_in, q_scr, True),
                   ("V", wv_in, myV, False)]
            flat = [(wh, w_in, dst, rot, oc)
                    for (wh, w_in, dst, rot) in seq for oc in range(32)]
            p.dma(g, stwb[0][:, :], flat[0][1][0], "G", wait=False)
            gtargetA = [p.cnt["G"]]
            tstoreA = {}
            with nc.psum_tensor("l1pA0", [128, 512], F32) as psA0, \
                 nc.psum_tensor("l1pA1", [128, 512], F32) as psA1:
                for fi, (which, w_in, dst, do_rot, oc) in enumerate(flat):
                    if fi + 1 < len(flat):
                        if fi >= 1:
                            g.wait_ge(p.sems["T"], tstoreA[fi - 1])
                        nxt = flat[fi + 1]
                        p.dma(g, stwb[(fi + 1) % 2][:, :], nxt[1][nxt[4]],
                              "G", wait=False)
                        gtargetA.append(p.cnt["G"])
                    stw = stwb[fi % 2]
                    tensor.wait_ge(p.sems["G"], gtargetA[fi])
                    for tt in range(NTT):
                        psA = psA0 if step % 2 == 0 else psA1
                        if step >= 2:
                            tensor.wait_ge(p.sems["V"], vstore[step - 2])
                        for k in range(NK):
                            sl = slice(k * TOK + tt * 512,
                                       k * TOK + tt * 512 + 512)
                            mm = tensor.matmul(
                                psA[:, :], stw[:, k * 128:(k + 1) * 128],
                                xln[:, sl], start=(k == 0),
                                stop=(k == NK - 1))
                        p.inc(mm, "T", 1)
                        vector.wait_ge(p.sems["T"], p.cnt["T"])
                        if step >= 2:
                            p.wait(vector, "E", estore[step - 2])
                        evb = rev[step % 2]
                        sl2 = slice(tt * 512, (tt + 1) * 512)
                        if do_rot:
                            pe, po = psA[0:64, :], psA[64:128, :]
                            ce = cos_sb[0:64, sl2]
                            se = sin_sb[0:64, sl2]
                            vector.tensor_tensor(rta[:, :], pe, ce, ALU.mult)
                            vector.tensor_tensor(rtb[:, :], po, se, ALU.mult)
                            vector.tensor_sub(evb[0:64, :], rta[:, :],
                                              rtb[:, :])
                            vector.tensor_tensor(rta[:, :], pe, se, ALU.mult)
                            vector.tensor_tensor(rtb[:, :], po, ce, ALU.mult)
                            v = vector.tensor_add(evb[64:128, :],
                                                  rta[:, :], rtb[:, :])
                        else:
                            v = vector.tensor_copy(evb[:, :], psA[:, :])
                        p.inc(v, "V", 1)
                        vstore[step] = p.cnt["V"]
                        scalar.wait_ge(p.sems["V"], p.cnt["V"])
                        p.inc(scalar.dma_start(
                            out=dst[oc * 128:(oc + 1) * 128,
                                    tt * 512:(tt + 1) * 512],
                            in_=evb[:, :]), "E", 16)
                        estore[step] = p.cnt["E"]
                        step += 1
                    tstoreA[fi] = p.cnt["T"]
            p.wait(scalar, "E")
            p.barrier()

    stamp_sb = bufs["stamp_sb"]
    p.dma(g, stamp_sb[:, :], stamp_in, "G")
    me = g.partition_id()
    parity = g.to_reg(me % 2)
    gc0 = p.cnt["G"]

    def shr_copy(dstK, dstV, fl):
        p.cnt["G"] += 16
        g.dma_start(out=dstK, in_=myK).then_inc(p.sems["G"], 16)
        p.cnt["G"] += 16
        g.dma_start(out=dstV, in_=myV).then_inc(p.sems["G"], 16)
        g.wait_ge(p.sems["G"], p.cnt["G"])
        p.cnt["G"] += 16
        g.dma_start(out=APc(fl.tensor, 0, [[1, 1], [1, 1]]),
                    in_=stamp_sb[0:1, 0:1]).then_inc(p.sems["G"], 16)
        g.wait_ge(p.sems["G"], p.cnt["G"])

    def poll3(fl):
        preg = nc.alloc_register(ET.Pool, nc.make_name("poll3", True))
        creg = nc.alloc_register(ET.Pool, nc.make_name("pcond3", True))
        g.reg_mov(creg, 1)
        with g.While(creg):
            g.reg_load(preg, APc(fl.tensor, 0, [[1, 1], [1, 1]]))
            g.reg_alu(creg, preg, MAGIC3, ALU.not_equal)

    with g.If_eq(parity, 0):
        shr_copy(shKA, shVA, flA)
        poll3(flB)
    gc_after = p.cnt["G"]
    with g.Else():
        p.cnt["G"] = gc0
        shr_copy(shKB, shVB, flB)
        poll3(flA)
    p.cnt["G"] = gc_after
    p.barrier()

    my_par = None
    with ExitStack() as esb:
        mask_sb = p.sb("mask_sb", (128, 144 * 128), BF16, es=esb)
        ksA = p.sb("ksA", (128, TOK), BF16, es=esb)
        ksB = p.sb("ksB", (128, TOK), BF16, es=esb)
        vsA = p.sb("vsA", (128, TOK), BF16, es=esb)
        vsB = p.sb("vsB", (128, TOK), BF16, es=esb)
        qt = p.sb("qt", (128, 128), BF16, es=esb)
        ssb = p.sb("ssb", (128, 4096), F32, es=esb)
        estrip = p.sb("estrip", (128, 4096), BF16, es=esb)
        etstrip = p.sb("etstrip", (128, 4096), BF16, es=esb)
        eyeb = p.sb("eyeb3", (128, 128), BF16, es=esb)
        mxc = p.sb("mxc", (128, 1), F32, es=esb)
        rsc = p.sb("rsc", (128, 1), F32, es=esb)
        aev = p.sb("aev", (128, 128), BF16, es=esb)
        for mq in range(16):
            msl = slice(mq * 9, (mq + 1) * 9)
            p.dma(g, mask_sb[:, mq * 9 * 128:(mq + 1) * 9 * 128].rearrange(
                      "p (m k) -> p m k", m=9),
                  mask_in[msl].rearrange("m p k -> p m k"), "G", wait=False)
        p.dma(g, eyeb[:, :], bufs["eye_f32"][:, :] if False else
              bufs["eye_in_ap"], "G", wait=False)
        p.wait(g, "G")
        p.barrier()
        with nc.psum_tensor("s3ps0", [128, 512], F32) as sp0, \
             nc.psum_tensor("s3ps1", [128, 512], F32) as sp1, \
             nc.psum_tensor("et3ps", [128, 128], BF16) as etp, \
             nc.psum_tensor("pv3ps0", [128, 128], F32) as pvp0, \
             nc.psum_tensor("pv3ps1", [128, 128], F32) as pvp1:
            sps = [sp0, sp1]
            pvs = [pvp0, pvp1]
            pvst = 0
            for h in range(32):
                p.wait(sync, "T")
                p.dma(sync, ksA[:, :], shKA[h * 128:(h + 1) * 128, :], "S",
                      wait=False)
                p.dma(sync, ksB[:, :], shKB[h * 128:(h + 1) * 128, :], "S",
                      wait=False)
                for b in range(16):
                    p.dma(sync, vsA[:, b * 128:(b + 1) * 128],
                          shVA[h * 128:(h + 1) * 128,
                               b * 128:(b + 1) * 128], "S", wait=False,
                          transpose=True)
                    p.dma(sync, vsB[:, b * 128:(b + 1) * 128],
                          shVB[h * 128:(h + 1) * 128,
                               b * 128:(b + 1) * 128], "S", wait=False,
                          transpose=True)
                p.wait(tensor, "S")
                for qb in range(16):
                    lo = qb if qb < 8 else qb + 8
                    hi = qb + 8 if qb < 8 else qb + 16
                    nblk = hi + 1
                    p.wait(sync, "T")
                    p.dma(sync, qt[:, :],
                          q_scr[h * 128:(h + 1) * 128,
                                qb * 128:(qb + 1) * 128], "S", wait=False)
                    p.wait(tensor, "S")
                    runs = []
                    gg = 0
                    while gg < nblk:
                        half, lb = _own(gg)
                        run = [gg]
                        while (gg + 1 < nblk and
                               _own(gg + 1)[0] == half and
                               _own(gg + 1)[1] == _own(gg)[1] + 1):
                            gg += 1
                            run.append(gg)
                        runs.append((half, _own(run[0])[1], run))
                        gg += 1
                    spst = 0
                    for half, lb0, run in runs:
                        ks = ksA if half == 0 else ksB
                        for c0 in range(0, len(run), 4):
                            w = min(4, len(run) - c0) * 128
                            psu = sps[spst % 2]
                            if spst >= 2:
                                p.wait(tensor, "V")
                            mm = tensor.matmul(
                                psu[:, 0:w], qt[:, :],
                                ks[:, (lb0 + c0) * 128:(lb0 + c0) * 128 + w],
                                start=True, stop=True)
                            p.inc(mm, "T", 1)
                            p.wait(vector, "T")
                            gfirst = run[c0]
                            for ii in range(w // 128):
                                gcur = gfirst + ii
                                dstsl = slice(gcur * 128, gcur * 128 + 128)
                                srcsl = slice(ii * 128, ii * 128 + 128)
                                if gcur >= lo:
                                    mi = 9 * qb + (gcur - lo)
                                    v = vector.tensor_tensor(
                                        ssb[:, dstsl], psu[:, srcsl],
                                        mask_sb[:, mi * 128:(mi + 1) * 128],
                                        ALU.add)
                                else:
                                    v = vector.tensor_copy(ssb[:, dstsl],
                                                           psu[:, srcsl])
                            p.inc(v, "V", 1)
                            spst += 1
                    klen = nblk * 128
                    p.wait(vector, "V")
                    p.inc(vector.reduce_max(mxc[:, :], ssb[:, 0:klen],
                                            axis=AX.X, negate=True), "V", 1)
                    p.wait(vector, "V")
                    p.inc(vector.tensor_scalar_mul(mxc[:, :], mxc[:, :], SC),
                          "V", 1)
                    p.wait(scalar, "V")
                    p.inc(scalar.activation(out=estrip[:, 0:klen],
                                            in_=ssb[:, 0:klen], func=AF.Exp,
                                            bias=mxc[:, 0:1], scale=SC,
                                            accum_out=rsc[:, :]), "A", 1)
                    p.wait(vector, "A")
                    p.inc(vector.reciprocal(rsc[:, :], rsc[:, :]), "V", 1)
                    pvp = pvs[pvst % 2]
                    pvst += 1
                    for gg2 in range(nblk):
                        p.wait(tensor, "A" if gg2 == 0 else "V")
                        p.inc(tensor.transpose(
                            etp[:, :], estrip[:, gg2 * 128:(gg2 + 1) * 128],
                            eyeb[:, :]), "T", 1)
                        p.wait(vector, "T")
                        p.inc(vector.tensor_copy(
                            etstrip[:, gg2 * 128:(gg2 + 1) * 128],
                            etp[:, :]), "V", 1)
                    p.wait(tensor, "V")
                    for gg2 in range(nblk):
                        half, lb = _own(gg2)
                        vs = vsA if half == 0 else vsB
                        mm = tensor.matmul(
                            pvp[:, :],
                            etstrip[:, gg2 * 128:(gg2 + 1) * 128],
                            vs[:, lb * 128:(lb + 1) * 128],
                            start=(gg2 == 0), stop=(gg2 == nblk - 1))
                    p.inc(mm, "T", 1)
                    p.wait(vector, "T")
                    v = vector.tensor_scalar(aev[:, :], pvp[:, :],
                                             rsc[:, 0:1], None, ALU.mult)
                    p.inc(v, "V", 1)
                    p.wait(scalar, "V")
                    p.inc(scalar.dma_start(
                        out=att_scr[qb * 128:(qb + 1) * 128,
                                    h * 128:(h + 1) * 128],
                        in_=aev[:, :]), "E", 16)
        p.wait(scalar, "E")
        p.barrier()

    with ExitStack() as esc:
        attT = p.sb("attT", (128, NK * TOK), BF16, es=esc)
        bufs["strips"] = [p.sb("wosA", (128, NK * 128), BF16, es=esc),
                          p.sb("wosB", (128, NK * 128), BF16, es=esc)]
        yev = [p.sb(f"yo{i}", (128, 512), F32, es=esc) for i in range(2)]
        for c in range(NK):
            for tt in range(NTT):
                p.inc(scalar.dma_start(
                    out=attT[:, c * TOK + tt * 512:c * TOK + tt * 512 + 512],
                    in_=att_scr[tt * 512:(tt + 1) * 512,
                                c * 128:(c + 1) * 128],
                    transpose=True), "E", 16)
        p.wait(scalar, "E")
        p.barrier()
        gstore = {}
        p.sem("EVwo")

        def wo_cb(oc, tt, psu, step):
            evb = yev[step % 2]
            p.wait(vector, "T")
            if step >= 2:
                p.wait(vector, "G", gstore[step - 2])
            v = vector.tensor_copy(evb[:, :], psu[:, :])
            p.inc(v, "EVwo", 1)
            p.wait(g, "EVwo")
            p.inc(g.dma_start(out=x_state[oc * 128:(oc + 1) * 128,
                                          tt * 512:(tt + 1) * 512],
                              in_=evb[:, :], accum_op=ALU.add), "G", 16)
            gstore[step] = p.cnt["G"]

        emit_linear2(p, bufs, attT, wo_in, 32, wo_cb, free_sem="EVwo",
                     tag="wo")
        p.wait(g, "G")
    p.barrier()


def build_L3():
    p = Prog("mk_L3")
    nc = p.nc
    sync, scalar, tensor, vector, g = (nc.sync, nc.scalar, nc.tensor,
                                       nc.vector, nc.gpsimd)

    x_in = p.din("xT3", (D, TOK))
    parts_in = p.din("partials3", (8, 64))
    qs_in = p.din("qs3", (1, TOK))
    dense_in = p.din("dense3", (D, TOK))
    assign_in = p.din("assign", (1, D), I32)
    iota_in = p.din("iota64", (64, 1))
    eye_in = p.din("eye", (128, 128))
    onesrow_in = p.din("onesrow", (1, 128))
    onescol_in = p.din("onescol", (128, 1))
    w12_in = p.din("w12p", (64, 128, NK * 128))
    b12_in = p.din("b12c", (128, 64))
    w3_in = p.din("w3p", (32, 128, NK * 128))
    b3_in = p.din("b3c", (128, 32))
    wq_in = p.din("wqp", (32, 128, NK * 128))
    wk_in = p.din("wkp", (32, 128, NK * 128))
    wv_in = p.din("wvp", (32, 128, NK * 128))
    wo_in = p.din("wop", (32, 128, NK * 128))
    cos_in = p.din("cosb", (128, TOK))
    sin_in = p.din("sinb", (128, TOK))
    mask_in = p.din("maskb", (144, 128, 128), BF16)
    stamp_in = p.din("stamp3", (1, 1), I32)

    x_state = p.dout("xout", (D, TOK))

    x4_scr = p.dscr("x4_scr", (D, TOK), BF16)
    q_scr = p.dscr("q_scr", (D, TOK), BF16)
    myK = p.dscr("myK", (D, TOK), BF16)
    myV = p.dscr("myV", (D, TOK), BF16)
    att_scr = p.dscr("att_scr", (TOK, D), BF16)
    shKA = p.dscr("shKA", (D, TOK), BF16, shared=True)
    shKB = p.dscr("shKB", (D, TOK), BF16, shared=True)
    shVA = p.dscr("shVA", (D, TOK), BF16, shared=True)
    shVB = p.dscr("shVB", (D, TOK), BF16, shared=True)
    flA = p.dscr("flA3", (1, 4), I32, shared=True)
    flB = p.dscr("flB3", (1, 4), I32, shared=True)

    import os
    STAGE = int(os.environ.get("L3STAGE", "4"))

    bufs = {}
    bufs["xf32"] = [p.sb("xf32a", (128, TOK)), p.sb("xf32b", (128, TOK))]
    bufs["sq"] = p.sb("sq", (128, TOK))
    bufs["rsb"] = p.sb("rsb", (128, TOK))
    bufs["bb"] = p.sb("bb", (128, TOK))
    bufs["ones_col"] = p.sb("ones_col", (128, 1))
    bufs["onesrow"] = p.sb("onesrow_sb", (1, 128))
    bufs["rs_half"] = p.sb("rs_half", (128, 32))
    bufs["stamp_sb"] = p.sb("stamp_sb", (1, 1), I32)
    bufs["eye_in_ap"] = eye_in

    with p.es, nc.body("main", valid_engines=ALL_ENG):
        p.dma(sync, bufs["ones_col"][:, :], onescol_in, "S", wait=False)
        p.dma(sync, bufs["onesrow"][:, :], onesrow_in, "S", wait=False)
        p.dma(sync, x_state, x_in, "S", wait=False)
        p.wait(sync, "S")
        p.barrier()

        with ExitStack() as esc:
            for nm, shp, dt in (("parts_sb", (8, 64), F32),
                                ("csel_col", (64, 1), F32),
                                ("ass_i", (1, D), I32),
                                ("ass_f", (1, D), F32),
                                ("iota_sb", (64, 1), F32),
                                ("oh_sb", (64, D), F32),
                                ("qs_row", (1, TOK), F32),
                                ("QS_sb", (128, TOK), F32)):
                bufs[nm] = p.sb(nm, shp, dt, es=esc)
            p.dma(sync, bufs["iota_sb"][:, :], iota_in, "S", wait=False)
            emit_csel(p, bufs, parts_in, qs_in, assign_in, tag="c3")
            emit_maskaccum(p, bufs, dense_in, x_state, tag="m3")

        if STAGE >= 2:
            emit_l1ffn(p, bufs, x_state, w12_in, b12_in, w3_in, b3_in,
                       x4_scr, tag="f4")
        if STAGE >= 3:
            emit_l1attn(p, bufs, x_state, wq_in, wk_in, wv_in, wo_in,
                        cos_in, sin_in, mask_in, stamp_in, shKA, shKB,
                        shVA, shVB, flA, flB, q_scr, myK, myV, att_scr)
        if STAGE >= 4:
            emit_l1ffn(p, bufs, x_state, w12_in, b12_in, w3_in, b3_in,
                       x4_scr, tag="f6")

        p.wait(sync, "E")
        p.wait(sync, "S")
        p.wait(sync, "G")
        p.wait(sync, "V")
        p.wait(sync, "T")
    return p


def _host_prep3(inp, core):
    h = {}
    h["w12p"] = _prep_oc(np.asarray(inp["L1_ffn_w12"], np.float32))
    h["b12c"] = _col_chunks(np.asarray(inp["L1_ffn_b12"], np.float32))
    h["w3p"] = _prep_oc(np.asarray(inp["L1_ffn_w3"], np.float32))
    h["b3c"] = _col_chunks(np.asarray(inp["L1_ffn_b3"], np.float32))
    perm = np.concatenate([hh * 128 + np.r_[np.arange(0, 128, 2),
                                            np.arange(1, 128, 2)]
                           for hh in range(32)])
    wq = np.asarray(inp["L1_attn_wq"], np.float32)[perm]
    wk = np.asarray(inp["L1_attn_wk"], np.float32)[perm]
    h["wqp"] = _prep_oc(wq)
    h["wkp"] = _prep_oc(wk)
    h["wvp"] = _prep_oc(np.asarray(inp["L1_attn_wv"], np.float32))
    h["wop"] = _prep_oc(np.asarray(inp["L1_attn_wo"], np.float32))
    return h


def _host_core3(core):
    import ml_dtypes
    rows = _tok_rows(core)
    t = rows.astype(np.float64)
    cos = np.cos(t).astype(np.float32)
    sin = np.sin(t).astype(np.float32)
    cosb = np.ascontiguousarray(np.broadcast_to(cos[None, :], (128, TOK)))
    sinb = np.ascontiguousarray(np.broadcast_to(sin[None, :], (128, TOK)))
    parity = core % 2
    tril = np.tril(np.ones((128, 128), np.float32))
    m_none = np.zeros((128, 128), np.float32)
    m_tril = np.where(tril > 0, 0.0, -1e9).astype(np.float32)
    m_full = np.full((128, 128), -1e9, np.float32)
    tiles = np.zeros((144, 128, 128), np.float32)
    for qb in range(16):
        lo = qb if qb < 8 else qb + 8
        gq = _gq(parity, qb)
        for i in range(9):
            gcur = lo + i
            if gcur < gq:
                tiles[9 * qb + i] = m_none
            elif gcur == gq:
                tiles[9 * qb + i] = m_tril
            else:
                tiles[9 * qb + i] = m_full
    maskb = tiles.astype(ml_dtypes.bfloat16)
    return cosb, sinb, maskb



_HOST = {}


def _host_prep(inp):
    h = {}
    h["w0f"] = _prep_oc(np.asarray(inp["L0_ffn_weight"], np.float32))
    h["b0f"] = _col_chunks(np.asarray(inp["L0_ffn_bias"], np.float32))
    cent = np.asarray(inp["L0_ffn_centroids"], np.float32)
    h["centT"] = np.ascontiguousarray(cent.T.reshape(NK, 128, 64))
    h["assign"] = np.asarray(inp["L0_ffn_assignments"], np.int32).reshape(1, D)
    h["eye"] = np.eye(128, dtype=np.float32)
    h["onesrow"] = np.ones((1, 128), np.float32)
    h["onescol"] = np.ones((128, 1), np.float32)
    h["iota64"] = np.arange(64, dtype=np.float32).reshape(64, 1)
    cw = np.asarray(inp["L0_attn_conv_w"], np.float32)
    h["cw"] = np.stack([_prep_co(cw[t]) for t in range(3)])
    h["cb"] = _col_chunks(np.asarray(inp["L0_attn_conv_b"], np.float32))
    h["wq0"] = _prep_co(np.asarray(inp["L0_attn_wq"], np.float32).reshape(1024, 1024))
    h["wk0"] = _prep_co(np.asarray(inp["L0_attn_wk"], np.float32).reshape(1024, 1024))
    h["wv0r"] = np.ascontiguousarray(
        np.asarray(inp["L0_attn_wv"], np.float32).reshape(8, 128, 1024))
    h["wo0"] = _prep_co(np.asarray(inp["L0_attn_wo"], np.float32).reshape(1024, 1024))
    h["proj0"] = _prep_co(np.asarray(inp["L0_attn_proj_w"], np.float32))
    h["bq0"] = _col_chunks(np.asarray(inp["L0_attn_bq"], np.float32).reshape(-1))
    h["bk0"] = _col_chunks(np.asarray(inp["L0_attn_bk"], np.float32).reshape(-1))
    h["bv0b"] = np.ascontiguousarray(np.broadcast_to(
        np.asarray(inp["L0_attn_bv"], np.float32).reshape(1, 1024), (128, 1024)))
    h["bo0"] = _col_chunks(np.asarray(inp["L0_attn_bo"], np.float32))
    h["pb0"] = _col_chunks(np.asarray(inp["L0_attn_proj_b"], np.float32))
    tril = np.tril(np.ones((128, 128), bool))
    h["trilm"] = np.where(tril, 0.0, -1e9).astype(np.float32)
    return h


def kernel(**inp):
    x_full = np.asarray(inp["L0_attn_inputs"], np.float32)
    core_rows = [_tok_rows(c) for c in range(NCORES)]
    xT = [np.ascontiguousarray(x_full[c // 2][core_rows[c]].T)
          for c in range(NCORES)]
    h = _host_prep(inp)
    h3 = _host_prep3(inp, 0)
    percore = [_host_core3(c) for c in range(NCORES)]

    p1 = build_L1()
    in_maps = [{"xT": xT[c], "w0f": h["w0f"], "b0f": h["b0f"],
                "centT": h["centT"], "eye": h["eye"],
                "onesrow": h["onesrow"], "onescol": h["onescol"]}
               for c in range(NCORES)]
    r1 = run_bass_kernel_spmd(p1.nc, in_maps, core_ids=list(range(NCORES)))
    l1 = r1.results
    parts1 = np.concatenate([l1[c]["partials"] for c in range(NCORES)],
                            axis=0)

    p2 = build_L2()
    in_maps = [{"xT": xT[c], "partials1": parts1, "qs1": l1[c]["qs"],
                "dense1": l1[c]["dense"], "assign": h["assign"],
                "iota64": h["iota64"], "eye": h["eye"],
                "onesrow": h["onesrow"], "onescol": h["onescol"],
                "cw": h["cw"], "cb": h["cb"], "wq0": h["wq0"],
                "wk0": h["wk0"], "wv0r": h["wv0r"], "wo0": h["wo0"],
                "proj0": h["proj0"], "bq0": h["bq0"], "bk0": h["bk0"],
                "bv0b": h["bv0b"], "bo0": h["bo0"], "pb0": h["pb0"],
                "trilm": h["trilm"], "w0f": h["w0f"], "b0f": h["b0f"],
                "centT": h["centT"], "stamp": np.array([[MAGIC]], np.int32)}
               for c in range(NCORES)]
    r2 = run_bass_kernel_spmd(p2.nc, in_maps, core_ids=list(range(NCORES)))
    l2 = r2.results
    parts3 = np.concatenate([l2[c]["partials3"] for c in range(NCORES)],
                            axis=0)

    p3 = build_L3()
    in_maps = []
    for c in range(NCORES):
        cosb, sinb, maskb = percore[c]
        in_maps.append(
            {"xT3": l2[c]["x2"], "partials3": parts3, "qs3": l2[c]["qs3"],
             "dense3": l2[c]["dense3"], "assign": h["assign"],
             "iota64": h["iota64"], "eye": h["eye"],
             "onesrow": h["onesrow"], "onescol": h["onescol"],
             "w12p": h3["w12p"], "b12c": h3["b12c"], "w3p": h3["w3p"],
             "b3c": h3["b3c"], "wqp": h3["wqp"], "wkp": h3["wkp"],
             "wvp": h3["wvp"], "wop": h3["wop"], "cosb": cosb,
             "sinb": sinb, "maskb": maskb,
             "stamp3": np.array([[MAGIC3]], np.int32)})
    r3 = run_bass_kernel_spmd(p3.nc, in_maps, core_ids=list(range(NCORES)))

    out = np.empty((4, T, D), np.float32)
    for c in range(NCORES):
        out[c // 2][core_rows[c]] = r3.results[c]["xout"].T
    return out


# version 05
# speedup vs baseline: 1.3708x; 1.2526x over previous
import numpy as np
from contextlib import ExitStack

import concourse.bass as bass
import concourse.mybir as mybir
from concourse.ap import AP as APc
from concourse.bass_utils import run_bass_kernel_spmd

F32 = mybir.dt.float32
BF16 = mybir.dt.bfloat16
I32 = mybir.dt.int32
ALU = mybir.AluOpType
AF = mybir.ActivationFunctionType
AX = mybir.AxisListType
ET = mybir.EngineType
ALL_ENG = [ET.PE, ET.DVE, ET.Activation, ET.Pool, ET.SP]

D = 4096
TOK = 2048
NK = D // 128
NTT = TOK // 512
NCORES = 8
T = 4096
EPS = 1e-5
THR = 0.01
INVTEMP = 10.0


def _tok_rows(core):
    if core % 2 == 0:
        return np.r_[0:1024, 3072:4096]
    return np.r_[1024:3072]


def _prep_oc(w):
    no, nk = w.shape[0] // 128, w.shape[1] // 128
    return np.ascontiguousarray(
        w.reshape(no, 128, nk, 128).transpose(0, 3, 2, 1).astype(np.float32))


def _prep_co(w):
    nk, no = w.shape[0] // 128, w.shape[1] // 128
    return np.ascontiguousarray(
        w.reshape(nk, 128, no, 128).transpose(2, 1, 0, 3).astype(np.float32))


def _col_chunks(b):
    return np.ascontiguousarray(b.reshape(-1, 128).T.astype(np.float32))



class Prog:

    def __init__(self, name):
        self.nc = bass.Bass("TRN2", target_bir_lowering=False, debug=False,
                            name=name)
        self.es = ExitStack()
        self.sems = {}
        self.cnt = {}
        self.inputs = {}
        self.outputs = {}
        self._names = {}

    def sem(self, name):
        if name not in self.sems:
            h = self.es.enter_context(self.nc.semaphore(name))
            self.sems[name] = h
            self.cnt[name] = 0
        return self.sems[name]

    def inc(self, instr, name, n=1):
        self.sem(name)
        self.cnt[name] += n
        instr.then_inc(self.sems[name], n)
        return self.cnt[name]

    def wait(self, eng, name, val=None):
        self.sem(name)
        eng.wait_ge(self.sems[name], self.cnt[name] if val is None else val)

    def dma(self, eng, out, in_, sname, wait=True, **kw):
        self.inc(eng.dma_start(out=out, in_=in_, **kw), sname, 16)
        if wait:
            self.wait(eng, sname)

    def barrier(self):
        self.nc.all_engine_barrier()

    def din(self, name, shape, dtype=F32):
        t = self.nc.dram_tensor(name, list(shape), dtype, kind="ExternalInput")
        self.inputs[name] = [int(s) for s in shape]
        return t.ap()

    def dout(self, name, shape, dtype=F32):
        t = self.nc.dram_tensor(name, list(shape), dtype, kind="ExternalOutput")
        self.outputs[name] = [int(s) for s in shape]
        return t.ap()

    def dscr(self, name, shape, dtype=F32, shared=False):
        t = self.nc.dram_tensor(name, list(shape), dtype, kind="Internal",
                                addr_space="Shared" if shared else "Local")
        return t.ap()

    def sb(self, name, shape, dtype=F32, es=None):
        n = self._names.get(name, 0)
        self._names[name] = n + 1
        if n:
            name = f"{name}_{n}"
        return (es or self.es).enter_context(
            self.nc.sbuf_tensor(name, list(shape), dtype))



def emit_ln_stats(p, bufs, x_hbm, tag="", n_tok=TOK):
    nc = p.nc
    sync, scalar, tensor, vector = nc.sync, nc.scalar, nc.tensor, nc.vector
    nts = n_tok // 512
    xf32 = bufs["xf32"]
    sq = bufs["sq"]
    ones_col = bufs["ones_col"]
    onesrow = bufs["onesrow"]
    rsb, bb = bufs["rsb"], bufs["bb"]
    srow, qrow, trow = bufs["srow"], bufs["qrow"], bufs["trow"]

    with nc.psum_tensor(f"ps_s{tag}", [1, n_tok], F32) as ps_s, \
         nc.psum_tensor(f"ps_q{tag}", [1, n_tok], F32) as ps_q:
        for k in range(NK):
            buf = xf32[k % 2]
            p.wait(sync, "T")
            p.dma(sync, buf[:, :], x_hbm[k * 128:(k + 1) * 128, :], "S",
                  wait=False)
            p.wait(scalar, "S")
            p.wait(scalar, "T")
            p.inc(scalar.activation(out=sq[:, :], in_=buf[:, :],
                                    func=AF.Square), "A", 1)
            p.wait(tensor, "A")
            for tt in range(nts):
                sl = slice(tt * 512, (tt + 1) * 512)
                tensor.matmul(ps_s[0:1, sl], ones_col[:, 0:1], buf[:, sl],
                              start=(k == 0), stop=(k == NK - 1))
                mm = tensor.matmul(ps_q[0:1, sl], ones_col[:, 0:1], sq[:, sl],
                                   start=(k == 0), stop=(k == NK - 1))
            p.inc(mm, "T", 1)

        p.wait(vector, "T")
        vector.tensor_scalar_mul(srow[:, :], ps_s[0:1, :], 1.0 / D)
        p.inc(vector.tensor_scalar_mul(qrow[:, :], ps_q[0:1, :], 1.0 / D),
              "V", 1)
        p.wait(vector, "V")
        p.inc(vector.tensor_tensor(trow[:, :], srow[:, :], srow[:, :],
                                   ALU.mult), "V", 1)
        p.wait(vector, "V")
        p.inc(vector.tensor_sub(qrow[:, :], qrow[:, :], trow[:, :]), "V", 1)
        p.wait(vector, "V")
        p.inc(vector.tensor_scalar_add(qrow[:, :], qrow[:, :], float(EPS)),
              "V", 1)
        p.wait(scalar, "V")
        p.inc(scalar.activation(out=trow[:, :], in_=qrow[:, :], func=AF.Ln),
              "A", 1)
        p.wait(scalar, "A")
        p.inc(scalar.activation(out=qrow[:, :], in_=trow[:, :], func=AF.Exp,
                                scale=-0.5), "A", 1)
        p.wait(vector, "A")
        p.inc(vector.tensor_tensor(srow[:, :], srow[:, :], qrow[:, :],
                                   ALU.mult), "V", 1)
        p.wait(vector, "V")
        p.inc(vector.tensor_scalar_mul(srow[:, :], srow[:, :], -1.0), "V", 1)

    with nc.psum_tensor(f"ps_bA{tag}", [128, n_tok], F32) as ps_bA, \
         nc.psum_tensor(f"ps_bB{tag}", [128, n_tok], F32) as ps_bB:
        p.wait(tensor, "V")
        for tt in range(nts):
            sl = slice(tt * 512, (tt + 1) * 512)
            tensor.matmul(ps_bA[:, sl], onesrow[0:1, :], qrow[0:1, sl],
                          start=True, stop=True)
            mm = tensor.matmul(ps_bB[:, sl], onesrow[0:1, :], srow[0:1, sl],
                               start=True, stop=True)
        p.inc(mm, "T", 1)
        p.wait(vector, "T")
        vector.tensor_copy(rsb[:, :], ps_bA[:, :])
        p.inc(vector.tensor_copy(bb[:, :], ps_bB[:, :]), "V", 1)

    p.barrier()


def emit_ln_norm(p, bufs, x_hbm, xln_sb, consume_f32=None, n_tok=TOK):
    nc = p.nc
    sync, vector = nc.sync, nc.vector
    xf32, sq = bufs["xf32"], bufs["sq"]
    rsb, bb = bufs["rsb"], bufs["bb"]
    for k in range(NK):
        buf = xf32[k % 2]
        p.wait(sync, "V")
        p.dma(sync, buf[:, :], x_hbm[k * 128:(k + 1) * 128, :], "S",
              wait=False)
        p.wait(vector, "S")
        p.wait(vector, "T")
        vector.tensor_tensor(sq[:, :], buf[:, :], rsb[:, :], ALU.mult)
        p.inc(vector.tensor_tensor(xln_sb[:, k * n_tok:(k + 1) * n_tok],
                                   sq[:, :], bb[:, :], ALU.add), "V", 1)
        if consume_f32 is not None:
            p.wait(vector, "V")
            p.inc(vector.tensor_tensor(sq[:, :], sq[:, :], bb[:, :],
                                       ALU.add), "V", 1)
            consume_f32(k)
    p.barrier()


def emit_dots(p, bufs, ps_dots, k):
    nc = p.nc
    tensor = nc.tensor
    sq = bufs["sq"]
    centT_sb = bufs["centT_sb"]
    p.wait(tensor, "V")
    for j in range(16):
        mm = tensor.matmul(
            ps_dots[:, j * 64:(j + 1) * 64],
            sq[:, j * 128:(j + 1) * 128],
            centT_sb[:, k * 64:(k + 1) * 64],
            start=(k == 0 and j in (0, 8)), stop=(k == NK - 1))
    p.inc(mm, "T", 1)


def emit_gate(p, bufs, ps_dots, partials_out, qs_out, tag=""):
    nc = p.nc
    sync, scalar, tensor, vector = nc.sync, nc.scalar, nc.tensor, nc.vector
    e, hits = bufs["e64"], bufs["hits"]
    mx, rowsum = bufs["mx"], bufs["rowsum"]
    qsc = bufs["qsc"]
    ones_col, eye = bufs["ones_col"], bufs["eye_f32"]
    cnt_sb, qsr_sb = bufs["cnt_sb"], bufs["qsr"]

    with nc.psum_tensor(f"ps_cnt{tag}", [1, 64], F32) as ps_cnt:
        for j in range(16):
            reg = ps_dots[:, j * 64:(j + 1) * 64]
            p.wait(vector, "T")
            p.wait(vector, "A")
            p.inc(vector.reduce_max(mx[:, :], reg, axis=AX.X, negate=True),
                  "V", 1)
            p.wait(vector, "V")
            p.inc(vector.tensor_scalar_mul(mx[:, :], mx[:, :], INVTEMP),
                  "V", 1)
            p.wait(scalar, "V")
            p.inc(scalar.activation(out=e[:, :], in_=reg, func=AF.Exp,
                                    bias=mx[:, 0:1], scale=INVTEMP,
                                    accum_out=rowsum[:, :]), "A", 1)
            p.wait(vector, "A")
            p.inc(vector.tensor_scalar_mul(rowsum[:, :], rowsum[:, :], THR),
                  "V", 1)
            p.wait(vector, "V")
            p.inc(vector.tensor_scalar(hits[:, :], e[:, :], rowsum[:, 0:1],
                                       None, ALU.is_gt), "V", 1)
            p.wait(vector, "V")
            p.inc(vector.reduce_max(qsc[:, j:j + 1], hits[:, :], axis=AX.X),
                  "V", 1)
            p.wait(tensor, "V")
            p.inc(tensor.matmul(ps_cnt[0:1, :], ones_col[:, 0:1], hits[:, :],
                                start=(j == 0), stop=(j == 15)), "T", 1)
        p.wait(vector, "T")
        p.inc(vector.tensor_copy(cnt_sb[:, :], ps_cnt[0:1, :]), "V", 1)
        p.wait(sync, "V")
        p.dma(sync, partials_out, cnt_sb[:, :], "S")

    with nc.psum_tensor(f"ps_qt{tag}", [128, 128], F32) as ps_qt:
        p.wait(tensor, "V")
        p.inc(tensor.transpose(ps_qt[0:16, :], qsc[:, :], eye[:, :]), "T", 1)
        p.wait(vector, "T")
        p.inc(vector.tensor_copy(qsr_sb[:, :], ps_qt[0:16, :]), "V", 1)
        p.wait(sync, "V")
        p.dma(sync, qs_out.rearrange("a (b c) -> (a b) c", b=16),
              qsr_sb[:, :], "S")
    p.barrier()


def emit_linear(p, bufs, xsrc_sb, strips_hbm, n_oc, out_cb, n_tok=TOK,
                nk=NK, tag=""):
    nc = p.nc
    g, tensor = nc.gpsimd, nc.tensor
    nts = n_tok // 512
    sbufs = bufs["strips"]
    ev = "EV" + tag
    p.sem(ev)
    with nc.psum_tensor(f"lin_p0{tag}", [128, 512], F32) as p0, \
         nc.psum_tensor(f"lin_p1{tag}", [128, 512], F32) as p1:
        pss = [p0, p1]
        p.dma(g, sbufs[0][:, :], strips_hbm[0], "G", wait=False)
        gtarget = [p.cnt["G"]]
        tload = [None, None]
        step = 0
        for oc in range(n_oc):
            if oc + 1 < n_oc:
                if tload[(oc + 1) % 2] is not None:
                    g.wait_ge(p.sems["T"], tload[(oc + 1) % 2])
                p.dma(g, sbufs[(oc + 1) % 2][:, :], strips_hbm[oc + 1], "G",
                      wait=False)
                gtarget.append(p.cnt["G"])
            sbw = sbufs[oc % 2]
            tensor.wait_ge(p.sems["G"], gtarget[oc])
            for tt in range(nts):
                psu = pss[step % 2]
                if step >= 2:
                    tensor.wait_ge(p.sems[ev], step - 1)
                for k in range(nk):
                    mm = tensor.matmul(
                        psu[:, :],
                        sbw[:, k * 128:(k + 1) * 128],
                        xsrc_sb[:, k * n_tok + tt * 512:
                                k * n_tok + tt * 512 + 512],
                        start=(k == 0), stop=(k == nk - 1))
                p.inc(mm, "T", 1)
                out_cb(oc, tt, psu, step)
                step += 1
            tload[oc % 2] = p.cnt["T"]
    p.barrier()



def build_L1():
    p = Prog("mk_L1")
    nc = p.nc
    sync, scalar, tensor, vector, g = (nc.sync, nc.scalar, nc.tensor,
                                       nc.vector, nc.gpsimd)

    x_hbm = p.din("xT", (D, TOK))
    w0f = p.din("w0f", (32, 128, NK * 128))
    b0f_in = p.din("b0f", (128, 32))
    centT_in = p.din("centT", (NK, 128, 64))
    eye_in = p.din("eye", (128, 128))
    onesrow_in = p.din("onesrow", (1, 128))
    onescol_in = p.din("onescol", (128, 1))

    partials_out = p.dout("partials", (1, 64))
    qs_out = p.dout("qs", (1, TOK))
    dense_out = p.dout("dense", (D, TOK))

    bufs = {}
    bufs["xln"] = p.sb("xln", (128, NK * TOK), BF16)
    bufs["xf32"] = [p.sb("xf32a", (128, TOK)), p.sb("xf32b", (128, TOK))]
    bufs["sq"] = p.sb("sq", (128, TOK))
    bufs["rsb"] = p.sb("rsb", (128, TOK))
    bufs["bb"] = p.sb("bb", (128, TOK))
    bufs["ones_col"] = p.sb("ones_col", (128, 1))
    bufs["onesrow"] = p.sb("onesrow_sb", (1, 128))
    bufs["eye_f32"] = p.sb("eye_sb", (128, 128))
    bufs["b0f_sb"] = p.sb("b0f_sb", (128, 32))

    with p.es, nc.body("main", valid_engines=ALL_ENG):
        p.dma(sync, bufs["ones_col"][:, :], onescol_in, "S", wait=False)
        p.dma(sync, bufs["onesrow"][:, :], onesrow_in, "S", wait=False)
        p.dma(sync, bufs["eye_f32"][:, :], eye_in, "S", wait=False)
        p.dma(sync, bufs["b0f_sb"][:, :], b0f_in, "S", wait=False)

        with ExitStack() as es2:
            for nm, shp, dt in (("srow", (1, TOK), F32), ("qrow", (1, TOK), F32),
                                ("trow", (1, TOK), F32),
                                ("centT_sb", (128, NK * 64), F32),
                                ("e64", (128, 64), F32), ("hits", (128, 64), F32),
                                ("mx", (128, 1), F32), ("rowsum", (128, 1), F32),
                                ("qsc", (128, 16), F32), ("cnt_sb", (1, 64), F32),
                                ("qsr", (16, 128), F32)):
                bufs[nm] = p.sb(nm, shp, dt, es=es2)
            p.dma(sync,
                  bufs["centT_sb"][:, :].rearrange("p (k j) -> p k j", k=NK),
                  centT_in.rearrange("k p j -> p k j"), "S", wait=False)
            p.wait(sync, "S")
            p.barrier()

            emit_ln_stats(p, bufs, x_hbm, tag="1")
            with nc.psum_tensor("ps_dots", [128, 1024], F32) as ps_dots:
                emit_ln_norm(p, bufs, x_hbm, bufs["xln"],
                             consume_f32=lambda k: emit_dots(p, bufs, ps_dots, k))
                emit_gate(p, bufs, ps_dots, partials_out, qs_out, tag="1")

        with ExitStack() as es3:
            bufs["strips"] = [p.sb("stripA", (128, NK * 128), BF16, es=es3),
                              p.sb("stripB", (128, NK * 128), BF16, es=es3)]
            bufs["dense_ev"] = [p.sb("dense_ev0", (128, 512), F32, es=es3),
                                p.sb("dense_ev1", (128, 512), F32, es=es3)]
            estore = {}

            def dense_cb(oc, tt, psu, step):
                evb = bufs["dense_ev"][step % 2]
                p.wait(vector, "T")
                if step >= 2:
                    p.wait(vector, "E", estore[step - 2])
                v = vector.tensor_scalar(evb[:, :], psu[:, :],
                                         bufs["b0f_sb"][:, oc:oc + 1], None,
                                         ALU.add)
                p.inc(v, "EVd1", 1)
                p.wait(scalar, "EVd1")
                p.inc(scalar.dma_start(
                    out=dense_out[oc * 128:(oc + 1) * 128,
                                  tt * 512:(tt + 1) * 512], in_=evb[:, :]),
                    "E", 16)
                estore[step] = p.cnt["E"]

            p.sem("EVd1")
            emit_linear(p, bufs, bufs["xln"], w0f, 32, dense_cb, tag="d1")

            p.wait(sync, "E")
            p.wait(sync, "S")
    return p




PADW = 2052
SEG = [1, 1027]
MAGIC = 0x5A17


def _sap(handle, offset, dims):
    free = int(np.prod(handle.shape[1:]))
    return APc(handle, offset, [[free, int(handle.shape[0])]] + dims)


def emit_csel(p, bufs, partials_in, qs_in, assign_in, tag=""):
    nc = p.nc
    sync, scalar, tensor, vector = nc.sync, nc.scalar, nc.tensor, nc.vector
    parts = bufs["parts_sb"]
    csel = bufs["csel_col"]
    ass_i = bufs["ass_i"]
    ass_f = bufs["ass_f"]
    iota = bufs["iota_sb"]
    oh = bufs["oh_sb"]
    rs_half = bufs["rs_half"]
    qs_row = bufs["qs_row"]
    qs_b = bufs["QS_sb"]
    onesrow = bufs["onesrow"]
    ones_col = bufs["ones_col"]

    p.dma(sync, parts[:, :], partials_in, "S", wait=False)
    p.dma(sync, ass_i[:, :], assign_in, "S", wait=False)
    p.dma(sync, qs_row[:, :], qs_in, "S", wait=False)
    p.wait(sync, "S")
    p.barrier()
    with nc.psum_tensor(f"ps_cs{tag}", [64, 1], F32) as ps_cs, \
         nc.psum_tensor(f"ps_oh{tag}", [64, 512], F32) as ps_oh, \
         nc.psum_tensor(f"ps_rs{tag}", [128, 32], F32) as ps_rs, \
         nc.psum_tensor(f"ps_qb{tag}", [128, TOK], F32) as ps_qb:
        p.inc(tensor.matmul(ps_cs[:, :], parts[:, :], ones_col[0:8, 0:1],
                            start=True, stop=True), "T", 1)
        p.wait(vector, "T")
        p.inc(vector.tensor_scalar(csel[:, :], ps_cs[:, :], 0.0, None,
                                   ALU.is_gt), "V", 1)
        p.inc(vector.tensor_copy(ass_f[:, :], ass_i[:, :]), "V", 1)
        for gg in range(D // 512):
            sl = slice(gg * 512, (gg + 1) * 512)
            p.wait(tensor, "V")
            p.inc(tensor.matmul(ps_oh[:, :], onesrow[0:1, 0:64],
                                ass_f[0:1, sl], start=True, stop=True),
                  "T", 1)
            p.wait(vector, "T")
            p.inc(vector.tensor_scalar(oh[:, sl], ps_oh[:, :],
                                       iota[:, 0:1], None, ALU.is_equal),
                  "V", 1)
        p.wait(tensor, "V")
        for oc in range(32):
            mm = tensor.matmul(ps_rs[:, oc:oc + 1],
                               oh[:, oc * 128:(oc + 1) * 128],
                               csel[:, 0:1], start=(oc == 0), stop=(oc == 31))
        p.inc(mm, "T", 1)
        p.wait(vector, "T")
        p.inc(vector.tensor_scalar_mul(rs_half[:, :], ps_rs[:, :], 0.5),
              "V", 1)
        p.wait(tensor, "V")
        for tt in range(NTT):
            sl = slice(tt * 512, (tt + 1) * 512)
            mm = tensor.matmul(ps_qb[:, sl], onesrow[0:1, :],
                               qs_row[0:1, sl], start=True, stop=True)
        p.inc(mm, "T", 1)
        p.wait(vector, "T")
        p.inc(vector.tensor_copy(qs_b[:, :], ps_qb[:, :]), "V", 1)
    p.barrier()


def emit_maskaccum(p, bufs, dense_in, x_state, tag=""):
    nc = p.nc
    sync, vector, g = nc.sync, nc.vector, nc.gpsimd
    rs_half, qs_b = bufs["rs_half"], bufs["QS_sb"]
    dbuf = bufs["xf32"]
    tbuf = bufs["sq"]
    for oc in range(NK):
        buf = dbuf[oc % 2]
        p.wait(sync, "G")
        p.dma(sync, buf[:, :], dense_in[oc * 128:(oc + 1) * 128, :], "S",
              wait=False)
        p.wait(vector, "S")
        p.wait(vector, "G")
        vector.tensor_scalar(tbuf[:, :], buf[:, :],
                             rs_half[:, oc:oc + 1], None, ALU.mult)
        p.inc(vector.tensor_tensor(tbuf[:, :], tbuf[:, :], qs_b[:, :],
                                   ALU.mult), "V", 1)
        p.wait(g, "V")
        p.dma(g, x_state[oc * 128:(oc + 1) * 128, :], tbuf[:, :], "G",
              wait=False, accum_op=ALU.add)
    p.wait(g, "G")
    p.barrier()


def emit_ln_norm_padded(p, bufs, x_hbm, xlnp, n_tok=TOK):
    nc = p.nc
    sync, vector = nc.sync, nc.vector
    xf32, sq = bufs["xf32"], bufs["sq"]
    rsb, bb = bufs["rsb"], bufs["bb"]
    for k in range(NK):
        buf = xf32[k % 2]
        p.wait(sync, "V")
        p.dma(sync, buf[:, :], x_hbm[k * 128:(k + 1) * 128, :], "S",
              wait=False)
        p.wait(vector, "S")
        vector.tensor_tensor(sq[:, :], buf[:, :], rsb[:, :], ALU.mult)
        out_ap = _sap(xlnp, k * PADW + 1, [[1026, 2], [1, 1024]])
        in_ap = _sap(sq, 0, [[1024, 2], [1, 1024]])
        in2_ap = _sap(bb, 0, [[1024, 2], [1, 1024]])
        p.inc(vector.tensor_tensor(out_ap, in_ap, in2_ap, ALU.add), "V", 1)
    p.barrier()


def emit_halo(p, bufs, xlnp, shA, shB, flA, flB, stamp_in):
    nc = p.nc
    g = nc.gpsimd
    zcol = bufs["zcol_bf"]
    stamp_sb = bufs["stamp_sb"]
    p.inc(g.memset(zcol[:, :], 0.0), "V", 1)
    p.dma(g, stamp_sb[:, :], stamp_in, "G")
    p.wait(g, "V")

    loc = [0, 1023, 1024, 2047]
    lcol = [1, 1024, 1027, 2050]
    me = g.partition_id()
    parity = g.to_reg(me % 2)

    def writes(sh, fl):
        for i, c in enumerate(loc):
            for k in range(NK):
                src = _sap(xlnp, k * PADW + lcol[i], [[1, 1]])
                dst = APc(sh.tensor, i * D + k * 128, [[1, 128], [1, 1]])
                p.cnt["G"] += 16
                g.dma_start(out=dst, in_=src).then_inc(p.sems["G"], 16)
        for k in range(NK):
            dst = APc(sh.tensor, 4 * D + k * 128, [[1, 128], [1, 1]])
            p.cnt["G"] += 16
            g.dma_start(out=dst, in_=zcol[:, 0:1]).then_inc(p.sems["G"], 16)
        g.wait_ge(p.sems["G"], p.cnt["G"])
        p.cnt["G"] += 16
        g.dma_start(out=APc(fl.tensor, 0, [[1, 1], [1, 1]]),
                    in_=stamp_sb[0:1, 0:1]).then_inc(p.sems["G"], 16)
        g.wait_ge(p.sems["G"], p.cnt["G"])

    def poll(fl):
        preg = nc.alloc_register(ET.Pool, p.nc.make_name("poll", True))
        creg = nc.alloc_register(ET.Pool, p.nc.make_name("pcond", True))
        g.reg_mov(creg, 1)
        with g.While(creg):
            g.reg_load(preg, APc(fl.tensor, 0, [[1, 1], [1, 1]]))
            g.reg_alu(creg, preg, MAGIC, ALU.not_equal)

    def reads(srcs):
        halocol = [0, 1025, 1026, 2051]
        for i, (sh, slot) in enumerate(srcs):
            for k in range(NK):
                src = APc(sh.tensor, slot * D + k * 128, [[1, 128], [1, 1]])
                dst = _sap(xlnp, k * PADW + halocol[i], [[1, 1]])
                p.cnt["G"] += 16
                g.dma_start(out=dst, in_=src).then_inc(p.sems["G"], 16)
        g.wait_ge(p.sems["G"], p.cnt["G"])

    gc0 = p.cnt["G"]
    with g.If_eq(parity, 0):
        writes(shA, flA)
        poll(flB)
        reads([(shA, 4), (shB, 0), (shB, 3), (shA, 4)])
    gc_after = p.cnt["G"]
    with g.Else():
        p.cnt["G"] = gc0
        writes(shB, flB)
        poll(flA)
        reads([(shA, 1), (shB, 2), (shB, 1), (shA, 2)])
        assert p.cnt["G"] == gc_after, "halo branch sem counts diverge"
    p.cnt["G"] = gc_after
    p.barrier()


def emit_conv(p, bufs, xlnp, cw_in, qkv_scr):
    nc = p.nc
    g, tensor, vector, scalar = nc.gpsimd, nc.tensor, nc.vector, nc.scalar
    strips = bufs["cstrips"]
    cb_sb = bufs["cb_sb"]
    ev = bufs["conv_ev"]
    base = [1, 513, 1027, 1539]
    n_oc = 24
    p.sem("EVcv")
    estore = {}
    with nc.psum_tensor("cv_p0", [128, 512], F32) as p0, \
         nc.psum_tensor("cv_p1", [128, 512], F32) as p1:
        pss = [p0, p1]
        step = 0
        for t in range(3):
            p.dma(g, strips[0][t][:, :], cw_in[t, 0], "G", wait=False)
        gtargetC = [p.cnt["G"]]
        tstoreC = {}
        for oc in range(n_oc):
            if oc + 1 < n_oc:
                if oc >= 1:
                    g.wait_ge(p.sems["T"], tstoreC[oc - 1])
                for t in range(3):
                    p.dma(g, strips[(oc + 1) % 2][t][:, :], cw_in[t, oc + 1],
                          "G", wait=False)
                gtargetC.append(p.cnt["G"])
            sb3 = strips[oc % 2]
            tensor.wait_ge(p.sems["G"], gtargetC[oc])
            for tt in range(NTT):
                psu = pss[step % 2]
                if step >= 2:
                    tensor.wait_ge(p.sems["EVcv"], step - 1)
                first = True
                for t in range(3):
                    for k in range(NK):
                        mm = tensor.matmul(
                            psu[:, :], sb3[t][:, k * 128:(k + 1) * 128],
                            _sap(xlnp, k * PADW + base[tt] + t - 1,
                                 [[1, 512]]),
                            start=first, stop=(t == 2 and k == NK - 1))
                        first = False
                p.inc(mm, "T", 1)
                evb = ev[step % 2]
                p.wait(vector, "T")
                if step >= 2:
                    p.wait(vector, "E", estore[step - 2])
                v = vector.tensor_scalar(evb[:, :], psu[:, :],
                                         cb_sb[:, oc:oc + 1], None, ALU.add)
                p.inc(v, "EVcv", 1)
                p.wait(scalar, "EVcv")
                p.inc(scalar.dma_start(
                    out=qkv_scr[oc * 128:(oc + 1) * 128,
                                tt * 512:(tt + 1) * 512],
                    in_=evb[:, :]), "E", 16)
                estore[step] = p.cnt["E"]
                step += 1
            tstoreC[oc] = p.cnt["T"]
    p.wait(scalar, "E")
    p.barrier()


def emit_l0attn_blocks(p, bufs, qkv_scr, x_state):
    nc = p.nc
    sync, scalar, tensor, vector, g = (nc.sync, nc.scalar, nc.tensor,
                                       nc.vector, nc.gpsimd)
    wq_sb, wk_sb = bufs["wq_sb"], bufs["wk_sb"]
    wv_sb = bufs["wv_sb"]
    wo_sb = bufs["wo_sb"]
    proj_sb = bufs["proj_sb"]
    bq_sb, bk_sb = bufs["bq_sb"], bufs["bk_sb"]
    bvb_sb = bufs["bvb_sb"]
    bo_sb, pb_sb = bufs["bo_sb"], bufs["pb_sb"]
    tril_sb = bufs["tril_sb"]
    eyeb = bufs["eye_bf"]
    qsb = bufs["q_sb"]; ksb = bufs["k_sb"]; vsb = bufs["v_sb"]
    qh_sb, kh_sb = bufs["qh_sb"], bufs["kh_sb"]
    vh_sb = bufs["vh_sb"]
    sm_sb = bufs["sm_sb"]
    e_sb = bufs["e_sb"]
    et_sb = bufs["et_sb"]
    o_sb = bufs["o_sb"]
    oT_sb = bufs["oT_sb"]
    o2_sb = bufs["o2_sb"]
    mx1, rs1 = bufs["mx"], bufs["rowsum"]
    yev = bufs["yev"]

    SC = 0.125
    for blk in range(16):
        p.wait(sync, "T")
        for ch in range(8):
            for dstbuf, row0 in ((qsb, 0), (ksb, 1024), (vsb, 2048)):
                p.dma(sync, dstbuf[:, ch * 128:(ch + 1) * 128],
                      qkv_scr[row0 + ch * 128:row0 + (ch + 1) * 128,
                              blk * 128:(blk + 1) * 128], "S", wait=False)
        p.wait(sync, "S")
        p.barrier()
        with nc.psum_tensor(f"qh_ps{blk}", [128, 128], F32) as qh_ps, \
             nc.psum_tensor(f"kh_ps{blk}", [128, 128], F32) as kh_ps, \
             nc.psum_tensor(f"vh_ps{blk}", [128, 1024], F32) as vh_ps:
            for hp in range(8):
                for c in range(8):
                    tensor.matmul(qh_ps[:, :],
                                  wq_sb[:, (hp * 8 + c) * 128:
                                        (hp * 8 + c) * 128 + 128],
                                  qsb[:, c * 128:(c + 1) * 128],
                                  start=(c == 0), stop=(c == 7))
                    mm = tensor.matmul(kh_ps[:, :],
                                       wk_sb[:, (hp * 8 + c) * 128:
                                             (hp * 8 + c) * 128 + 128],
                                       ksb[:, c * 128:(c + 1) * 128],
                                       start=(c == 0), stop=(c == 7))
                p.inc(mm, "T", 1)
                p.wait(vector, "T")
                vector.tensor_scalar(qh_sb[:, hp * 128:(hp + 1) * 128],
                                     qh_ps[:, :], bq_sb[:, hp:hp + 1], SC,
                                     ALU.add, ALU.mult)
                p.inc(vector.tensor_scalar(
                    kh_sb[:, hp * 128:(hp + 1) * 128], kh_ps[:, :],
                    bk_sb[:, hp:hp + 1], None, ALU.add), "V", 1)
                p.wait(tensor, "V")
            for c in range(8):
                for nn in range(2):
                    mm = tensor.matmul(
                        vh_ps[:, nn * 512:(nn + 1) * 512],
                        vsb[:, c * 128:(c + 1) * 128],
                        wv_sb[:, c * 1024 + nn * 512:c * 1024 + nn * 512 + 512],
                        start=(c == 0), stop=(c == 7))
            p.inc(mm, "T", 1)
            p.wait(vector, "T")
            p.inc(vector.tensor_tensor(vh_sb[:, :], vh_ps[:, :],
                                       bvb_sb[:, :], ALU.add), "V", 1)
            p.wait(tensor, "V")
        with nc.psum_tensor(f"s_ps{blk}", [128, 128], F32) as s_ps, \
             nc.psum_tensor(f"et_ps{blk}", [128, 128], BF16) as et_ps, \
             nc.psum_tensor(f"pv_ps{blk}", [128, 64], F32) as pv_ps:
            for hh in range(16):
                hp, hi = hh // 2, hh % 2
                lq = qh_sb[hi * 64:hi * 64 + 64, hp * 128:(hp + 1) * 128]
                lk = kh_sb[hi * 64:hi * 64 + 64, hp * 128:(hp + 1) * 128]
                p.inc(tensor.matmul(s_ps[:, :], lq, lk, start=True,
                                    stop=True), "T", 1)
                p.wait(vector, "T")
                p.inc(vector.tensor_tensor(sm_sb[:, :], s_ps[:, :],
                                           tril_sb[:, :], ALU.add), "V", 1)
                p.wait(vector, "V")
                p.inc(vector.reduce_max(mx1[:, :], sm_sb[:, :], axis=AX.X,
                                        negate=True), "V", 1)
                p.wait(scalar, "V")
                p.inc(scalar.activation(out=e_sb[:, :], in_=sm_sb[:, :],
                                        func=AF.Exp, bias=mx1[:, 0:1],
                                        scale=1.0, accum_out=rs1[:, :]),
                      "A", 1)
                p.wait(vector, "A")
                p.inc(vector.reciprocal(rs1[:, :], rs1[:, :]), "V", 1)
                p.wait(tensor, "A")
                p.inc(tensor.transpose(et_ps[:, :], e_sb[:, :], eyeb[:, :]),
                      "T", 1)
                p.wait(vector, "T")
                p.inc(vector.tensor_copy(et_sb[:, :], et_ps[:, :]), "V", 1)
                p.wait(tensor, "V")
                p.inc(tensor.matmul(pv_ps[:, :], et_sb[:, :],
                                    vh_sb[:, hh * 64:(hh + 1) * 64],
                                    start=True, stop=True), "T", 1)
                p.wait(vector, "T")
                p.inc(vector.tensor_scalar(o_sb[:, hh * 64:(hh + 1) * 64],
                                           pv_ps[:, :], rs1[:, 0:1], None,
                                           ALU.mult), "V", 1)
                p.wait(tensor, "V")
        with nc.psum_tensor(f"ot_ps{blk}", [128, 128], BF16) as ot_ps:
            for c in range(8):
                p.wait(tensor, "V")
                p.inc(tensor.transpose(ot_ps[:, :],
                                       o_sb[:, c * 128:(c + 1) * 128],
                                       eyeb[:, :]), "T", 1)
                p.wait(vector, "T")
                p.inc(vector.tensor_copy(oT_sb[:, c * 128:(c + 1) * 128],
                                         ot_ps[:, :]), "V", 1)
        with nc.psum_tensor(f"o2_ps{blk}", [128, 128], F32) as o2_ps:
            for wc in range(8):
                p.wait(tensor, "V")
                for c in range(8):
                    mm = tensor.matmul(o2_ps[:, :],
                                       wo_sb[:, (wc * 8 + c) * 128:
                                             (wc * 8 + c) * 128 + 128],
                                       oT_sb[:, c * 128:(c + 1) * 128],
                                       start=(c == 0), stop=(c == 7))
                p.inc(mm, "T", 1)
                p.wait(vector, "T")
                p.inc(vector.tensor_scalar(o2_sb[:, wc * 128:(wc + 1) * 128],
                                           o2_ps[:, :], bo_sb[:, wc:wc + 1],
                                           None, ALU.add), "V", 1)
        with nc.psum_tensor(f"y_ps0{blk}", [128, 128], F32) as y0, \
             nc.psum_tensor(f"y_ps1{blk}", [128, 128], F32) as y1:
            ys = [y0, y1]
            for oc in range(32):
                psu = ys[oc % 2]
                if oc >= 2:
                    p.wait(tensor, "Yev")
                p.wait(tensor, "V")
                for c in range(8):
                    mm = tensor.matmul(psu[:, :],
                                       proj_sb[:, (oc * 8 + c) * 128:
                                               (oc * 8 + c) * 128 + 128],
                                       o2_sb[:, c * 128:(c + 1) * 128],
                                       start=(c == 0), stop=(c == 7))
                p.inc(mm, "T", 1)
                p.wait(vector, "T")
                p.wait(vector, "G")
                v = vector.tensor_scalar(yev[:, :], psu[:, :],
                                         pb_sb[:, oc:oc + 1], None, ALU.add)
                p.inc(v, "Yev", 1)
                p.wait(g, "Yev")
                p.dma(g, x_state[oc * 128:(oc + 1) * 128,
                                 blk * 128:(blk + 1) * 128],
                      yev[:, :], "G", wait=True, accum_op=ALU.add)
    p.wait(g, "G")
    p.barrier()


def build_L2():
    p = Prog("mk_L2")
    nc = p.nc
    sync, scalar, tensor, vector, g = (nc.sync, nc.scalar, nc.tensor,
                                       nc.vector, nc.gpsimd)
    p.sem("Yev")

    x_in = p.din("xT", (D, TOK))
    parts_in = p.din("partials1", (8, 64))
    qs_in = p.din("qs1", (1, TOK))
    dense_in = p.din("dense1", (D, TOK))
    assign_in = p.din("assign", (1, D), I32)
    iota_in = p.din("iota64", (64, 1))
    eye_in = p.din("eye", (128, 128))
    onesrow_in = p.din("onesrow", (1, 128))
    onescol_in = p.din("onescol", (128, 1))
    cw_in = p.din("cw", (3, 24, 128, NK * 128))
    cb_in = p.din("cb", (128, 24))
    wq_in = p.din("wq0", (8, 128, 1024))
    wk_in = p.din("wk0", (8, 128, 1024))
    wv_in = p.din("wv0r", (8, 128, 1024))
    wo_in = p.din("wo0", (8, 128, 1024))
    proj_in = p.din("proj0", (32, 128, 1024))
    bq_in = p.din("bq0", (128, 8))
    bk_in = p.din("bk0", (128, 8))
    bvb_in = p.din("bv0b", (128, 1024))
    bo_in = p.din("bo0", (128, 8))
    pb_in = p.din("pb0", (128, 32))
    tril_in = p.din("trilm", (128, 128))
    w0f = p.din("w0f", (32, 128, NK * 128))
    b0f_in = p.din("b0f", (128, 32))
    centT_in = p.din("centT", (NK, 128, 64))
    stamp_in = p.din("stamp", (1, 1), I32)

    x_state = p.dout("x2", (D, TOK))
    partials_out = p.dout("partials3", (1, 64))
    qs_out = p.dout("qs3", (1, TOK))
    dense_out = p.dout("dense3", (D, TOK))

    qkv_scr = p.dscr("qkv_scr", (3072, TOK), BF16)
    shA = p.dscr("sh_haloA", (5, D), BF16, shared=True)
    shB = p.dscr("sh_haloB", (5, D), BF16, shared=True)
    flA = p.dscr("sh_flagA", (1, 4), I32, shared=True)
    flB = p.dscr("sh_flagB", (1, 4), I32, shared=True)

    import os
    STAGE = int(os.environ.get("L2STAGE", "3"))
    DBG = os.environ.get("L2DBG", "") == "1"
    if DBG:
        dbg_qkv = p.dout("dbg_qkv", (3072, TOK), F32)

    bufs = {}
    bufs["xf32"] = [p.sb("xf32a", (128, TOK)), p.sb("xf32b", (128, TOK))]
    bufs["sq"] = p.sb("sq", (128, TOK))
    bufs["ones_col"] = p.sb("ones_col", (128, 1))
    bufs["onesrow"] = p.sb("onesrow_sb", (1, 128))
    bufs["eye_f32"] = p.sb("eye_sb", (128, 128))
    bufs["rs_half"] = p.sb("rs_half", (128, 32))
    bufs["b0f_sb"] = p.sb("b0f_sb", (128, 32))

    with p.es, nc.body("main", valid_engines=ALL_ENG):
        p.dma(sync, bufs["ones_col"][:, :], onescol_in, "S", wait=False)
        p.dma(sync, bufs["onesrow"][:, :], onesrow_in, "S", wait=False)
        p.dma(sync, bufs["eye_f32"][:, :], eye_in, "S", wait=False)
        p.dma(sync, bufs["b0f_sb"][:, :], b0f_in, "S", wait=False)
        p.dma(sync, x_state, x_in, "S", wait=False)
        p.wait(sync, "S")
        p.barrier()

        with ExitStack() as esc:
            for nm, shp, dt in (("parts_sb", (8, 64), F32),
                                ("csel_col", (64, 1), F32),
                                ("ass_i", (1, D), I32), ("ass_f", (1, D), F32),
                                ("iota_sb", (64, 1), F32),
                                ("oh_sb", (64, D), F32),
                                ("qs_row", (1, TOK), F32),
                                ("QS_sb", (128, TOK), F32)):
                bufs[nm] = p.sb(nm, shp, dt, es=esc)
            p.dma(sync, bufs["iota_sb"][:, :], iota_in, "S", wait=False)
            emit_csel(p, bufs, parts_in, qs_in, assign_in, tag="c1")
            emit_maskaccum(p, bufs, dense_in, x_state, tag="m1")

        if STAGE >= 2:
            with ExitStack() as esl:
                xlnp = p.sb("xlnp", (128, NK * PADW), BF16, es=esl)
                with ExitStack() as esln2:
                    bufs["rsb"] = p.sb("rsb2", (128, TOK), F32, es=esln2)
                    bufs["bb"] = p.sb("bb2", (128, TOK), F32, es=esln2)
                    with ExitStack() as esr:
                        for nm, shp, dt in (("srow", (1, TOK), F32),
                                            ("qrow", (1, TOK), F32),
                                            ("trow", (1, TOK), F32)):
                            bufs[nm] = p.sb(nm, shp, dt, es=esr)
                        emit_ln_stats(p, bufs, x_state, tag="2")
                    emit_ln_norm_padded(p, bufs, x_state, xlnp)
                with ExitStack() as esh:
                    bufs["zcol_bf"] = p.sb("zcol_bf", (128, 1), BF16, es=esh)
                    bufs["stamp_sb"] = p.sb("stamp_sb", (1, 1), I32, es=esh)
                    emit_halo(p, bufs, xlnp, shA, shB, flA, flB, stamp_in)
                with ExitStack() as esv:
                    bufs["cstrips"] = [
                        [p.sb(f"cst{b}_{t}", (128, NK * 128), BF16, es=esv)
                         for t in range(3)] for b in range(2)]
                    bufs["cb_sb"] = p.sb("cb_sb", (128, 24), F32, es=esv)
                    bufs["conv_ev"] = [p.sb("cev0", (128, 512), BF16, es=esv),
                                       p.sb("cev1", (128, 512), BF16, es=esv)]
                    p.dma(sync, bufs["cb_sb"][:, :], cb_in, "S")
                    emit_conv(p, bufs, xlnp, cw_in, qkv_scr)
                    if DBG:
                        for r in range(24):
                            p.dma(g, dbg_qkv[r * 128:(r + 1) * 128, :],
                                  qkv_scr[r * 128:(r + 1) * 128, :], "G",
                                  wait=False)
                        p.wait(g, "G")
                        p.barrier()
            with ExitStack() as esa:
                for nm, shp, dt, src in (
                        ("wq_sb", (128, 8 * 1024), BF16, wq_in),
                        ("wk_sb", (128, 8 * 1024), BF16, wk_in),
                        ("wv_sb", (128, 8 * 1024), BF16, wv_in),
                        ("wo_sb", (128, 8 * 1024), BF16, wo_in),
                        ("proj_sb", (128, 32 * 1024), BF16, proj_in)):
                    bufs[nm] = p.sb(nm, shp, dt, es=esa)
                    p.dma(g, bufs[nm][:, :].rearrange(
                        "p (a b) -> p a b", b=src.shape[-1]),
                        src.rearrange("a p b -> p a b"), "G", wait=False)
                for nm, shp, src in (("bq_sb", (128, 8), bq_in),
                                     ("bk_sb", (128, 8), bk_in),
                                     ("bvb_sb", (128, 1024), bvb_in),
                                     ("bo_sb", (128, 8), bo_in),
                                     ("pb_sb", (128, 32), pb_in),
                                     ("tril_sb", (128, 128), tril_in)):
                    bufs[nm] = p.sb(nm, shp, F32, es=esa)
                    p.dma(sync, bufs[nm][:, :], src, "S", wait=False)
                bufs["eye_bf"] = p.sb("eye_bf", (128, 128), BF16, es=esa)
                p.dma(g, bufs["eye_bf"][:, :], eye_in, "G", wait=False)
                for nm, shp, dt in (
                        ("q_sb", (128, 1024), BF16), ("k_sb", (128, 1024), BF16),
                        ("v_sb", (128, 1024), BF16),
                        ("qh_sb", (128, 8 * 128), BF16),
                        ("kh_sb", (128, 8 * 128), BF16),
                        ("vh_sb", (128, 1024), BF16),
                        ("sm_sb", (128, 128), F32), ("e_sb", (128, 128), BF16),
                        ("et_sb", (128, 128), BF16),
                        ("o_sb", (128, 1024), BF16),
                        ("oT_sb", (128, 8 * 128), BF16),
                        ("o2_sb", (128, 8 * 128), BF16),
                        ("mx", (128, 1), F32), ("rowsum", (128, 1), F32),
                        ("yev", (128, 128), F32)):
                    bufs[nm] = p.sb(nm, shp, dt, es=esa)
                p.wait(sync, "S")
                p.wait(g, "G")
                p.barrier()
                emit_l0attn_blocks(p, bufs, qkv_scr, x_state)

        if STAGE >= 3:
            with ExitStack() as es2:
                bufs["xln"] = p.sb("xln", (128, NK * TOK), BF16, es=es2)
                bufs["rsb"] = p.sb("rsb3", (128, TOK), F32, es=es2)
                bufs["bb"] = p.sb("bb3", (128, TOK), F32, es=es2)
                with ExitStack() as es2b:
                    for nm, shp, dt in (("srow", (1, TOK), F32),
                                        ("qrow", (1, TOK), F32),
                                        ("trow", (1, TOK), F32)):
                        bufs[nm] = p.sb(nm, shp, dt, es=es2b)
                    emit_ln_stats(p, bufs, x_state, tag="3")
                with ExitStack() as es2c:
                    for nm, shp, dt in (("centT_sb", (128, NK * 64), F32),
                                        ("e64", (128, 64), F32),
                                        ("hits", (128, 64), F32),
                                        ("mx", (128, 1), F32),
                                        ("rowsum", (128, 1), F32),
                                        ("qsc", (128, 16), F32),
                                        ("cnt_sb", (1, 64), F32),
                                        ("qsr", (16, 128), F32)):
                        bufs[nm] = p.sb(nm, shp, dt, es=es2c)
                    p.dma(sync,
                          bufs["centT_sb"][:, :].rearrange(
                              "p (k j) -> p k j", k=NK),
                          centT_in.rearrange("k p j -> p k j"), "S",
                          wait=False)
                    p.wait(sync, "S")
                    p.barrier()
                    with nc.psum_tensor("ps_dots3", [128, 1024],
                                        F32) as ps_dots:
                        emit_ln_norm(p, bufs, x_state, bufs["xln"],
                                     consume_f32=lambda k: emit_dots(
                                         p, bufs, ps_dots, k))
                        emit_gate(p, bufs, ps_dots, partials_out, qs_out,
                                  tag="3")
                with ExitStack() as es3:
                    bufs["strips"] = [
                        p.sb("stripA", (128, NK * 128), BF16, es=es3),
                        p.sb("stripB", (128, NK * 128), BF16, es=es3)]
                    bufs["dense_ev"] = [
                        p.sb("dense_ev0", (128, 512), F32, es=es3),
                        p.sb("dense_ev1", (128, 512), F32, es=es3)]
                    estore = {}

                    def dense_cb(oc, tt, psu, step):
                        evb = bufs["dense_ev"][step % 2]
                        p.wait(vector, "T")
                        if step >= 2:
                            p.wait(vector, "E", estore[step - 2])
                        v = vector.tensor_scalar(evb[:, :], psu[:, :],
                                                 bufs["b0f_sb"][:, oc:oc + 1],
                                                 None, ALU.add)
                        p.inc(v, "EVd3", 1)
                        p.wait(scalar, "EVd3")
                        p.inc(scalar.dma_start(
                            out=dense_out[oc * 128:(oc + 1) * 128,
                                          tt * 512:(tt + 1) * 512],
                            in_=evb[:, :]), "E", 16)
                        estore[step] = p.cnt["E"]

                    p.sem("EVd3")
                    emit_linear(p, bufs, bufs["xln"], w0f, 32, dense_cb,
                                tag="d3")

        p.wait(sync, "E")
        p.wait(sync, "S")
        p.wait(sync, "G")
    return p




MAGIC3 = 0x5A18


def emit_l1ffn(p, bufs, x_state, w12_in, b12_in, w3_in, b3_in, x4_scr,
               tag=""):
    nc = p.nc
    sync, scalar, tensor, vector, g = (nc.sync, nc.scalar, nc.tensor,
                                       nc.vector, nc.gpsimd)
    with ExitStack() as esf:
        xln = p.sb("xlnF" + tag, (128, NK * TOK), BF16, es=esf)
        with ExitStack() as esr:
            for nm in ("srow", "qrow", "trow"):
                bufs[nm] = p.sb(nm + "F" + tag, (1, TOK), F32, es=esr)
            emit_ln_stats(p, bufs, x_state, tag="F" + tag)
        emit_ln_norm(p, bufs, x_state, xln)
        with ExitStack() as esw:
            st1b = [p.sb("w12sA%d" % i + tag, (128, NK * 128), BF16,
                         es=esw) for i in range(2)]
            st2b = [p.sb("w12sB%d" % i + tag, (128, NK * 128), BF16,
                         es=esw) for i in range(2)]
            b12_sb = p.sb("b12_sb" + tag, (128, 64), F32, es=esw)
            t1 = p.sb("t1_" + tag, (128, 512), F32, es=esw)
            x4ev = [p.sb(f"x4ev{i}_" + tag, (128, 512), BF16, es=esw)
                    for i in range(2)]
            p.dma(sync, b12_sb[:, :], b12_in, "S")
            ev = "EVf" + tag
            p.sem(ev)
            estore = {}
            step = 0
            tstore = {}
            with nc.psum_tensor("f1psa" + tag, [128, 512], F32) as ps1a, \
                 nc.psum_tensor("f2psa" + tag, [128, 512], F32) as ps2a, \
                 nc.psum_tensor("f1psb" + tag, [128, 512], F32) as ps1b, \
                 nc.psum_tensor("f2psb" + tag, [128, 512], F32) as ps2b:
                p.dma(g, st1b[0][:, :], w12_in[0], "G", wait=False)
                p.dma(g, st2b[0][:, :], w12_in[32], "G", wait=False)
                gtarget = [p.cnt["G"]]
                for j in range(32):
                    if j + 1 < 32:
                        if j >= 1:
                            g.wait_ge(p.sems["T"], tstore[j - 1])
                        p.dma(g, st1b[(j + 1) % 2][:, :], w12_in[j + 1],
                              "G", wait=False)
                        p.dma(g, st2b[(j + 1) % 2][:, :], w12_in[j + 33],
                              "G", wait=False)
                        gtarget.append(p.cnt["G"])
                    st1, st2 = st1b[j % 2], st2b[j % 2]
                    tensor.wait_ge(p.sems["G"], gtarget[j])
                    for tt in range(NTT):
                        ps1 = ps1a if step % 2 == 0 else ps1b
                        ps2 = ps2a if step % 2 == 0 else ps2b
                        if step >= 2:
                            tensor.wait_ge(p.sems[ev], step - 1)
                        for k in range(NK):
                            sl = slice(k * TOK + tt * 512,
                                       k * TOK + tt * 512 + 512)
                            tensor.matmul(ps1[:, :],
                                          st1[:, k * 128:(k + 1) * 128],
                                          xln[:, sl], start=(k == 0),
                                          stop=(k == NK - 1))
                            mm = tensor.matmul(ps2[:, :],
                                               st2[:, k * 128:(k + 1) * 128],
                                               xln[:, sl], start=(k == 0),
                                               stop=(k == NK - 1))
                        p.inc(mm, "T", 1)
                        scalar.wait_ge(p.sems["T"], p.cnt["T"])
                        p.inc(scalar.activation(
                            out=t1[:, :], in_=ps1[:, :], func=AF.Silu,
                            bias=b12_sb[:, j:j + 1], scale=1.0), "A", 1)
                        vector.wait_ge(p.sems["A"], p.cnt["A"])
                        if step >= 2:
                            p.wait(vector, "E", estore[step - 2])
                        evb = x4ev[step % 2]
                        v = vector.scalar_tensor_tensor(
                            evb[:, :], ps2[:, :], b12_sb[:, j + 32:j + 33],
                            t1[:, :], ALU.add, ALU.mult)
                        p.inc(v, ev, 1)
                        scalar.wait_ge(p.sems[ev], p.cnt[ev])
                        p.inc(scalar.dma_start(
                            out=x4_scr[j * 128:(j + 1) * 128,
                                       tt * 512:(tt + 1) * 512],
                            in_=evb[:, :]), "E", 16)
                        estore[step] = p.cnt["E"]
                        step += 1
                    tstore[j] = p.cnt["T"]
            p.wait(scalar, "E")
            p.barrier()
    with ExitStack() as esw3:
        x4r = p.sb("x4r" + tag, (128, NK * TOK), BF16, es=esw3)
        b3_sb = p.sb("b3_sb" + tag, (128, 32), F32, es=esw3)
        bufs["strips"] = [p.sb("w3sA" + tag, (128, NK * 128), BF16, es=esw3),
                          p.sb("w3sB" + tag, (128, NK * 128), BF16, es=esw3)]
        yev = [p.sb(f"yf{i}_" + tag, (128, 512), F32, es=esw3)
               for i in range(2)]
        p.dma(sync, b3_sb[:, :], b3_in, "S", wait=False)
        for k in range(NK):
            p.dma(g, x4r[:, k * TOK:(k + 1) * TOK],
                  x4_scr[k * 128:(k + 1) * 128, :], "G", wait=False)
        p.wait(sync, "S")
        p.wait(g, "G")
        p.barrier()
        gstore = {}
        evn = "EVw3" + tag
        p.sem(evn)

        def w3_cb(oc, tt, psu, step):
            evb = yev[step % 2]
            p.wait(vector, "T")
            if step >= 2:
                p.wait(vector, "G", gstore[step - 2])
            v = vector.tensor_scalar(evb[:, :], psu[:, :],
                                     b3_sb[:, oc:oc + 1], 0.5,
                                     ALU.add, ALU.mult)
            p.inc(v, evn, 1)
            p.wait(g, evn)
            p.inc(g.dma_start(out=x_state[oc * 128:(oc + 1) * 128,
                                          tt * 512:(tt + 1) * 512],
                              in_=evb[:, :], accum_op=ALU.add), "G", 16)
            gstore[step] = p.cnt["G"]

        emit_linear2(p, bufs, x4r, w3_in, 32, w3_cb, free_sem=evn,
                     tag="w3" + tag)
        p.wait(g, "G")
    p.barrier()


def emit_linear2(p, bufs, xsrc_sb, strips_hbm, n_oc, out_cb, free_sem,
                 n_tok=TOK, nk=NK, tag=""):
    nc = p.nc
    g, tensor = nc.gpsimd, nc.tensor
    nts = n_tok // 512
    sbufs = bufs["strips"]
    with nc.psum_tensor(f"lin_p0{tag}", [128, 512], F32) as p0, \
         nc.psum_tensor(f"lin_p1{tag}", [128, 512], F32) as p1:
        pss = [p0, p1]
        p.dma(g, sbufs[0][:, :], strips_hbm[0], "G", wait=False)
        gtarget = [p.cnt["G"]]
        tload = [None, None]
        step = 0
        for oc in range(n_oc):
            if oc + 1 < n_oc:
                if tload[(oc + 1) % 2] is not None:
                    g.wait_ge(p.sems["T"], tload[(oc + 1) % 2])
                p.dma(g, sbufs[(oc + 1) % 2][:, :], strips_hbm[oc + 1], "G",
                      wait=False)
                gtarget.append(p.cnt["G"])
            sbw = sbufs[oc % 2]
            tensor.wait_ge(p.sems["G"], gtarget[oc])
            for tt in range(nts):
                psu = pss[step % 2]
                if step >= 2:
                    tensor.wait_ge(p.sems[free_sem], step - 1)
                for k in range(nk):
                    mm = tensor.matmul(
                        psu[:, :], sbw[:, k * 128:(k + 1) * 128],
                        xsrc_sb[:, k * n_tok + tt * 512:
                                k * n_tok + tt * 512 + 512],
                        start=(k == 0), stop=(k == nk - 1))
                p.inc(mm, "T", 1)
                out_cb(oc, tt, psu, step)
                step += 1
            tload[oc % 2] = p.cnt["T"]
    p.barrier()


def _gq(parity, qb):
    return (qb if qb < 8 else 16 + qb) if parity == 0 else 8 + qb


def _own(gg):
    if gg < 8:
        return 0, gg
    if gg >= 24:
        return 0, gg - 16
    return 1, gg - 8


def emit_l1attn(p, bufs, x_state, wq_in, wk_in, wv_in, wo_in, cos_in,
                sin_in, mask_in, stamp_in, shKA, shKB, shVA, shVB, flA, flB,
                q_scr, myK, myV, att_scr):
    nc = p.nc
    sync, scalar, tensor, vector, g = (nc.sync, nc.scalar, nc.tensor,
                                       nc.vector, nc.gpsimd)
    SC = 1.0 / np.sqrt(128.0)

    with ExitStack() as esa:
        xln = p.sb("xlnA", (128, NK * TOK), BF16, es=esa)
        with ExitStack() as esr:
            for nm in ("srow", "qrow", "trow"):
                bufs[nm] = p.sb(nm + "A1", (1, TOK), F32, es=esr)
            emit_ln_stats(p, bufs, x_state, tag="A1")
        emit_ln_norm(p, bufs, x_state, xln)
        with ExitStack() as esw:
            cos_sb = p.sb("cos_sb", (128, TOK), F32, es=esw)
            sin_sb = p.sb("sin_sb", (128, TOK), F32, es=esw)
            stwb = [p.sb("l1aw%d" % i, (128, NK * 128), BF16, es=esw)
                    for i in range(2)]
            rta = p.sb("rta", (64, 512), F32, es=esw)
            rtb = p.sb("rtb", (64, 512), F32, es=esw)
            rev = [p.sb(f"l1rev{i}", (128, 512), BF16, es=esw)
                   for i in range(2)]
            p.dma(sync, cos_sb[:, :], cos_in, "S", wait=False)
            p.dma(sync, sin_sb[:, :], sin_in, "S", wait=False)
            p.wait(sync, "S")
            p.barrier()
            estore = {}
            vstore = {}
            step = 0
            seq = [("K", wk_in, myK, True), ("Q", wq_in, q_scr, True),
                   ("V", wv_in, myV, False)]
            flat = [(wh, w_in, dst, rot, oc)
                    for (wh, w_in, dst, rot) in seq for oc in range(32)]
            p.dma(g, stwb[0][:, :], flat[0][1][0], "G", wait=False)
            gtargetA = [p.cnt["G"]]
            tstoreA = {}
            with nc.psum_tensor("l1pA0", [128, 512], F32) as psA0, \
                 nc.psum_tensor("l1pA1", [128, 512], F32) as psA1:
                for fi, (which, w_in, dst, do_rot, oc) in enumerate(flat):
                    if fi + 1 < len(flat):
                        if fi >= 1:
                            g.wait_ge(p.sems["T"], tstoreA[fi - 1])
                        nxt = flat[fi + 1]
                        p.dma(g, stwb[(fi + 1) % 2][:, :], nxt[1][nxt[4]],
                              "G", wait=False)
                        gtargetA.append(p.cnt["G"])
                    stw = stwb[fi % 2]
                    tensor.wait_ge(p.sems["G"], gtargetA[fi])
                    for tt in range(NTT):
                        psA = psA0 if step % 2 == 0 else psA1
                        if step >= 2:
                            tensor.wait_ge(p.sems["V"], vstore[step - 2])
                        for k in range(NK):
                            sl = slice(k * TOK + tt * 512,
                                       k * TOK + tt * 512 + 512)
                            mm = tensor.matmul(
                                psA[:, :], stw[:, k * 128:(k + 1) * 128],
                                xln[:, sl], start=(k == 0),
                                stop=(k == NK - 1))
                        p.inc(mm, "T", 1)
                        vector.wait_ge(p.sems["T"], p.cnt["T"])
                        if step >= 2:
                            p.wait(vector, "E", estore[step - 2])
                        evb = rev[step % 2]
                        sl2 = slice(tt * 512, (tt + 1) * 512)
                        if do_rot:
                            pe, po = psA[0:64, :], psA[64:128, :]
                            ce = cos_sb[0:64, sl2]
                            se = sin_sb[0:64, sl2]
                            vector.tensor_tensor(rta[:, :], pe, ce, ALU.mult)
                            vector.tensor_tensor(rtb[:, :], po, se, ALU.mult)
                            vector.tensor_sub(evb[0:64, :], rta[:, :],
                                              rtb[:, :])
                            vector.tensor_tensor(rta[:, :], pe, se, ALU.mult)
                            vector.tensor_tensor(rtb[:, :], po, ce, ALU.mult)
                            v = vector.tensor_add(evb[64:128, :],
                                                  rta[:, :], rtb[:, :])
                        else:
                            v = vector.tensor_copy(evb[:, :], psA[:, :])
                        p.inc(v, "V", 1)
                        vstore[step] = p.cnt["V"]
                        scalar.wait_ge(p.sems["V"], p.cnt["V"])
                        p.inc(scalar.dma_start(
                            out=dst[oc * 128:(oc + 1) * 128,
                                    tt * 512:(tt + 1) * 512],
                            in_=evb[:, :]), "E", 16)
                        estore[step] = p.cnt["E"]
                        step += 1
                    tstoreA[fi] = p.cnt["T"]
            p.wait(scalar, "E")
            p.barrier()

    stamp_sb = bufs["stamp_sb"]
    p.dma(g, stamp_sb[:, :], stamp_in, "G")
    me = g.partition_id()
    parity = g.to_reg(me % 2)
    gc0 = p.cnt["G"]

    def shr_copy(dstK, dstV, fl):
        p.cnt["G"] += 16
        g.dma_start(out=dstK, in_=myK).then_inc(p.sems["G"], 16)
        p.cnt["G"] += 16
        g.dma_start(out=dstV, in_=myV).then_inc(p.sems["G"], 16)
        g.wait_ge(p.sems["G"], p.cnt["G"])
        p.cnt["G"] += 16
        g.dma_start(out=APc(fl.tensor, 0, [[1, 1], [1, 1]]),
                    in_=stamp_sb[0:1, 0:1]).then_inc(p.sems["G"], 16)
        g.wait_ge(p.sems["G"], p.cnt["G"])

    def poll3(fl):
        preg = nc.alloc_register(ET.Pool, nc.make_name("poll3", True))
        creg = nc.alloc_register(ET.Pool, nc.make_name("pcond3", True))
        g.reg_mov(creg, 1)
        with g.While(creg):
            g.reg_load(preg, APc(fl.tensor, 0, [[1, 1], [1, 1]]))
            g.reg_alu(creg, preg, MAGIC3, ALU.not_equal)

    with g.If_eq(parity, 0):
        shr_copy(shKA, shVA, flA)
        poll3(flB)
    gc_after = p.cnt["G"]
    with g.Else():
        p.cnt["G"] = gc0
        shr_copy(shKB, shVB, flB)
        poll3(flA)
    p.cnt["G"] = gc_after
    p.barrier()

    with ExitStack() as esb:
        mask_sb = p.sb("mask_sb", (128, 144 * 128), BF16, es=esb)
        ksA = [p.sb(f"ksA{i}", (128, TOK), BF16, es=esb) for i in range(2)]
        ksB = [p.sb(f"ksB{i}", (128, TOK), BF16, es=esb) for i in range(2)]
        vsA = [p.sb(f"vsA{i}", (128, TOK), BF16, es=esb) for i in range(2)]
        vsB = [p.sb(f"vsB{i}", (128, TOK), BF16, es=esb) for i in range(2)]
        qt = [p.sb(f"qt{i}", (128, 128), BF16, es=esb) for i in range(2)]
        ssb = [p.sb(f"ssb{i}", (128, 4096), F32, es=esb) for i in range(2)]
        estrip = [p.sb(f"estrip{i}", (128, 4096), BF16, es=esb)
                  for i in range(2)]
        etstrip = p.sb("etstrip", (128, 4096), BF16, es=esb)
        eyeb = p.sb("eyeb3", (128, 128), BF16, es=esb)
        mxc = [p.sb(f"mxc{i}", (128, 1), F32, es=esb) for i in range(2)]
        rsc = [p.sb(f"rsc{i}", (128, 1), F32, es=esb) for i in range(2)]
        aev = [p.sb(f"aev{i}", (128, 128), BF16, es=esb) for i in range(2)]
        for mq in range(16):
            msl = slice(mq * 9, (mq + 1) * 9)
            p.dma(g, mask_sb[:, mq * 9 * 128:(mq + 1) * 9 * 128].rearrange(
                      "p (m k) -> p m k", m=9),
                  mask_in[msl].rearrange("m p k -> p m k"), "G", wait=False)
        p.dma(g, eyeb[:, :], bufs["eye_in_ap"], "G", wait=False)
        p.wait(g, "G")
        p.barrier()

        def strip_loads(h, par):
            p.dma(sync, ksA[par][:, :], shKA[h * 128:(h + 1) * 128, :], "S",
                  wait=False)
            p.dma(sync, ksB[par][:, :], shKB[h * 128:(h + 1) * 128, :], "S",
                  wait=False)
            for bb2 in range(16):
                p.dma(sync, vsA[par][:, bb2 * 128:(bb2 + 1) * 128],
                      shVA[h * 128:(h + 1) * 128,
                           bb2 * 128:(bb2 + 1) * 128], "S", wait=False,
                      transpose=True)
                p.dma(sync, vsB[par][:, bb2 * 128:(bb2 + 1) * 128],
                      shVB[h * 128:(h + 1) * 128,
                           bb2 * 128:(bb2 + 1) * 128], "S", wait=False,
                      transpose=True)
            return p.cnt["S"]

        def q_load(h, qb, par):
            p.dma(sync, qt[par][:, :],
                  q_scr[h * 128:(h + 1) * 128, qb * 128:(qb + 1) * 128],
                  "S", wait=False)
            return p.cnt["S"]

        items = [(h, qb) for h in range(32) for qb in range(16)]
        sload = {}
        qload = {}
        vev = {}
        aexp = {}
        vcopy = {}
        vpv = {}
        tscore = {}
        ttrans = {}
        estore3 = {}

        sload[0] = strip_loads(0, 0)
        qload[0] = q_load(0, 0, 0)

        with nc.psum_tensor("s3ps0", [128, 512], F32) as sp0, \
             nc.psum_tensor("s3ps1", [128, 512], F32) as sp1, \
             nc.psum_tensor("et3ps", [128, 4096], BF16) as etp, \
             nc.psum_tensor("pv3ps0", [128, 128], F32) as pvp0, \
             nc.psum_tensor("pv3ps1", [128, 128], F32) as pvp1:
            sps = [sp0, sp1]
            pvs = [pvp0, pvp1]
            spst = 0

            def do_s3(j):
                hj, qbj = items[j]
                pj = j % 2
                loj = qbj if qbj < 8 else qbj + 8
                hij = qbj + 8 if qbj < 8 else qbj + 16
                nb = hij + 1
                vAj = vsA[hj % 2]
                vBj = vsB[hj % 2]
                tensor.wait_ge(p.sems["A"], aexp[j])
                if j >= 1:
                    tensor.wait_ge(p.sems["V"], vcopy[j - 1])
                for g2 in range(nb):
                    mm = tensor.transpose(
                        etp[:, g2 * 128:(g2 + 1) * 128],
                        estrip[pj][:, g2 * 128:(g2 + 1) * 128], eyeb[:, :])
                p.inc(mm, "T", 1)
                vector.wait_ge(p.sems["T"], p.cnt["T"])
                v = vector.tensor_copy(etstrip[:, 0:nb * 128],
                                       etp[:, 0:nb * 128])
                p.inc(v, "V", 1)
                vcopy[j] = p.cnt["V"]
                pvp = pvs[j % 2]
                tensor.wait_ge(p.sems["V"], p.cnt["V"])
                if j >= 2:
                    tensor.wait_ge(p.sems["V"], vpv[j - 2])
                for g2 in range(nb):
                    half, lb = _own(g2)
                    vsx = vAj if half == 0 else vBj
                    mm = tensor.matmul(
                        pvp[:, :], etstrip[:, g2 * 128:(g2 + 1) * 128],
                        vsx[:, lb * 128:(lb + 1) * 128],
                        start=(g2 == 0), stop=(g2 == nb - 1))
                p.inc(mm, "T", 1)
                ttrans[j] = p.cnt["T"]
                vector.wait_ge(p.sems["T"], p.cnt["T"])
                if j >= 2:
                    p.wait(vector, "E", estore3.get(j - 2, 0))
                v = vector.tensor_scalar(aev[j % 2][:, :], pvp[:, :],
                                         rsc[pj][:, 0:1], None, ALU.mult)
                p.inc(v, "V", 1)
                vpv[j] = p.cnt["V"]
                scalar.wait_ge(p.sems["V"], p.cnt["V"])
                p.inc(scalar.dma_start(
                    out=att_scr[qbj * 128:(qbj + 1) * 128,
                                hj * 128:(hj + 1) * 128],
                    in_=aev[j % 2][:, :]), "E", 16)
                estore3[j] = p.cnt["E"]
            for idx, (h, qb) in enumerate(items):
                par = idx % 2
                if idx + 1 < len(items):
                    h2, qb2 = items[idx + 1]
                    if qb2 == 0:
                        if h2 >= 2:
                            sync.wait_ge(p.sems["T"],
                                         ttrans[(h2 - 1) * 16 - 1])
                        sload[h2] = strip_loads(h2, h2 % 2)
                    if idx >= 1:
                        sync.wait_ge(p.sems["T"], tscore[idx - 1])
                    qload[idx + 1] = q_load(h2, qb2, (idx + 1) % 2)
                lo = qb if qb < 8 else qb + 8
                hi = qb + 8 if qb < 8 else qb + 16
                nblk = hi + 1
                klen = nblk * 128
                kA, kB = ksA[h % 2], ksB[h % 2]
                vA, vB = vsA[h % 2], vsB[h % 2]

                tensor.wait_ge(p.sems["S"], max(sload[h], qload[idx]))
                if idx >= 2:
                    tensor.wait_ge(p.sems["A"], aexp.get(idx - 2, 0))
                runs = []
                gg = 0
                while gg < nblk:
                    half, lb = _own(gg)
                    run = [gg]
                    while (gg + 1 < nblk and _own(gg + 1)[0] == half and
                           _own(gg + 1)[1] == _own(gg)[1] + 1):
                        gg += 1
                        run.append(gg)
                    runs.append((half, _own(run[0])[1], run))
                    gg += 1
                for half, lb0, run in runs:
                    ks = kA if half == 0 else kB
                    for c0 in range(0, len(run), 4):
                        w = min(4, len(run) - c0) * 128
                        psu = sps[spst % 2]
                        if spst >= 2:
                            vector_tgt = vev.get(("chunk", spst - 2))
                            if vector_tgt:
                                tensor.wait_ge(p.sems["V"], vector_tgt)
                        mm = tensor.matmul(
                            psu[:, 0:w], qt[par][:, :],
                            ks[:, (lb0 + c0) * 128:(lb0 + c0) * 128 + w],
                            start=True, stop=True)
                        p.inc(mm, "T", 1)
                        vector.wait_ge(p.sems["T"], p.cnt["T"])
                        gfirst = run[c0]
                        glast = gfirst + w // 128 - 1
                        dst0 = gfirst * 128
                        if glast < lo:
                            v = vector.tensor_copy(
                                ssb[par][:, dst0:dst0 + w], psu[:, 0:w])
                        elif gfirst >= lo:
                            mi = 9 * qb + (gfirst - lo)
                            v = vector.tensor_tensor(
                                ssb[par][:, dst0:dst0 + w], psu[:, 0:w],
                                mask_sb[:, mi * 128:mi * 128 + w], ALU.add)
                        else:
                            wa = (lo - gfirst) * 128
                            vector.tensor_copy(
                                ssb[par][:, dst0:dst0 + wa], psu[:, 0:wa])
                            mi = 9 * qb
                            v = vector.tensor_tensor(
                                ssb[par][:, dst0 + wa:dst0 + w],
                                psu[:, wa:w],
                                mask_sb[:, mi * 128:mi * 128 + (w - wa)],
                                ALU.add)
                        p.inc(v, "V", 1)
                        vev[("chunk", spst)] = p.cnt["V"]
                        spst += 1
                vev[idx] = p.cnt["V"]
                tscore[idx] = p.cnt["T"]

                if idx >= 1:
                    do_s3(idx - 1)

                vector.wait_ge(p.sems["V"], vev[idx])
                p.inc(vector.reduce_max(mxc[par][:, :], ssb[par][:, 0:klen],
                                        axis=AX.X, negate=True), "V", 1)
                p.wait(vector, "V")
                p.inc(vector.tensor_scalar_mul(mxc[par][:, :],
                                               mxc[par][:, :], SC), "V", 1)
                scalar.wait_ge(p.sems["V"], p.cnt["V"])
                if idx >= 2:
                    scalar.wait_ge(p.sems["T"], ttrans.get(idx - 2, 0))
                p.inc(scalar.activation(out=estrip[par][:, 0:klen],
                                        in_=ssb[par][:, 0:klen], func=AF.Exp,
                                        bias=mxc[par][:, 0:1], scale=SC,
                                        accum_out=rsc[par][:, :]), "A", 1)
                aexp[idx] = p.cnt["A"]
                vector.wait_ge(p.sems["A"], p.cnt["A"])
                p.inc(vector.reciprocal(rsc[par][:, :], rsc[par][:, :]),
                      "V", 1)

            do_s3(len(items) - 1)

        p.wait(scalar, "E")
        p.barrier()

    with ExitStack() as esc:
        attT = p.sb("attT", (128, NK * TOK), BF16, es=esc)
        bufs["strips"] = [p.sb("wosA", (128, NK * 128), BF16, es=esc),
                          p.sb("wosB", (128, NK * 128), BF16, es=esc)]
        yev = [p.sb(f"yo{i}", (128, 512), F32, es=esc) for i in range(2)]
        for c in range(NK):
            for tt in range(NTT):
                p.inc(scalar.dma_start(
                    out=attT[:, c * TOK + tt * 512:c * TOK + tt * 512 + 512],
                    in_=att_scr[tt * 512:(tt + 1) * 512,
                                c * 128:(c + 1) * 128],
                    transpose=True), "E", 16)
        p.wait(scalar, "E")
        p.barrier()
        gstore = {}
        p.sem("EVwo")

        def wo_cb(oc, tt, psu, step):
            evb = yev[step % 2]
            p.wait(vector, "T")
            if step >= 2:
                p.wait(vector, "G", gstore[step - 2])
            v = vector.tensor_copy(evb[:, :], psu[:, :])
            p.inc(v, "EVwo", 1)
            p.wait(g, "EVwo")
            p.inc(g.dma_start(out=x_state[oc * 128:(oc + 1) * 128,
                                          tt * 512:(tt + 1) * 512],
                              in_=evb[:, :], accum_op=ALU.add), "G", 16)
            gstore[step] = p.cnt["G"]

        emit_linear2(p, bufs, attT, wo_in, 32, wo_cb, free_sem="EVwo",
                     tag="wo")
        p.wait(g, "G")
    p.barrier()


def build_L3():
    p = Prog("mk_L3")
    nc = p.nc
    sync, scalar, tensor, vector, g = (nc.sync, nc.scalar, nc.tensor,
                                       nc.vector, nc.gpsimd)

    x_in = p.din("xT3", (D, TOK))
    parts_in = p.din("partials3", (8, 64))
    qs_in = p.din("qs3", (1, TOK))
    dense_in = p.din("dense3", (D, TOK))
    assign_in = p.din("assign", (1, D), I32)
    iota_in = p.din("iota64", (64, 1))
    eye_in = p.din("eye", (128, 128))
    onesrow_in = p.din("onesrow", (1, 128))
    onescol_in = p.din("onescol", (128, 1))
    w12_in = p.din("w12p", (64, 128, NK * 128))
    b12_in = p.din("b12c", (128, 64))
    w3_in = p.din("w3p", (32, 128, NK * 128))
    b3_in = p.din("b3c", (128, 32))
    wq_in = p.din("wqp", (32, 128, NK * 128))
    wk_in = p.din("wkp", (32, 128, NK * 128))
    wv_in = p.din("wvp", (32, 128, NK * 128))
    wo_in = p.din("wop", (32, 128, NK * 128))
    cos_in = p.din("cosb", (128, TOK))
    sin_in = p.din("sinb", (128, TOK))
    mask_in = p.din("maskb", (144, 128, 128), BF16)
    stamp_in = p.din("stamp3", (1, 1), I32)

    x_state = p.dout("xout", (D, TOK))

    x4_scr = p.dscr("x4_scr", (D, TOK), BF16)
    q_scr = p.dscr("q_scr", (D, TOK), BF16)
    myK = p.dscr("myK", (D, TOK), BF16)
    myV = p.dscr("myV", (D, TOK), BF16)
    att_scr = p.dscr("att_scr", (TOK, D), BF16)
    shKA = p.dscr("shKA", (D, TOK), BF16, shared=True)
    shKB = p.dscr("shKB", (D, TOK), BF16, shared=True)
    shVA = p.dscr("shVA", (D, TOK), BF16, shared=True)
    shVB = p.dscr("shVB", (D, TOK), BF16, shared=True)
    flA = p.dscr("flA3", (1, 4), I32, shared=True)
    flB = p.dscr("flB3", (1, 4), I32, shared=True)

    import os
    STAGE = int(os.environ.get("L3STAGE", "4"))

    bufs = {}
    bufs["xf32"] = [p.sb("xf32a", (128, TOK)), p.sb("xf32b", (128, TOK))]
    bufs["sq"] = p.sb("sq", (128, TOK))
    bufs["rsb"] = p.sb("rsb", (128, TOK))
    bufs["bb"] = p.sb("bb", (128, TOK))
    bufs["ones_col"] = p.sb("ones_col", (128, 1))
    bufs["onesrow"] = p.sb("onesrow_sb", (1, 128))
    bufs["rs_half"] = p.sb("rs_half", (128, 32))
    bufs["stamp_sb"] = p.sb("stamp_sb", (1, 1), I32)
    bufs["eye_in_ap"] = eye_in

    with p.es, nc.body("main", valid_engines=ALL_ENG):
        p.dma(sync, bufs["ones_col"][:, :], onescol_in, "S", wait=False)
        p.dma(sync, bufs["onesrow"][:, :], onesrow_in, "S", wait=False)
        p.dma(sync, x_state, x_in, "S", wait=False)
        p.wait(sync, "S")
        p.barrier()

        with ExitStack() as esc:
            for nm, shp, dt in (("parts_sb", (8, 64), F32),
                                ("csel_col", (64, 1), F32),
                                ("ass_i", (1, D), I32),
                                ("ass_f", (1, D), F32),
                                ("iota_sb", (64, 1), F32),
                                ("oh_sb", (64, D), F32),
                                ("qs_row", (1, TOK), F32),
                                ("QS_sb", (128, TOK), F32)):
                bufs[nm] = p.sb(nm, shp, dt, es=esc)
            p.dma(sync, bufs["iota_sb"][:, :], iota_in, "S", wait=False)
            emit_csel(p, bufs, parts_in, qs_in, assign_in, tag="c3")
            emit_maskaccum(p, bufs, dense_in, x_state, tag="m3")

        if STAGE >= 2:
            emit_l1ffn(p, bufs, x_state, w12_in, b12_in, w3_in, b3_in,
                       x4_scr, tag="f4")
        if STAGE >= 3:
            emit_l1attn(p, bufs, x_state, wq_in, wk_in, wv_in, wo_in,
                        cos_in, sin_in, mask_in, stamp_in, shKA, shKB,
                        shVA, shVB, flA, flB, q_scr, myK, myV, att_scr)
        if STAGE >= 4:
            emit_l1ffn(p, bufs, x_state, w12_in, b12_in, w3_in, b3_in,
                       x4_scr, tag="f6")

        p.wait(sync, "E")
        p.wait(sync, "S")
        p.wait(sync, "G")
        p.wait(sync, "V")
        p.wait(sync, "T")
    return p


def _host_prep3(inp, core):
    h = {}
    h["w12p"] = _prep_oc(np.asarray(inp["L1_ffn_w12"], np.float32))
    h["b12c"] = _col_chunks(np.asarray(inp["L1_ffn_b12"], np.float32))
    h["w3p"] = _prep_oc(np.asarray(inp["L1_ffn_w3"], np.float32))
    h["b3c"] = _col_chunks(np.asarray(inp["L1_ffn_b3"], np.float32))
    perm = np.concatenate([hh * 128 + np.r_[np.arange(0, 128, 2),
                                            np.arange(1, 128, 2)]
                           for hh in range(32)])
    wq = np.asarray(inp["L1_attn_wq"], np.float32)[perm]
    wk = np.asarray(inp["L1_attn_wk"], np.float32)[perm]
    h["wqp"] = _prep_oc(wq)
    h["wkp"] = _prep_oc(wk)
    h["wvp"] = _prep_oc(np.asarray(inp["L1_attn_wv"], np.float32))
    h["wop"] = _prep_oc(np.asarray(inp["L1_attn_wo"], np.float32))
    return h


def _host_core3(core):
    import ml_dtypes
    rows = _tok_rows(core)
    t = rows.astype(np.float64)
    cos = np.cos(t).astype(np.float32)
    sin = np.sin(t).astype(np.float32)
    cosb = np.ascontiguousarray(np.broadcast_to(cos[None, :], (128, TOK)))
    sinb = np.ascontiguousarray(np.broadcast_to(sin[None, :], (128, TOK)))
    parity = core % 2
    tril = np.tril(np.ones((128, 128), np.float32))
    m_none = np.zeros((128, 128), np.float32)
    m_tril = np.where(tril > 0, 0.0, -1e9).astype(np.float32)
    m_full = np.full((128, 128), -1e9, np.float32)
    tiles = np.zeros((144, 128, 128), np.float32)
    for qb in range(16):
        lo = qb if qb < 8 else qb + 8
        gq = _gq(parity, qb)
        for i in range(9):
            gcur = lo + i
            if gcur < gq:
                tiles[9 * qb + i] = m_none
            elif gcur == gq:
                tiles[9 * qb + i] = m_tril
            else:
                tiles[9 * qb + i] = m_full
    maskb = tiles.astype(ml_dtypes.bfloat16)
    return cosb, sinb, maskb



_HOST = {}


def _host_prep(inp):
    h = {}
    h["w0f"] = _prep_oc(np.asarray(inp["L0_ffn_weight"], np.float32))
    h["b0f"] = _col_chunks(np.asarray(inp["L0_ffn_bias"], np.float32))
    cent = np.asarray(inp["L0_ffn_centroids"], np.float32)
    h["centT"] = np.ascontiguousarray(cent.T.reshape(NK, 128, 64))
    h["assign"] = np.asarray(inp["L0_ffn_assignments"], np.int32).reshape(1, D)
    h["eye"] = np.eye(128, dtype=np.float32)
    h["onesrow"] = np.ones((1, 128), np.float32)
    h["onescol"] = np.ones((128, 1), np.float32)
    h["iota64"] = np.arange(64, dtype=np.float32).reshape(64, 1)
    cw = np.asarray(inp["L0_attn_conv_w"], np.float32)
    h["cw"] = np.stack([_prep_co(cw[t]) for t in range(3)])
    h["cb"] = _col_chunks(np.asarray(inp["L0_attn_conv_b"], np.float32))
    h["wq0"] = _prep_co(np.asarray(inp["L0_attn_wq"], np.float32).reshape(1024, 1024))
    h["wk0"] = _prep_co(np.asarray(inp["L0_attn_wk"], np.float32).reshape(1024, 1024))
    h["wv0r"] = np.ascontiguousarray(
        np.asarray(inp["L0_attn_wv"], np.float32).reshape(8, 128, 1024))
    h["wo0"] = _prep_co(np.asarray(inp["L0_attn_wo"], np.float32).reshape(1024, 1024))
    h["proj0"] = _prep_co(np.asarray(inp["L0_attn_proj_w"], np.float32))
    h["bq0"] = _col_chunks(np.asarray(inp["L0_attn_bq"], np.float32).reshape(-1))
    h["bk0"] = _col_chunks(np.asarray(inp["L0_attn_bk"], np.float32).reshape(-1))
    h["bv0b"] = np.ascontiguousarray(np.broadcast_to(
        np.asarray(inp["L0_attn_bv"], np.float32).reshape(1, 1024), (128, 1024)))
    h["bo0"] = _col_chunks(np.asarray(inp["L0_attn_bo"], np.float32))
    h["pb0"] = _col_chunks(np.asarray(inp["L0_attn_proj_b"], np.float32))
    tril = np.tril(np.ones((128, 128), bool))
    h["trilm"] = np.where(tril, 0.0, -1e9).astype(np.float32)
    return h


def kernel(**inp):
    x_full = np.asarray(inp["L0_attn_inputs"], np.float32)
    core_rows = [_tok_rows(c) for c in range(NCORES)]
    xT = [np.ascontiguousarray(x_full[c // 2][core_rows[c]].T)
          for c in range(NCORES)]
    h = _host_prep(inp)
    h3 = _host_prep3(inp, 0)
    percore = [_host_core3(c) for c in range(NCORES)]

    p1 = build_L1()
    in_maps = [{"xT": xT[c], "w0f": h["w0f"], "b0f": h["b0f"],
                "centT": h["centT"], "eye": h["eye"],
                "onesrow": h["onesrow"], "onescol": h["onescol"]}
               for c in range(NCORES)]
    r1 = run_bass_kernel_spmd(p1.nc, in_maps, core_ids=list(range(NCORES)))
    l1 = r1.results
    parts1 = np.concatenate([l1[c]["partials"] for c in range(NCORES)],
                            axis=0)

    p2 = build_L2()
    in_maps = [{"xT": xT[c], "partials1": parts1, "qs1": l1[c]["qs"],
                "dense1": l1[c]["dense"], "assign": h["assign"],
                "iota64": h["iota64"], "eye": h["eye"],
                "onesrow": h["onesrow"], "onescol": h["onescol"],
                "cw": h["cw"], "cb": h["cb"], "wq0": h["wq0"],
                "wk0": h["wk0"], "wv0r": h["wv0r"], "wo0": h["wo0"],
                "proj0": h["proj0"], "bq0": h["bq0"], "bk0": h["bk0"],
                "bv0b": h["bv0b"], "bo0": h["bo0"], "pb0": h["pb0"],
                "trilm": h["trilm"], "w0f": h["w0f"], "b0f": h["b0f"],
                "centT": h["centT"], "stamp": np.array([[MAGIC]], np.int32)}
               for c in range(NCORES)]
    r2 = run_bass_kernel_spmd(p2.nc, in_maps, core_ids=list(range(NCORES)))
    l2 = r2.results
    parts3 = np.concatenate([l2[c]["partials3"] for c in range(NCORES)],
                            axis=0)

    p3 = build_L3()
    in_maps = []
    for c in range(NCORES):
        cosb, sinb, maskb = percore[c]
        in_maps.append(
            {"xT3": l2[c]["x2"], "partials3": parts3, "qs3": l2[c]["qs3"],
             "dense3": l2[c]["dense3"], "assign": h["assign"],
             "iota64": h["iota64"], "eye": h["eye"],
             "onesrow": h["onesrow"], "onescol": h["onescol"],
             "w12p": h3["w12p"], "b12c": h3["b12c"], "w3p": h3["w3p"],
             "b3c": h3["b3c"], "wqp": h3["wqp"], "wkp": h3["wkp"],
             "wvp": h3["wvp"], "wop": h3["wop"], "cosb": cosb,
             "sinb": sinb, "maskb": maskb,
             "stamp3": np.array([[MAGIC3]], np.int32)})
    r3 = run_bass_kernel_spmd(p3.nc, in_maps, core_ids=list(range(NCORES)))

    out = np.empty((4, T, D), np.float32)
    for c in range(NCORES):
        out[c // 2][core_rows[c]] = r3.results[c]["xout"].T
    return out


# version 06
# speedup vs baseline: 1.4875x; 1.0851x over previous
import numpy as np
from contextlib import ExitStack

import concourse.bass as bass
import concourse.mybir as mybir
from concourse.ap import AP as APc
from concourse.bass_utils import run_bass_kernel_spmd

F32 = mybir.dt.float32
BF16 = mybir.dt.bfloat16
I32 = mybir.dt.int32
ALU = mybir.AluOpType
AF = mybir.ActivationFunctionType
AX = mybir.AxisListType
ET = mybir.EngineType
ALL_ENG = [ET.PE, ET.DVE, ET.Activation, ET.Pool, ET.SP]

D = 4096
TOK = 2048
NK = D // 128
NTT = TOK // 512
NCORES = 8
T = 4096
EPS = 1e-5
THR = 0.01
INVTEMP = 10.0


def _tok_rows(core):
    if core % 2 == 0:
        return np.r_[0:1024, 3072:4096]
    return np.r_[1024:3072]


def _prep_oc(w):
    no, nk = w.shape[0] // 128, w.shape[1] // 128
    return np.ascontiguousarray(
        w.reshape(no, 128, nk, 128).transpose(0, 3, 2, 1).astype(np.float32))


def _prep_co(w):
    nk, no = w.shape[0] // 128, w.shape[1] // 128
    return np.ascontiguousarray(
        w.reshape(nk, 128, no, 128).transpose(2, 1, 0, 3).astype(np.float32))


def _col_chunks(b):
    return np.ascontiguousarray(b.reshape(-1, 128).T.astype(np.float32))



class Prog:

    def __init__(self, name):
        self.nc = bass.Bass("TRN2", target_bir_lowering=False, debug=False,
                            name=name)
        self.es = ExitStack()
        self.sems = {}
        self.cnt = {}
        self.inputs = {}
        self.outputs = {}
        self._names = {}

    def sem(self, name):
        if name not in self.sems:
            h = self.es.enter_context(self.nc.semaphore(name))
            self.sems[name] = h
            self.cnt[name] = 0
        return self.sems[name]

    def inc(self, instr, name, n=1):
        self.sem(name)
        self.cnt[name] += n
        instr.then_inc(self.sems[name], n)
        return self.cnt[name]

    def wait(self, eng, name, val=None):
        self.sem(name)
        eng.wait_ge(self.sems[name], self.cnt[name] if val is None else val)

    def dma(self, eng, out, in_, sname, wait=True, **kw):
        self.inc(eng.dma_start(out=out, in_=in_, **kw), sname, 16)
        if wait:
            self.wait(eng, sname)

    def barrier(self):
        self.nc.all_engine_barrier()

    def din(self, name, shape, dtype=F32):
        t = self.nc.dram_tensor(name, list(shape), dtype, kind="ExternalInput")
        self.inputs[name] = [int(s) for s in shape]
        return t.ap()

    def dout(self, name, shape, dtype=F32):
        t = self.nc.dram_tensor(name, list(shape), dtype, kind="ExternalOutput")
        self.outputs[name] = [int(s) for s in shape]
        return t.ap()

    def dscr(self, name, shape, dtype=F32, shared=False):
        t = self.nc.dram_tensor(name, list(shape), dtype, kind="Internal",
                                addr_space="Shared" if shared else "Local")
        return t.ap()

    def sb(self, name, shape, dtype=F32, es=None):
        n = self._names.get(name, 0)
        self._names[name] = n + 1
        if n:
            name = f"{name}_{n}"
        return (es or self.es).enter_context(
            self.nc.sbuf_tensor(name, list(shape), dtype))



def emit_ln_stats(p, bufs, x_hbm, tag="", n_tok=TOK):
    nc = p.nc
    sync, scalar, tensor, vector = nc.sync, nc.scalar, nc.tensor, nc.vector
    nts = n_tok // 512
    xf32 = bufs["xf32"]
    sq = bufs["sq"]
    ones_col = bufs["ones_col"]
    onesrow = bufs["onesrow"]
    rsb, bb = bufs["rsb"], bufs["bb"]
    srow, qrow, trow = bufs["srow"], bufs["qrow"], bufs["trow"]

    with nc.psum_tensor(f"ps_s{tag}", [1, n_tok], F32) as ps_s, \
         nc.psum_tensor(f"ps_q{tag}", [1, n_tok], F32) as ps_q:
        for k in range(NK):
            buf = xf32[k % 2]
            p.wait(sync, "T")
            p.dma(sync, buf[:, :], x_hbm[k * 128:(k + 1) * 128, :], "S",
                  wait=False)
            p.wait(scalar, "S")
            p.wait(scalar, "T")
            p.inc(scalar.activation(out=sq[:, :], in_=buf[:, :],
                                    func=AF.Square), "A", 1)
            p.wait(tensor, "A")
            for tt in range(nts):
                sl = slice(tt * 512, (tt + 1) * 512)
                tensor.matmul(ps_s[0:1, sl], ones_col[:, 0:1], buf[:, sl],
                              start=(k == 0), stop=(k == NK - 1))
                mm = tensor.matmul(ps_q[0:1, sl], ones_col[:, 0:1], sq[:, sl],
                                   start=(k == 0), stop=(k == NK - 1))
            p.inc(mm, "T", 1)

        p.wait(vector, "T")
        vector.tensor_scalar_mul(srow[:, :], ps_s[0:1, :], 1.0 / D)
        p.inc(vector.tensor_scalar_mul(qrow[:, :], ps_q[0:1, :], 1.0 / D),
              "V", 1)
        p.wait(vector, "V")
        p.inc(vector.tensor_tensor(trow[:, :], srow[:, :], srow[:, :],
                                   ALU.mult), "V", 1)
        p.wait(vector, "V")
        p.inc(vector.tensor_sub(qrow[:, :], qrow[:, :], trow[:, :]), "V", 1)
        p.wait(vector, "V")
        p.inc(vector.tensor_scalar_add(qrow[:, :], qrow[:, :], float(EPS)),
              "V", 1)
        p.wait(scalar, "V")
        p.inc(scalar.activation(out=trow[:, :], in_=qrow[:, :], func=AF.Ln),
              "A", 1)
        p.wait(scalar, "A")
        p.inc(scalar.activation(out=qrow[:, :], in_=trow[:, :], func=AF.Exp,
                                scale=-0.5), "A", 1)
        p.wait(vector, "A")
        p.inc(vector.tensor_tensor(srow[:, :], srow[:, :], qrow[:, :],
                                   ALU.mult), "V", 1)
        p.wait(vector, "V")
        p.inc(vector.tensor_scalar_mul(srow[:, :], srow[:, :], -1.0), "V", 1)

    with nc.psum_tensor(f"ps_bA{tag}", [128, n_tok], F32) as ps_bA, \
         nc.psum_tensor(f"ps_bB{tag}", [128, n_tok], F32) as ps_bB:
        p.wait(tensor, "V")
        for tt in range(nts):
            sl = slice(tt * 512, (tt + 1) * 512)
            tensor.matmul(ps_bA[:, sl], onesrow[0:1, :], qrow[0:1, sl],
                          start=True, stop=True)
            mm = tensor.matmul(ps_bB[:, sl], onesrow[0:1, :], srow[0:1, sl],
                               start=True, stop=True)
        p.inc(mm, "T", 1)
        p.wait(vector, "T")
        vector.tensor_copy(rsb[:, :], ps_bA[:, :])
        p.inc(vector.tensor_copy(bb[:, :], ps_bB[:, :]), "V", 1)

    p.barrier()


def emit_ln_norm(p, bufs, x_hbm, xln_sb, consume_f32=None, n_tok=TOK):
    nc = p.nc
    sync, vector = nc.sync, nc.vector
    xf32, sq = bufs["xf32"], bufs["sq"]
    rsb, bb = bufs["rsb"], bufs["bb"]
    for k in range(NK):
        buf = xf32[k % 2]
        p.wait(sync, "V")
        p.dma(sync, buf[:, :], x_hbm[k * 128:(k + 1) * 128, :], "S",
              wait=False)
        p.wait(vector, "S")
        p.wait(vector, "T")
        vector.tensor_tensor(sq[:, :], buf[:, :], rsb[:, :], ALU.mult)
        p.inc(vector.tensor_tensor(xln_sb[:, k * n_tok:(k + 1) * n_tok],
                                   sq[:, :], bb[:, :], ALU.add), "V", 1)
        if consume_f32 is not None:
            p.wait(vector, "V")
            p.inc(vector.tensor_tensor(sq[:, :], sq[:, :], bb[:, :],
                                       ALU.add), "V", 1)
            consume_f32(k)
    p.barrier()


def emit_dots(p, bufs, ps_dots, k):
    nc = p.nc
    tensor = nc.tensor
    sq = bufs["sq"]
    centT_sb = bufs["centT_sb"]
    p.wait(tensor, "V")
    for j in range(16):
        mm = tensor.matmul(
            ps_dots[:, j * 64:(j + 1) * 64],
            sq[:, j * 128:(j + 1) * 128],
            centT_sb[:, k * 64:(k + 1) * 64],
            start=(k == 0 and j in (0, 8)), stop=(k == NK - 1))
    p.inc(mm, "T", 1)


def emit_gate(p, bufs, ps_dots, partials_out, qs_out, tag=""):
    nc = p.nc
    sync, scalar, tensor, vector = nc.sync, nc.scalar, nc.tensor, nc.vector
    e, hits = bufs["e64"], bufs["hits"]
    mx, rowsum = bufs["mx"], bufs["rowsum"]
    qsc = bufs["qsc"]
    ones_col, eye = bufs["ones_col"], bufs["eye_f32"]
    cnt_sb, qsr_sb = bufs["cnt_sb"], bufs["qsr"]

    with nc.psum_tensor(f"ps_cnt{tag}", [1, 64], F32) as ps_cnt:
        for j in range(16):
            reg = ps_dots[:, j * 64:(j + 1) * 64]
            p.wait(vector, "T")
            p.wait(vector, "A")
            p.inc(vector.reduce_max(mx[:, :], reg, axis=AX.X, negate=True),
                  "V", 1)
            p.wait(vector, "V")
            p.inc(vector.tensor_scalar_mul(mx[:, :], mx[:, :], INVTEMP),
                  "V", 1)
            p.wait(scalar, "V")
            p.inc(scalar.activation(out=e[:, :], in_=reg, func=AF.Exp,
                                    bias=mx[:, 0:1], scale=INVTEMP,
                                    accum_out=rowsum[:, :]), "A", 1)
            p.wait(vector, "A")
            p.inc(vector.tensor_scalar_mul(rowsum[:, :], rowsum[:, :], THR),
                  "V", 1)
            p.wait(vector, "V")
            p.inc(vector.tensor_scalar(hits[:, :], e[:, :], rowsum[:, 0:1],
                                       None, ALU.is_gt), "V", 1)
            p.wait(vector, "V")
            p.inc(vector.reduce_max(qsc[:, j:j + 1], hits[:, :], axis=AX.X),
                  "V", 1)
            p.wait(tensor, "V")
            p.inc(tensor.matmul(ps_cnt[0:1, :], ones_col[:, 0:1], hits[:, :],
                                start=(j == 0), stop=(j == 15)), "T", 1)
        p.wait(vector, "T")
        p.inc(vector.tensor_copy(cnt_sb[:, :], ps_cnt[0:1, :]), "V", 1)
        p.wait(sync, "V")
        p.dma(sync, partials_out, cnt_sb[:, :], "S")

    with nc.psum_tensor(f"ps_qt{tag}", [128, 128], F32) as ps_qt:
        p.wait(tensor, "V")
        p.inc(tensor.transpose(ps_qt[0:16, :], qsc[:, :], eye[:, :]), "T", 1)
        p.wait(vector, "T")
        p.inc(vector.tensor_copy(qsr_sb[:, :], ps_qt[0:16, :]), "V", 1)
        p.wait(sync, "V")
        p.dma(sync, qs_out.rearrange("a (b c) -> (a b) c", b=16),
              qsr_sb[:, :], "S")
    p.barrier()


def emit_linear(p, bufs, xsrc_sb, strips_hbm, n_oc, out_cb, n_tok=TOK,
                nk=NK, tag=""):
    nc = p.nc
    g, tensor = nc.gpsimd, nc.tensor
    nts = n_tok // 512
    sbufs = bufs["strips"]
    ev = "EV" + tag
    p.sem(ev)
    with nc.psum_tensor(f"lin_p0{tag}", [128, 512], F32) as p0, \
         nc.psum_tensor(f"lin_p1{tag}", [128, 512], F32) as p1:
        pss = [p0, p1]
        p.dma(g, sbufs[0][:, :], strips_hbm[0], "G", wait=False)
        gtarget = [p.cnt["G"]]
        tload = [None, None]
        step = 0
        for oc in range(n_oc):
            if oc + 1 < n_oc:
                if tload[(oc + 1) % 2] is not None:
                    g.wait_ge(p.sems["T"], tload[(oc + 1) % 2])
                p.dma(g, sbufs[(oc + 1) % 2][:, :], strips_hbm[oc + 1], "G",
                      wait=False)
                gtarget.append(p.cnt["G"])
            sbw = sbufs[oc % 2]
            tensor.wait_ge(p.sems["G"], gtarget[oc])
            for tt in range(nts):
                psu = pss[step % 2]
                if step >= 2:
                    tensor.wait_ge(p.sems[ev], step - 1)
                for k in range(nk):
                    mm = tensor.matmul(
                        psu[:, :],
                        sbw[:, k * 128:(k + 1) * 128],
                        xsrc_sb[:, k * n_tok + tt * 512:
                                k * n_tok + tt * 512 + 512],
                        start=(k == 0), stop=(k == nk - 1))
                p.inc(mm, "T", 1)
                out_cb(oc, tt, psu, step)
                step += 1
            tload[oc % 2] = p.cnt["T"]
    p.barrier()



def build_L1():
    p = Prog("mk_L1")
    nc = p.nc
    sync, scalar, tensor, vector, g = (nc.sync, nc.scalar, nc.tensor,
                                       nc.vector, nc.gpsimd)

    x_hbm = p.din("xT", (D, TOK))
    w0f = p.din("w0f", (32, 128, NK * 128))
    b0f_in = p.din("b0f", (128, 32))
    centT_in = p.din("centT", (NK, 128, 64))
    eye_in = p.din("eye", (128, 128))
    onesrow_in = p.din("onesrow", (1, 128))
    onescol_in = p.din("onescol", (128, 1))

    partials_out = p.dout("partials", (1, 64))
    qs_out = p.dout("qs", (1, TOK))
    dense_out = p.dout("dense", (D, TOK))

    bufs = {}
    bufs["xln"] = p.sb("xln", (128, NK * TOK), BF16)
    bufs["xf32"] = [p.sb("xf32a", (128, TOK)), p.sb("xf32b", (128, TOK))]
    bufs["sq"] = p.sb("sq", (128, TOK))
    bufs["rsb"] = p.sb("rsb", (128, TOK))
    bufs["bb"] = p.sb("bb", (128, TOK))
    bufs["ones_col"] = p.sb("ones_col", (128, 1))
    bufs["onesrow"] = p.sb("onesrow_sb", (1, 128))
    bufs["eye_f32"] = p.sb("eye_sb", (128, 128))
    bufs["b0f_sb"] = p.sb("b0f_sb", (128, 32))

    with p.es, nc.body("main", valid_engines=ALL_ENG):
        p.dma(sync, bufs["ones_col"][:, :], onescol_in, "S", wait=False)
        p.dma(sync, bufs["onesrow"][:, :], onesrow_in, "S", wait=False)
        p.dma(sync, bufs["eye_f32"][:, :], eye_in, "S", wait=False)
        p.dma(sync, bufs["b0f_sb"][:, :], b0f_in, "S", wait=False)

        with ExitStack() as es2:
            for nm, shp, dt in (("srow", (1, TOK), F32), ("qrow", (1, TOK), F32),
                                ("trow", (1, TOK), F32),
                                ("centT_sb", (128, NK * 64), F32),
                                ("e64", (128, 64), F32), ("hits", (128, 64), F32),
                                ("mx", (128, 1), F32), ("rowsum", (128, 1), F32),
                                ("qsc", (128, 16), F32), ("cnt_sb", (1, 64), F32),
                                ("qsr", (16, 128), F32)):
                bufs[nm] = p.sb(nm, shp, dt, es=es2)
            p.dma(sync,
                  bufs["centT_sb"][:, :].rearrange("p (k j) -> p k j", k=NK),
                  centT_in.rearrange("k p j -> p k j"), "S", wait=False)
            p.wait(sync, "S")
            p.barrier()

            emit_ln_stats(p, bufs, x_hbm, tag="1")
            with nc.psum_tensor("ps_dots", [128, 1024], F32) as ps_dots:
                emit_ln_norm(p, bufs, x_hbm, bufs["xln"],
                             consume_f32=lambda k: emit_dots(p, bufs, ps_dots, k))
                emit_gate(p, bufs, ps_dots, partials_out, qs_out, tag="1")

        with ExitStack() as es3:
            bufs["strips"] = [p.sb("stripA", (128, NK * 128), BF16, es=es3),
                              p.sb("stripB", (128, NK * 128), BF16, es=es3)]
            bufs["dense_ev"] = [p.sb("dense_ev0", (128, 512), F32, es=es3),
                                p.sb("dense_ev1", (128, 512), F32, es=es3)]
            estore = {}

            def dense_cb(oc, tt, psu, step):
                evb = bufs["dense_ev"][step % 2]
                p.wait(vector, "T")
                if step >= 2:
                    p.wait(vector, "E", estore[step - 2])
                v = vector.tensor_scalar(evb[:, :], psu[:, :],
                                         bufs["b0f_sb"][:, oc:oc + 1], None,
                                         ALU.add)
                p.inc(v, "EVd1", 1)
                p.wait(scalar, "EVd1")
                p.inc(scalar.dma_start(
                    out=dense_out[oc * 128:(oc + 1) * 128,
                                  tt * 512:(tt + 1) * 512], in_=evb[:, :]),
                    "E", 16)
                estore[step] = p.cnt["E"]

            p.sem("EVd1")
            emit_linear(p, bufs, bufs["xln"], w0f, 32, dense_cb, tag="d1")

            p.wait(sync, "E")
            p.wait(sync, "S")
    return p




PADW = 2052
SEG = [1, 1027]
MAGIC = 0x5A17


def _sap(handle, offset, dims):
    free = int(np.prod(handle.shape[1:]))
    return APc(handle, offset, [[free, int(handle.shape[0])]] + dims)


def emit_csel(p, bufs, partials_in, qs_in, assign_in, tag=""):
    nc = p.nc
    sync, scalar, tensor, vector = nc.sync, nc.scalar, nc.tensor, nc.vector
    parts = bufs["parts_sb"]
    csel = bufs["csel_col"]
    ass_i = bufs["ass_i"]
    ass_f = bufs["ass_f"]
    iota = bufs["iota_sb"]
    oh = bufs["oh_sb"]
    rs_half = bufs["rs_half"]
    qs_row = bufs["qs_row"]
    qs_b = bufs["QS_sb"]
    onesrow = bufs["onesrow"]
    ones_col = bufs["ones_col"]

    p.dma(sync, parts[:, :], partials_in, "S", wait=False)
    p.dma(sync, ass_i[:, :], assign_in, "S", wait=False)
    p.dma(sync, qs_row[:, :], qs_in, "S", wait=False)
    p.wait(sync, "S")
    p.barrier()
    with nc.psum_tensor(f"ps_cs{tag}", [64, 1], F32) as ps_cs, \
         nc.psum_tensor(f"ps_oh{tag}", [64, 512], F32) as ps_oh, \
         nc.psum_tensor(f"ps_rs{tag}", [128, 32], F32) as ps_rs, \
         nc.psum_tensor(f"ps_qb{tag}", [128, TOK], F32) as ps_qb:
        p.inc(tensor.matmul(ps_cs[:, :], parts[:, :], ones_col[0:8, 0:1],
                            start=True, stop=True), "T", 1)
        p.wait(vector, "T")
        p.inc(vector.tensor_scalar(csel[:, :], ps_cs[:, :], 0.0, None,
                                   ALU.is_gt), "V", 1)
        p.inc(vector.tensor_copy(ass_f[:, :], ass_i[:, :]), "V", 1)
        for gg in range(D // 512):
            sl = slice(gg * 512, (gg + 1) * 512)
            p.wait(tensor, "V")
            p.inc(tensor.matmul(ps_oh[:, :], onesrow[0:1, 0:64],
                                ass_f[0:1, sl], start=True, stop=True),
                  "T", 1)
            p.wait(vector, "T")
            p.inc(vector.tensor_scalar(oh[:, sl], ps_oh[:, :],
                                       iota[:, 0:1], None, ALU.is_equal),
                  "V", 1)
        p.wait(tensor, "V")
        for oc in range(32):
            mm = tensor.matmul(ps_rs[:, oc:oc + 1],
                               oh[:, oc * 128:(oc + 1) * 128],
                               csel[:, 0:1], start=(oc == 0), stop=(oc == 31))
        p.inc(mm, "T", 1)
        p.wait(vector, "T")
        p.inc(vector.tensor_scalar_mul(rs_half[:, :], ps_rs[:, :], 0.5),
              "V", 1)
        p.wait(tensor, "V")
        for tt in range(NTT):
            sl = slice(tt * 512, (tt + 1) * 512)
            mm = tensor.matmul(ps_qb[:, sl], onesrow[0:1, :],
                               qs_row[0:1, sl], start=True, stop=True)
        p.inc(mm, "T", 1)
        p.wait(vector, "T")
        p.inc(vector.tensor_copy(qs_b[:, :], ps_qb[:, :]), "V", 1)
    p.barrier()


def emit_maskaccum(p, bufs, dense_in, x_state, tag=""):
    nc = p.nc
    sync, vector, g = nc.sync, nc.vector, nc.gpsimd
    rs_half, qs_b = bufs["rs_half"], bufs["QS_sb"]
    dbuf = bufs["xf32"]
    tbuf = bufs["sq"]
    for oc in range(NK):
        buf = dbuf[oc % 2]
        p.wait(sync, "G")
        p.dma(sync, buf[:, :], dense_in[oc * 128:(oc + 1) * 128, :], "S",
              wait=False)
        p.wait(vector, "S")
        p.wait(vector, "G")
        vector.tensor_scalar(tbuf[:, :], buf[:, :],
                             rs_half[:, oc:oc + 1], None, ALU.mult)
        p.inc(vector.tensor_tensor(tbuf[:, :], tbuf[:, :], qs_b[:, :],
                                   ALU.mult), "V", 1)
        p.wait(g, "V")
        p.dma(g, x_state[oc * 128:(oc + 1) * 128, :], tbuf[:, :], "G",
              wait=False, accum_op=ALU.add)
    p.wait(g, "G")
    p.barrier()


def emit_ln_norm_padded(p, bufs, x_hbm, xlnp, n_tok=TOK):
    nc = p.nc
    sync, vector = nc.sync, nc.vector
    xf32, sq = bufs["xf32"], bufs["sq"]
    rsb, bb = bufs["rsb"], bufs["bb"]
    for k in range(NK):
        buf = xf32[k % 2]
        p.wait(sync, "V")
        p.dma(sync, buf[:, :], x_hbm[k * 128:(k + 1) * 128, :], "S",
              wait=False)
        p.wait(vector, "S")
        vector.tensor_tensor(sq[:, :], buf[:, :], rsb[:, :], ALU.mult)
        out_ap = _sap(xlnp, k * PADW + 1, [[1026, 2], [1, 1024]])
        in_ap = _sap(sq, 0, [[1024, 2], [1, 1024]])
        in2_ap = _sap(bb, 0, [[1024, 2], [1, 1024]])
        p.inc(vector.tensor_tensor(out_ap, in_ap, in2_ap, ALU.add), "V", 1)
    p.barrier()


def emit_halo(p, bufs, xlnp, shA, shB, flA, flB, stamp_in):
    nc = p.nc
    g = nc.gpsimd
    zcol = bufs["zcol_bf"]
    stamp_sb = bufs["stamp_sb"]
    p.inc(g.memset(zcol[:, :], 0.0), "V", 1)
    p.dma(g, stamp_sb[:, :], stamp_in, "G")
    p.wait(g, "V")

    loc = [0, 1023, 1024, 2047]
    lcol = [1, 1024, 1027, 2050]
    me = g.partition_id()
    parity = g.to_reg(me % 2)

    def writes(sh, fl):
        for i, c in enumerate(loc):
            for k in range(NK):
                src = _sap(xlnp, k * PADW + lcol[i], [[1, 1]])
                dst = APc(sh.tensor, i * D + k * 128, [[1, 128], [1, 1]])
                p.cnt["G"] += 16
                g.dma_start(out=dst, in_=src).then_inc(p.sems["G"], 16)
        for k in range(NK):
            dst = APc(sh.tensor, 4 * D + k * 128, [[1, 128], [1, 1]])
            p.cnt["G"] += 16
            g.dma_start(out=dst, in_=zcol[:, 0:1]).then_inc(p.sems["G"], 16)
        g.wait_ge(p.sems["G"], p.cnt["G"])
        p.cnt["G"] += 16
        g.dma_start(out=APc(fl.tensor, 0, [[1, 1], [1, 1]]),
                    in_=stamp_sb[0:1, 0:1]).then_inc(p.sems["G"], 16)
        g.wait_ge(p.sems["G"], p.cnt["G"])

    def poll(fl):
        preg = nc.alloc_register(ET.Pool, p.nc.make_name("poll", True))
        creg = nc.alloc_register(ET.Pool, p.nc.make_name("pcond", True))
        g.reg_mov(creg, 1)
        with g.While(creg):
            g.reg_load(preg, APc(fl.tensor, 0, [[1, 1], [1, 1]]))
            g.reg_alu(creg, preg, MAGIC, ALU.not_equal)

    def reads(srcs):
        halocol = [0, 1025, 1026, 2051]
        for i, (sh, slot) in enumerate(srcs):
            for k in range(NK):
                src = APc(sh.tensor, slot * D + k * 128, [[1, 128], [1, 1]])
                dst = _sap(xlnp, k * PADW + halocol[i], [[1, 1]])
                p.cnt["G"] += 16
                g.dma_start(out=dst, in_=src).then_inc(p.sems["G"], 16)
        g.wait_ge(p.sems["G"], p.cnt["G"])

    gc0 = p.cnt["G"]
    with g.If_eq(parity, 0):
        writes(shA, flA)
        poll(flB)
        reads([(shA, 4), (shB, 0), (shB, 3), (shA, 4)])
    gc_after = p.cnt["G"]
    with g.Else():
        p.cnt["G"] = gc0
        writes(shB, flB)
        poll(flA)
        reads([(shA, 1), (shB, 2), (shB, 1), (shA, 2)])
        assert p.cnt["G"] == gc_after, "halo branch sem counts diverge"
    p.cnt["G"] = gc_after
    p.barrier()


def emit_conv(p, bufs, xlnp, cw_in, qkv_scr):
    nc = p.nc
    g, tensor, vector, scalar = nc.gpsimd, nc.tensor, nc.vector, nc.scalar
    strips = bufs["cstrips"]
    cb_sb = bufs["cb_sb"]
    ev = bufs["conv_ev"]
    base = [1, 513, 1027, 1539]
    n_oc = 24
    p.sem("EVcv")
    estore = {}
    with nc.psum_tensor("cv_p0", [128, 512], F32) as p0, \
         nc.psum_tensor("cv_p1", [128, 512], F32) as p1:
        pss = [p0, p1]
        step = 0
        for t in range(3):
            p.dma(g, strips[0][t][:, :], cw_in[t, 0], "G", wait=False)
        gtargetC = [p.cnt["G"]]
        tstoreC = {}
        for oc in range(n_oc):
            if oc + 1 < n_oc:
                if oc >= 1:
                    g.wait_ge(p.sems["T"], tstoreC[oc - 1])
                for t in range(3):
                    p.dma(g, strips[(oc + 1) % 2][t][:, :], cw_in[t, oc + 1],
                          "G", wait=False)
                gtargetC.append(p.cnt["G"])
            sb3 = strips[oc % 2]
            tensor.wait_ge(p.sems["G"], gtargetC[oc])
            for tt in range(NTT):
                psu = pss[step % 2]
                if step >= 2:
                    tensor.wait_ge(p.sems["EVcv"], step - 1)
                first = True
                for t in range(3):
                    for k in range(NK):
                        mm = tensor.matmul(
                            psu[:, :], sb3[t][:, k * 128:(k + 1) * 128],
                            _sap(xlnp, k * PADW + base[tt] + t - 1,
                                 [[1, 512]]),
                            start=first, stop=(t == 2 and k == NK - 1))
                        first = False
                p.inc(mm, "T", 1)
                evb = ev[step % 2]
                p.wait(vector, "T")
                if step >= 2:
                    p.wait(vector, "E", estore[step - 2])
                v = vector.tensor_scalar(evb[:, :], psu[:, :],
                                         cb_sb[:, oc:oc + 1], None, ALU.add)
                p.inc(v, "EVcv", 1)
                p.wait(scalar, "EVcv")
                p.inc(scalar.dma_start(
                    out=qkv_scr[oc * 128:(oc + 1) * 128,
                                tt * 512:(tt + 1) * 512],
                    in_=evb[:, :]), "E", 16)
                estore[step] = p.cnt["E"]
                step += 1
            tstoreC[oc] = p.cnt["T"]
    p.wait(scalar, "E")
    p.barrier()


def emit_l0attn_blocks(p, bufs, qkv_scr, x_state):
    nc = p.nc
    sync, scalar, tensor, vector, g = (nc.sync, nc.scalar, nc.tensor,
                                       nc.vector, nc.gpsimd)
    wq_sb, wk_sb = bufs["wq_sb"], bufs["wk_sb"]
    wv_sb = bufs["wv_sb"]
    wo_sb = bufs["wo_sb"]
    proj_sb = bufs["proj_sb"]
    bq_sb, bk_sb = bufs["bq_sb"], bufs["bk_sb"]
    bvb_sb = bufs["bvb_sb"]
    bo_sb, pb_sb = bufs["bo_sb"], bufs["pb_sb"]
    tril_sb = bufs["tril_sb"]
    eyeb = bufs["eye_bf"]
    qsb = bufs["q_sb"]; ksb = bufs["k_sb"]; vsb = bufs["v_sb"]
    qh_sb, kh_sb = bufs["qh_sb"], bufs["kh_sb"]
    vh_sb = bufs["vh_sb"]
    sm_sb = bufs["sm_sb"]
    e_sb = bufs["e_sb"]
    et_sb = bufs["et_sb"]
    o_sb = bufs["o_sb"]
    oT_sb = bufs["oT_sb"]
    o2_sb = bufs["o2_sb"]
    mx1, rs1 = bufs["mx"], bufs["rowsum"]
    yev = bufs["yev"]
    SC = 0.125

    def load_block(blk, par):
        for ch in range(8):
            for dstbuf, row0 in ((qsb[par], 0), (ksb[par], 1024),
                                 (vsb[par], 2048)):
                p.dma(sync, dstbuf[:, ch * 128:(ch + 1) * 128],
                      qkv_scr[row0 + ch * 128:row0 + (ch + 1) * 128,
                              blk * 128:(blk + 1) * 128], "S", wait=False)
        return p.cnt["S"]

    sload = {0: load_block(0, 0)}
    tproj = {}
    for blk in range(16):
        par = blk % 2
        if blk + 1 < 16:
            if blk >= 1:
                sync.wait_ge(p.sems["T"], tproj[blk - 1])
            sload[blk + 1] = load_block(blk + 1, (blk + 1) % 2)
        tensor.wait_ge(p.sems["S"], sload[blk])
        with nc.psum_tensor(f"qh_ps0{blk}", [128, 128], F32) as qh0, \
             nc.psum_tensor(f"qh_ps1{blk}", [128, 128], F32) as qh1, \
             nc.psum_tensor(f"kh_ps0{blk}", [128, 128], F32) as kh0, \
             nc.psum_tensor(f"kh_ps1{blk}", [128, 128], F32) as kh1, \
             nc.psum_tensor(f"vh_ps{blk}", [128, 1024], F32) as vh_ps:
            vsthp = {}
            for hp in range(8):
                qh_ps = qh0 if hp % 2 == 0 else qh1
                kh_ps = kh0 if hp % 2 == 0 else kh1
                if hp >= 2:
                    tensor.wait_ge(p.sems["V"], vsthp[hp - 2])
                for c in range(8):
                    tensor.matmul(qh_ps[:, :],
                                  wq_sb[:, (hp * 8 + c) * 128:
                                        (hp * 8 + c) * 128 + 128],
                                  qsb[par][:, c * 128:(c + 1) * 128],
                                  start=(c == 0), stop=(c == 7))
                    mm = tensor.matmul(kh_ps[:, :],
                                       wk_sb[:, (hp * 8 + c) * 128:
                                             (hp * 8 + c) * 128 + 128],
                                       ksb[par][:, c * 128:(c + 1) * 128],
                                       start=(c == 0), stop=(c == 7))
                p.inc(mm, "T", 1)
                vector.wait_ge(p.sems["T"], p.cnt["T"])
                vector.tensor_scalar(qh_sb[:, hp * 128:(hp + 1) * 128],
                                     qh_ps[:, :], bq_sb[:, hp:hp + 1], SC,
                                     ALU.add, ALU.mult)
                p.inc(vector.tensor_scalar(
                    kh_sb[:, hp * 128:(hp + 1) * 128], kh_ps[:, :],
                    bk_sb[:, hp:hp + 1], None, ALU.add), "V", 1)
                vsthp[hp] = p.cnt["V"]
            for c in range(8):
                for nn in range(2):
                    mm = tensor.matmul(
                        vh_ps[:, nn * 512:(nn + 1) * 512],
                        vsb[par][:, c * 128:(c + 1) * 128],
                        wv_sb[:, c * 1024 + nn * 512:
                              c * 1024 + nn * 512 + 512],
                        start=(c == 0), stop=(c == 7))
            p.inc(mm, "T", 1)
            tproj[blk] = p.cnt["T"]
            vector.wait_ge(p.sems["T"], p.cnt["T"])
            p.inc(vector.tensor_tensor(vh_sb[:, :], vh_ps[:, :],
                                       bvb_sb[:, :], ALU.add), "V", 1)
            tensor.wait_ge(p.sems["V"], p.cnt["V"])
        with nc.psum_tensor(f"s_ps0{blk}", [128, 128], F32) as s0, \
             nc.psum_tensor(f"s_ps1{blk}", [128, 128], F32) as s1, \
             nc.psum_tensor(f"et_ps0{blk}", [128, 128], BF16) as et0, \
             nc.psum_tensor(f"et_ps1{blk}", [128, 128], BF16) as et1, \
             nc.psum_tensor(f"pv_ps0{blk}", [128, 64], F32) as pv0, \
             nc.psum_tensor(f"pv_ps1{blk}", [128, 64], F32) as pv1:
            sps = [s0, s1]; ets = [et0, et1]; pvs = [pv0, pv1]
            aexp = {}; vmsk = {}; vpv = {}; tpv = {}; vcp = {}

            def b_s3(j):
                jp = j % 2
                etp = ets[jp]
                tensor.wait_ge(p.sems["A"], aexp[j])
                if j >= 2:
                    tensor.wait_ge(p.sems["V"], vcp[j - 2])
                p.inc(tensor.transpose(etp[:, :], e_sb[jp][:, :],
                                       eyeb[:, :]), "T", 1)
                vector.wait_ge(p.sems["T"], p.cnt["T"])
                v = vector.tensor_copy(et_sb[:, :], etp[:, :])
                p.inc(v, "V", 1)
                vcp[j] = p.cnt["V"]
                pvp = pvs[jp]
                tensor.wait_ge(p.sems["V"], p.cnt["V"])
                if j >= 2:
                    tensor.wait_ge(p.sems["V"], vpv[j - 2])
                p.inc(tensor.matmul(pvp[:, :], et_sb[:, :],
                                    vh_sb[:, j * 64:(j + 1) * 64],
                                    start=True, stop=True), "T", 1)
                tpv[j] = p.cnt["T"]
                vector.wait_ge(p.sems["T"], p.cnt["T"])
                v = vector.tensor_scalar(o_sb[:, j * 64:(j + 1) * 64],
                                         pvp[:, :], rs1[jp][:, 0:1], None,
                                         ALU.mult)
                p.inc(v, "V", 1)
                vpv[j] = p.cnt["V"]

            for hh in range(16):
                hp, hi = hh // 2, hh % 2
                hpar = hh % 2
                lq = qh_sb[hi * 64:hi * 64 + 64, hp * 128:(hp + 1) * 128]
                lk = kh_sb[hi * 64:hi * 64 + 64, hp * 128:(hp + 1) * 128]
                spsu = sps[hpar]
                if hh >= 2:
                    tensor.wait_ge(p.sems["V"], vmsk[hh - 2])
                p.inc(tensor.matmul(spsu[:, :], lq, lk, start=True,
                                    stop=True), "T", 1)
                vector.wait_ge(p.sems["T"], p.cnt["T"])
                if hh >= 2:
                    vector.wait_ge(p.sems["A"], aexp[hh - 2])
                p.inc(vector.tensor_tensor(sm_sb[hpar][:, :], spsu[:, :],
                                           tril_sb[:, :], ALU.add), "V", 1)
                vmsk[hh] = p.cnt["V"]
                if hh >= 1:
                    b_s3(hh - 1)
                p.inc(vector.reduce_max(mx1[hpar][:, :], sm_sb[hpar][:, :],
                                        axis=AX.X, negate=True), "V", 1)
                scalar.wait_ge(p.sems["V"], p.cnt["V"])
                if hh >= 2:
                    scalar.wait_ge(p.sems["T"], tpv.get(hh - 2, 0))
                p.inc(scalar.activation(out=e_sb[hpar][:, :],
                                        in_=sm_sb[hpar][:, :], func=AF.Exp,
                                        bias=mx1[hpar][:, 0:1], scale=1.0,
                                        accum_out=rs1[hpar][:, :]), "A", 1)
                aexp[hh] = p.cnt["A"]
                vector.wait_ge(p.sems["A"], p.cnt["A"])
                p.inc(vector.reciprocal(rs1[hpar][:, :], rs1[hpar][:, :]),
                      "V", 1)
            b_s3(15)
            tensor.wait_ge(p.sems["V"], p.cnt["V"])
        with nc.psum_tensor(f"ot_ps{blk}", [128, 1024], BF16) as ot_ps:
            for c in range(8):
                mm = tensor.transpose(ot_ps[:, c * 128:(c + 1) * 128],
                                      o_sb[:, c * 128:(c + 1) * 128],
                                      eyeb[:, :])
            p.inc(mm, "T", 1)
            vector.wait_ge(p.sems["T"], p.cnt["T"])
            p.inc(vector.tensor_copy(oT_sb[:, :], ot_ps[:, :]), "V", 1)
            tensor.wait_ge(p.sems["V"], p.cnt["V"])
        with nc.psum_tensor(f"o2_ps0{blk}", [128, 128], F32) as o20, \
             nc.psum_tensor(f"o2_ps1{blk}", [128, 128], F32) as o21:
            vst2 = {}
            for wc in range(8):
                o2_ps = o20 if wc % 2 == 0 else o21
                if wc >= 2:
                    tensor.wait_ge(p.sems["V"], vst2[wc - 2])
                for c in range(8):
                    mm = tensor.matmul(o2_ps[:, :],
                                       wo_sb[:, (wc * 8 + c) * 128:
                                             (wc * 8 + c) * 128 + 128],
                                       oT_sb[:, c * 128:(c + 1) * 128],
                                       start=(c == 0), stop=(c == 7))
                p.inc(mm, "T", 1)
                vector.wait_ge(p.sems["T"], p.cnt["T"])
                p.inc(vector.tensor_scalar(o2_sb[:, wc * 128:(wc + 1) * 128],
                                           o2_ps[:, :], bo_sb[:, wc:wc + 1],
                                           None, ALU.add), "V", 1)
                vst2[wc] = p.cnt["V"]
            tensor.wait_ge(p.sems["V"], p.cnt["V"])
        with nc.psum_tensor(f"y_ps0{blk}", [128, 128], F32) as y0, \
             nc.psum_tensor(f"y_ps1{blk}", [128, 128], F32) as y1:
            ys = [y0, y1]
            gst = {}
            for oc in range(32):
                psu = ys[oc % 2]
                if oc >= 2:
                    tensor.wait_ge(p.sems["G"], gst[oc - 2])
                for c in range(8):
                    mm = tensor.matmul(psu[:, :],
                                       proj_sb[:, (oc * 8 + c) * 128:
                                               (oc * 8 + c) * 128 + 128],
                                       o2_sb[:, c * 128:(c + 1) * 128],
                                       start=(c == 0), stop=(c == 7))
                p.inc(mm, "T", 1)
                vector.wait_ge(p.sems["T"], p.cnt["T"])
                if oc >= 2:
                    vector.wait_ge(p.sems["G"], gst[oc - 2])
                v = vector.tensor_scalar(yev[oc % 2][:, :], psu[:, :],
                                         pb_sb[:, oc:oc + 1], None, ALU.add)
                p.inc(v, "V", 1)
                g.wait_ge(p.sems["V"], p.cnt["V"])
                p.inc(g.dma_start(out=x_state[oc * 128:(oc + 1) * 128,
                                              blk * 128:(blk + 1) * 128],
                                  in_=yev[oc % 2][:, :], accum_op=ALU.add),
                      "G", 16)
                gst[oc] = p.cnt["G"]
    p.wait(g, "G")
    p.barrier()


def build_L2():
    p = Prog("mk_L2")
    nc = p.nc
    sync, scalar, tensor, vector, g = (nc.sync, nc.scalar, nc.tensor,
                                       nc.vector, nc.gpsimd)
    p.sem("Yev")

    x_in = p.din("xT", (D, TOK))
    parts_in = p.din("partials1", (8, 64))
    qs_in = p.din("qs1", (1, TOK))
    dense_in = p.din("dense1", (D, TOK))
    assign_in = p.din("assign", (1, D), I32)
    iota_in = p.din("iota64", (64, 1))
    eye_in = p.din("eye", (128, 128))
    onesrow_in = p.din("onesrow", (1, 128))
    onescol_in = p.din("onescol", (128, 1))
    cw_in = p.din("cw", (3, 24, 128, NK * 128))
    cb_in = p.din("cb", (128, 24))
    wq_in = p.din("wq0", (8, 128, 1024))
    wk_in = p.din("wk0", (8, 128, 1024))
    wv_in = p.din("wv0r", (8, 128, 1024))
    wo_in = p.din("wo0", (8, 128, 1024))
    proj_in = p.din("proj0", (32, 128, 1024))
    bq_in = p.din("bq0", (128, 8))
    bk_in = p.din("bk0", (128, 8))
    bvb_in = p.din("bv0b", (128, 1024))
    bo_in = p.din("bo0", (128, 8))
    pb_in = p.din("pb0", (128, 32))
    tril_in = p.din("trilm", (128, 128))
    w0f = p.din("w0f", (32, 128, NK * 128))
    b0f_in = p.din("b0f", (128, 32))
    centT_in = p.din("centT", (NK, 128, 64))
    stamp_in = p.din("stamp", (1, 1), I32)

    x_state = p.dout("x2", (D, TOK))
    partials_out = p.dout("partials3", (1, 64))
    qs_out = p.dout("qs3", (1, TOK))
    dense_out = p.dout("dense3", (D, TOK))

    qkv_scr = p.dscr("qkv_scr", (3072, TOK), BF16)
    shA = p.dscr("sh_haloA", (5, D), BF16, shared=True)
    shB = p.dscr("sh_haloB", (5, D), BF16, shared=True)
    flA = p.dscr("sh_flagA", (1, 4), I32, shared=True)
    flB = p.dscr("sh_flagB", (1, 4), I32, shared=True)

    import os
    STAGE = int(os.environ.get("L2STAGE", "3"))
    DBG = os.environ.get("L2DBG", "") == "1"
    if DBG:
        dbg_qkv = p.dout("dbg_qkv", (3072, TOK), F32)

    bufs = {}
    bufs["xf32"] = [p.sb("xf32a", (128, TOK)), p.sb("xf32b", (128, TOK))]
    bufs["sq"] = p.sb("sq", (128, TOK))
    bufs["ones_col"] = p.sb("ones_col", (128, 1))
    bufs["onesrow"] = p.sb("onesrow_sb", (1, 128))
    bufs["eye_f32"] = p.sb("eye_sb", (128, 128))
    bufs["rs_half"] = p.sb("rs_half", (128, 32))
    bufs["b0f_sb"] = p.sb("b0f_sb", (128, 32))

    with p.es, nc.body("main", valid_engines=ALL_ENG):
        p.dma(sync, bufs["ones_col"][:, :], onescol_in, "S", wait=False)
        p.dma(sync, bufs["onesrow"][:, :], onesrow_in, "S", wait=False)
        p.dma(sync, bufs["eye_f32"][:, :], eye_in, "S", wait=False)
        p.dma(sync, bufs["b0f_sb"][:, :], b0f_in, "S", wait=False)
        p.dma(sync, x_state, x_in, "S", wait=False)
        p.wait(sync, "S")
        p.barrier()

        with ExitStack() as esc:
            for nm, shp, dt in (("parts_sb", (8, 64), F32),
                                ("csel_col", (64, 1), F32),
                                ("ass_i", (1, D), I32), ("ass_f", (1, D), F32),
                                ("iota_sb", (64, 1), F32),
                                ("oh_sb", (64, D), F32),
                                ("qs_row", (1, TOK), F32),
                                ("QS_sb", (128, TOK), F32)):
                bufs[nm] = p.sb(nm, shp, dt, es=esc)
            p.dma(sync, bufs["iota_sb"][:, :], iota_in, "S", wait=False)
            emit_csel(p, bufs, parts_in, qs_in, assign_in, tag="c1")
            emit_maskaccum(p, bufs, dense_in, x_state, tag="m1")

        if STAGE >= 2:
            with ExitStack() as esl:
                xlnp = p.sb("xlnp", (128, NK * PADW), BF16, es=esl)
                with ExitStack() as esln2:
                    bufs["rsb"] = p.sb("rsb2", (128, TOK), F32, es=esln2)
                    bufs["bb"] = p.sb("bb2", (128, TOK), F32, es=esln2)
                    with ExitStack() as esr:
                        for nm, shp, dt in (("srow", (1, TOK), F32),
                                            ("qrow", (1, TOK), F32),
                                            ("trow", (1, TOK), F32)):
                            bufs[nm] = p.sb(nm, shp, dt, es=esr)
                        emit_ln_stats(p, bufs, x_state, tag="2")
                    emit_ln_norm_padded(p, bufs, x_state, xlnp)
                with ExitStack() as esh:
                    bufs["zcol_bf"] = p.sb("zcol_bf", (128, 1), BF16, es=esh)
                    bufs["stamp_sb"] = p.sb("stamp_sb", (1, 1), I32, es=esh)
                    emit_halo(p, bufs, xlnp, shA, shB, flA, flB, stamp_in)
                with ExitStack() as esv:
                    bufs["cstrips"] = [
                        [p.sb(f"cst{b}_{t}", (128, NK * 128), BF16, es=esv)
                         for t in range(3)] for b in range(2)]
                    bufs["cb_sb"] = p.sb("cb_sb", (128, 24), F32, es=esv)
                    bufs["conv_ev"] = [p.sb("cev0", (128, 512), BF16, es=esv),
                                       p.sb("cev1", (128, 512), BF16, es=esv)]
                    p.dma(sync, bufs["cb_sb"][:, :], cb_in, "S")
                    emit_conv(p, bufs, xlnp, cw_in, qkv_scr)
                    if DBG:
                        for r in range(24):
                            p.dma(g, dbg_qkv[r * 128:(r + 1) * 128, :],
                                  qkv_scr[r * 128:(r + 1) * 128, :], "G",
                                  wait=False)
                        p.wait(g, "G")
                        p.barrier()
            with ExitStack() as esa:
                for nm, shp, dt, src in (
                        ("wq_sb", (128, 8 * 1024), BF16, wq_in),
                        ("wk_sb", (128, 8 * 1024), BF16, wk_in),
                        ("wv_sb", (128, 8 * 1024), BF16, wv_in),
                        ("wo_sb", (128, 8 * 1024), BF16, wo_in),
                        ("proj_sb", (128, 32 * 1024), BF16, proj_in)):
                    bufs[nm] = p.sb(nm, shp, dt, es=esa)
                    p.dma(g, bufs[nm][:, :].rearrange(
                        "p (a b) -> p a b", b=src.shape[-1]),
                        src.rearrange("a p b -> p a b"), "G", wait=False)
                for nm, shp, src in (("bq_sb", (128, 8), bq_in),
                                     ("bk_sb", (128, 8), bk_in),
                                     ("bvb_sb", (128, 1024), bvb_in),
                                     ("bo_sb", (128, 8), bo_in),
                                     ("pb_sb", (128, 32), pb_in),
                                     ("tril_sb", (128, 128), tril_in)):
                    bufs[nm] = p.sb(nm, shp, F32, es=esa)
                    p.dma(sync, bufs[nm][:, :], src, "S", wait=False)
                bufs["eye_bf"] = p.sb("eye_bf", (128, 128), BF16, es=esa)
                p.dma(g, bufs["eye_bf"][:, :], eye_in, "G", wait=False)
                for nm, shp, dt in (
                        ("qh_sb", (128, 8 * 128), BF16),
                        ("kh_sb", (128, 8 * 128), BF16),
                        ("vh_sb", (128, 1024), BF16),
                        ("et_sb", (128, 128), BF16),
                        ("o_sb", (128, 1024), BF16),
                        ("oT_sb", (128, 8 * 128), BF16),
                        ("o2_sb", (128, 8 * 128), BF16)):
                    bufs[nm] = p.sb(nm, shp, dt, es=esa)
                for nm, shp, dt in (
                        ("q_sb", (128, 1024), BF16),
                        ("k_sb", (128, 1024), BF16),
                        ("v_sb", (128, 1024), BF16),
                        ("sm_sb", (128, 128), F32),
                        ("e_sb", (128, 128), BF16),
                        ("mx", (128, 1), F32), ("rowsum", (128, 1), F32),
                        ("yev", (128, 128), F32)):
                    bufs[nm] = [p.sb(nm + str(i), shp, dt, es=esa)
                                for i in range(2)]
                p.wait(sync, "S")
                p.wait(g, "G")
                p.barrier()
                emit_l0attn_blocks(p, bufs, qkv_scr, x_state)

        if STAGE >= 3:
            with ExitStack() as es2:
                bufs["xln"] = p.sb("xln", (128, NK * TOK), BF16, es=es2)
                bufs["rsb"] = p.sb("rsb3", (128, TOK), F32, es=es2)
                bufs["bb"] = p.sb("bb3", (128, TOK), F32, es=es2)
                with ExitStack() as es2b:
                    for nm, shp, dt in (("srow", (1, TOK), F32),
                                        ("qrow", (1, TOK), F32),
                                        ("trow", (1, TOK), F32)):
                        bufs[nm] = p.sb(nm, shp, dt, es=es2b)
                    emit_ln_stats(p, bufs, x_state, tag="3")
                with ExitStack() as es2c:
                    for nm, shp, dt in (("centT_sb", (128, NK * 64), F32),
                                        ("e64", (128, 64), F32),
                                        ("hits", (128, 64), F32),
                                        ("mx", (128, 1), F32),
                                        ("rowsum", (128, 1), F32),
                                        ("qsc", (128, 16), F32),
                                        ("cnt_sb", (1, 64), F32),
                                        ("qsr", (16, 128), F32)):
                        bufs[nm] = p.sb(nm, shp, dt, es=es2c)
                    p.dma(sync,
                          bufs["centT_sb"][:, :].rearrange(
                              "p (k j) -> p k j", k=NK),
                          centT_in.rearrange("k p j -> p k j"), "S",
                          wait=False)
                    p.wait(sync, "S")
                    p.barrier()
                    with nc.psum_tensor("ps_dots3", [128, 1024],
                                        F32) as ps_dots:
                        emit_ln_norm(p, bufs, x_state, bufs["xln"],
                                     consume_f32=lambda k: emit_dots(
                                         p, bufs, ps_dots, k))
                        emit_gate(p, bufs, ps_dots, partials_out, qs_out,
                                  tag="3")
                with ExitStack() as es3:
                    bufs["strips"] = [
                        p.sb("stripA", (128, NK * 128), BF16, es=es3),
                        p.sb("stripB", (128, NK * 128), BF16, es=es3)]
                    bufs["dense_ev"] = [
                        p.sb("dense_ev0", (128, 512), F32, es=es3),
                        p.sb("dense_ev1", (128, 512), F32, es=es3)]
                    estore = {}

                    def dense_cb(oc, tt, psu, step):
                        evb = bufs["dense_ev"][step % 2]
                        p.wait(vector, "T")
                        if step >= 2:
                            p.wait(vector, "E", estore[step - 2])
                        v = vector.tensor_scalar(evb[:, :], psu[:, :],
                                                 bufs["b0f_sb"][:, oc:oc + 1],
                                                 None, ALU.add)
                        p.inc(v, "EVd3", 1)
                        p.wait(scalar, "EVd3")
                        p.inc(scalar.dma_start(
                            out=dense_out[oc * 128:(oc + 1) * 128,
                                          tt * 512:(tt + 1) * 512],
                            in_=evb[:, :]), "E", 16)
                        estore[step] = p.cnt["E"]

                    p.sem("EVd3")
                    emit_linear(p, bufs, bufs["xln"], w0f, 32, dense_cb,
                                tag="d3")

        p.wait(sync, "E")
        p.wait(sync, "S")
        p.wait(sync, "G")
    return p




MAGIC3 = 0x5A18


def emit_l1ffn(p, bufs, x_state, w12_in, b12_in, w3_in, b3_in, x4_scr,
               tag=""):
    nc = p.nc
    sync, scalar, tensor, vector, g = (nc.sync, nc.scalar, nc.tensor,
                                       nc.vector, nc.gpsimd)
    with ExitStack() as esf:
        xln = p.sb("xlnF" + tag, (128, NK * TOK), BF16, es=esf)
        with ExitStack() as esr:
            for nm in ("srow", "qrow", "trow"):
                bufs[nm] = p.sb(nm + "F" + tag, (1, TOK), F32, es=esr)
            emit_ln_stats(p, bufs, x_state, tag="F" + tag)
        emit_ln_norm(p, bufs, x_state, xln)
        with ExitStack() as esw:
            st1b = [p.sb("w12sA%d" % i + tag, (128, NK * 128), BF16,
                         es=esw) for i in range(2)]
            st2b = [p.sb("w12sB%d" % i + tag, (128, NK * 128), BF16,
                         es=esw) for i in range(2)]
            b12_sb = p.sb("b12_sb" + tag, (128, 64), F32, es=esw)
            t1 = p.sb("t1_" + tag, (128, 512), F32, es=esw)
            x4ev = [p.sb(f"x4ev{i}_" + tag, (128, 512), BF16, es=esw)
                    for i in range(2)]
            p.dma(sync, b12_sb[:, :], b12_in, "S")
            ev = "EVf" + tag
            p.sem(ev)
            estore = {}
            step = 0
            tstore = {}
            with nc.psum_tensor("f1psa" + tag, [128, 512], F32) as ps1a, \
                 nc.psum_tensor("f2psa" + tag, [128, 512], F32) as ps2a, \
                 nc.psum_tensor("f1psb" + tag, [128, 512], F32) as ps1b, \
                 nc.psum_tensor("f2psb" + tag, [128, 512], F32) as ps2b:
                p.dma(g, st1b[0][:, :], w12_in[0], "G", wait=False)
                p.dma(g, st2b[0][:, :], w12_in[32], "G", wait=False)
                gtarget = [p.cnt["G"]]
                for j in range(32):
                    if j + 1 < 32:
                        if j >= 1:
                            g.wait_ge(p.sems["T"], tstore[j - 1])
                        p.dma(g, st1b[(j + 1) % 2][:, :], w12_in[j + 1],
                              "G", wait=False)
                        p.dma(g, st2b[(j + 1) % 2][:, :], w12_in[j + 33],
                              "G", wait=False)
                        gtarget.append(p.cnt["G"])
                    st1, st2 = st1b[j % 2], st2b[j % 2]
                    tensor.wait_ge(p.sems["G"], gtarget[j])
                    for tt in range(NTT):
                        ps1 = ps1a if step % 2 == 0 else ps1b
                        ps2 = ps2a if step % 2 == 0 else ps2b
                        if step >= 2:
                            tensor.wait_ge(p.sems[ev], step - 1)
                        for k in range(NK):
                            sl = slice(k * TOK + tt * 512,
                                       k * TOK + tt * 512 + 512)
                            tensor.matmul(ps1[:, :],
                                          st1[:, k * 128:(k + 1) * 128],
                                          xln[:, sl], start=(k == 0),
                                          stop=(k == NK - 1))
                            mm = tensor.matmul(ps2[:, :],
                                               st2[:, k * 128:(k + 1) * 128],
                                               xln[:, sl], start=(k == 0),
                                               stop=(k == NK - 1))
                        p.inc(mm, "T", 1)
                        scalar.wait_ge(p.sems["T"], p.cnt["T"])
                        p.inc(scalar.activation(
                            out=t1[:, :], in_=ps1[:, :], func=AF.Silu,
                            bias=b12_sb[:, j:j + 1], scale=1.0), "A", 1)
                        vector.wait_ge(p.sems["A"], p.cnt["A"])
                        if step >= 2:
                            p.wait(vector, "E", estore[step - 2])
                        evb = x4ev[step % 2]
                        v = vector.scalar_tensor_tensor(
                            evb[:, :], ps2[:, :], b12_sb[:, j + 32:j + 33],
                            t1[:, :], ALU.add, ALU.mult)
                        p.inc(v, ev, 1)
                        scalar.wait_ge(p.sems[ev], p.cnt[ev])
                        p.inc(scalar.dma_start(
                            out=x4_scr[j * 128:(j + 1) * 128,
                                       tt * 512:(tt + 1) * 512],
                            in_=evb[:, :]), "E", 16)
                        estore[step] = p.cnt["E"]
                        step += 1
                    tstore[j] = p.cnt["T"]
            p.wait(scalar, "E")
            p.barrier()
    with ExitStack() as esw3:
        x4r = p.sb("x4r" + tag, (128, NK * TOK), BF16, es=esw3)
        b3_sb = p.sb("b3_sb" + tag, (128, 32), F32, es=esw3)
        bufs["strips"] = [p.sb("w3sA" + tag, (128, NK * 128), BF16, es=esw3),
                          p.sb("w3sB" + tag, (128, NK * 128), BF16, es=esw3)]
        yev = [p.sb(f"yf{i}_" + tag, (128, 512), F32, es=esw3)
               for i in range(2)]
        p.dma(sync, b3_sb[:, :], b3_in, "S", wait=False)
        for k in range(NK):
            p.dma(g, x4r[:, k * TOK:(k + 1) * TOK],
                  x4_scr[k * 128:(k + 1) * 128, :], "G", wait=False)
        p.wait(sync, "S")
        p.wait(g, "G")
        p.barrier()
        gstore = {}
        evn = "EVw3" + tag
        p.sem(evn)

        def w3_cb(oc, tt, psu, step):
            evb = yev[step % 2]
            p.wait(vector, "T")
            if step >= 2:
                p.wait(vector, "G", gstore[step - 2])
            v = vector.tensor_scalar(evb[:, :], psu[:, :],
                                     b3_sb[:, oc:oc + 1], 0.5,
                                     ALU.add, ALU.mult)
            p.inc(v, evn, 1)
            p.wait(g, evn)
            p.inc(g.dma_start(out=x_state[oc * 128:(oc + 1) * 128,
                                          tt * 512:(tt + 1) * 512],
                              in_=evb[:, :], accum_op=ALU.add), "G", 16)
            gstore[step] = p.cnt["G"]

        emit_linear2(p, bufs, x4r, w3_in, 32, w3_cb, free_sem=evn,
                     tag="w3" + tag)
        p.wait(g, "G")
    p.barrier()


def emit_linear2(p, bufs, xsrc_sb, strips_hbm, n_oc, out_cb, free_sem,
                 n_tok=TOK, nk=NK, tag=""):
    nc = p.nc
    g, tensor = nc.gpsimd, nc.tensor
    nts = n_tok // 512
    sbufs = bufs["strips"]
    with nc.psum_tensor(f"lin_p0{tag}", [128, 512], F32) as p0, \
         nc.psum_tensor(f"lin_p1{tag}", [128, 512], F32) as p1:
        pss = [p0, p1]
        p.dma(g, sbufs[0][:, :], strips_hbm[0], "G", wait=False)
        gtarget = [p.cnt["G"]]
        tload = [None, None]
        step = 0
        for oc in range(n_oc):
            if oc + 1 < n_oc:
                if tload[(oc + 1) % 2] is not None:
                    g.wait_ge(p.sems["T"], tload[(oc + 1) % 2])
                p.dma(g, sbufs[(oc + 1) % 2][:, :], strips_hbm[oc + 1], "G",
                      wait=False)
                gtarget.append(p.cnt["G"])
            sbw = sbufs[oc % 2]
            tensor.wait_ge(p.sems["G"], gtarget[oc])
            for tt in range(nts):
                psu = pss[step % 2]
                if step >= 2:
                    tensor.wait_ge(p.sems[free_sem], step - 1)
                for k in range(nk):
                    mm = tensor.matmul(
                        psu[:, :], sbw[:, k * 128:(k + 1) * 128],
                        xsrc_sb[:, k * n_tok + tt * 512:
                                k * n_tok + tt * 512 + 512],
                        start=(k == 0), stop=(k == nk - 1))
                p.inc(mm, "T", 1)
                out_cb(oc, tt, psu, step)
                step += 1
            tload[oc % 2] = p.cnt["T"]
    p.barrier()


def _gq(parity, qb):
    return (qb if qb < 8 else 16 + qb) if parity == 0 else 8 + qb


def _own(gg):
    if gg < 8:
        return 0, gg
    if gg >= 24:
        return 0, gg - 16
    return 1, gg - 8


def emit_l1attn(p, bufs, x_state, wq_in, wk_in, wv_in, wo_in, cos_in,
                sin_in, mask_in, stamp_in, shKA, shKB, shVA, shVB, flA, flB,
                q_scr, myK, myV, att_scr):
    nc = p.nc
    sync, scalar, tensor, vector, g = (nc.sync, nc.scalar, nc.tensor,
                                       nc.vector, nc.gpsimd)
    SC = 1.0 / np.sqrt(128.0)

    with ExitStack() as esa:
        xln = p.sb("xlnA", (128, NK * TOK), BF16, es=esa)
        with ExitStack() as esr:
            for nm in ("srow", "qrow", "trow"):
                bufs[nm] = p.sb(nm + "A1", (1, TOK), F32, es=esr)
            emit_ln_stats(p, bufs, x_state, tag="A1")
        emit_ln_norm(p, bufs, x_state, xln)
        with ExitStack() as esw:
            cos_sb = p.sb("cos_sb", (128, TOK), F32, es=esw)
            sin_sb = p.sb("sin_sb", (128, TOK), F32, es=esw)
            stwb = [p.sb("l1aw%d" % i, (128, NK * 128), BF16, es=esw)
                    for i in range(2)]
            rta = p.sb("rta", (64, 512), F32, es=esw)
            rtb = p.sb("rtb", (64, 512), F32, es=esw)
            rev = [p.sb(f"l1rev{i}", (128, 512), BF16, es=esw)
                   for i in range(2)]
            p.dma(sync, cos_sb[:, :], cos_in, "S", wait=False)
            p.dma(sync, sin_sb[:, :], sin_in, "S", wait=False)
            p.wait(sync, "S")
            p.barrier()
            estore = {}
            vstore = {}
            step = 0
            seq = [("K", wk_in, myK, True), ("Q", wq_in, q_scr, True),
                   ("V", wv_in, myV, False)]
            flat = [(wh, w_in, dst, rot, oc)
                    for (wh, w_in, dst, rot) in seq for oc in range(32)]
            p.dma(g, stwb[0][:, :], flat[0][1][0], "G", wait=False)
            gtargetA = [p.cnt["G"]]
            tstoreA = {}
            with nc.psum_tensor("l1pA0", [128, 512], F32) as psA0, \
                 nc.psum_tensor("l1pA1", [128, 512], F32) as psA1:
                for fi, (which, w_in, dst, do_rot, oc) in enumerate(flat):
                    if fi + 1 < len(flat):
                        if fi >= 1:
                            g.wait_ge(p.sems["T"], tstoreA[fi - 1])
                        nxt = flat[fi + 1]
                        p.dma(g, stwb[(fi + 1) % 2][:, :], nxt[1][nxt[4]],
                              "G", wait=False)
                        gtargetA.append(p.cnt["G"])
                    stw = stwb[fi % 2]
                    tensor.wait_ge(p.sems["G"], gtargetA[fi])
                    for tt in range(NTT):
                        psA = psA0 if step % 2 == 0 else psA1
                        if step >= 2:
                            tensor.wait_ge(p.sems["V"], vstore[step - 2])
                        for k in range(NK):
                            sl = slice(k * TOK + tt * 512,
                                       k * TOK + tt * 512 + 512)
                            mm = tensor.matmul(
                                psA[:, :], stw[:, k * 128:(k + 1) * 128],
                                xln[:, sl], start=(k == 0),
                                stop=(k == NK - 1))
                        p.inc(mm, "T", 1)
                        vector.wait_ge(p.sems["T"], p.cnt["T"])
                        if step >= 2:
                            p.wait(vector, "E", estore[step - 2])
                        evb = rev[step % 2]
                        sl2 = slice(tt * 512, (tt + 1) * 512)
                        if do_rot:
                            pe, po = psA[0:64, :], psA[64:128, :]
                            ce = cos_sb[0:64, sl2]
                            se = sin_sb[0:64, sl2]
                            vector.tensor_tensor(rta[:, :], pe, ce, ALU.mult)
                            vector.tensor_tensor(rtb[:, :], po, se, ALU.mult)
                            vector.tensor_sub(evb[0:64, :], rta[:, :],
                                              rtb[:, :])
                            vector.tensor_tensor(rta[:, :], pe, se, ALU.mult)
                            vector.tensor_tensor(rtb[:, :], po, ce, ALU.mult)
                            v = vector.tensor_add(evb[64:128, :],
                                                  rta[:, :], rtb[:, :])
                        else:
                            v = vector.tensor_copy(evb[:, :], psA[:, :])
                        p.inc(v, "V", 1)
                        vstore[step] = p.cnt["V"]
                        scalar.wait_ge(p.sems["V"], p.cnt["V"])
                        p.inc(scalar.dma_start(
                            out=dst[oc * 128:(oc + 1) * 128,
                                    tt * 512:(tt + 1) * 512],
                            in_=evb[:, :]), "E", 16)
                        estore[step] = p.cnt["E"]
                        step += 1
                    tstoreA[fi] = p.cnt["T"]
            p.wait(scalar, "E")
            p.barrier()

    stamp_sb = bufs["stamp_sb"]
    p.dma(g, stamp_sb[:, :], stamp_in, "G")
    me = g.partition_id()
    parity = g.to_reg(me % 2)
    gc0 = p.cnt["G"]

    def shr_copy(dstK, dstV, fl):
        p.cnt["G"] += 16
        g.dma_start(out=dstK, in_=myK).then_inc(p.sems["G"], 16)
        p.cnt["G"] += 16
        g.dma_start(out=dstV, in_=myV).then_inc(p.sems["G"], 16)
        g.wait_ge(p.sems["G"], p.cnt["G"])
        p.cnt["G"] += 16
        g.dma_start(out=APc(fl.tensor, 0, [[1, 1], [1, 1]]),
                    in_=stamp_sb[0:1, 0:1]).then_inc(p.sems["G"], 16)
        g.wait_ge(p.sems["G"], p.cnt["G"])

    def poll3(fl):
        preg = nc.alloc_register(ET.Pool, nc.make_name("poll3", True))
        creg = nc.alloc_register(ET.Pool, nc.make_name("pcond3", True))
        g.reg_mov(creg, 1)
        with g.While(creg):
            g.reg_load(preg, APc(fl.tensor, 0, [[1, 1], [1, 1]]))
            g.reg_alu(creg, preg, MAGIC3, ALU.not_equal)

    with g.If_eq(parity, 0):
        shr_copy(shKA, shVA, flA)
        poll3(flB)
    gc_after = p.cnt["G"]
    with g.Else():
        p.cnt["G"] = gc0
        shr_copy(shKB, shVB, flB)
        poll3(flA)
    p.cnt["G"] = gc_after
    p.barrier()

    with ExitStack() as esb:
        mask_sb = p.sb("mask_sb", (128, 144 * 128), BF16, es=esb)
        ksA = [p.sb(f"ksA{i}", (128, TOK), BF16, es=esb) for i in range(2)]
        ksB = [p.sb(f"ksB{i}", (128, TOK), BF16, es=esb) for i in range(2)]
        vsA = [p.sb(f"vsA{i}", (128, TOK), BF16, es=esb) for i in range(2)]
        vsB = [p.sb(f"vsB{i}", (128, TOK), BF16, es=esb) for i in range(2)]
        qt = [p.sb(f"qt{i}", (128, 128), BF16, es=esb) for i in range(2)]
        ssb = [p.sb(f"ssb{i}", (128, 4096), F32, es=esb) for i in range(2)]
        estrip = [p.sb(f"estrip{i}", (128, 4096), BF16, es=esb)
                  for i in range(2)]
        etstrip = p.sb("etstrip", (128, 4096), BF16, es=esb)
        eyeb = p.sb("eyeb3", (128, 128), BF16, es=esb)
        mxc = [p.sb(f"mxc{i}", (128, 1), F32, es=esb) for i in range(2)]
        rsc = [p.sb(f"rsc{i}", (128, 1), F32, es=esb) for i in range(2)]
        aev = [p.sb(f"aev{i}", (128, 128), BF16, es=esb) for i in range(2)]
        for mq in range(16):
            msl = slice(mq * 9, (mq + 1) * 9)
            p.dma(g, mask_sb[:, mq * 9 * 128:(mq + 1) * 9 * 128].rearrange(
                      "p (m k) -> p m k", m=9),
                  mask_in[msl].rearrange("m p k -> p m k"), "G", wait=False)
        p.dma(g, eyeb[:, :], bufs["eye_in_ap"], "G", wait=False)
        p.wait(g, "G")
        p.barrier()

        def strip_loads(h, par):
            p.dma(sync, ksA[par][:, :], shKA[h * 128:(h + 1) * 128, :], "S",
                  wait=False)
            p.dma(sync, ksB[par][:, :], shKB[h * 128:(h + 1) * 128, :], "S",
                  wait=False)
            for bb2 in range(16):
                p.dma(sync, vsA[par][:, bb2 * 128:(bb2 + 1) * 128],
                      shVA[h * 128:(h + 1) * 128,
                           bb2 * 128:(bb2 + 1) * 128], "S", wait=False,
                      transpose=True)
                p.dma(sync, vsB[par][:, bb2 * 128:(bb2 + 1) * 128],
                      shVB[h * 128:(h + 1) * 128,
                           bb2 * 128:(bb2 + 1) * 128], "S", wait=False,
                      transpose=True)
            return p.cnt["S"]

        def q_load(h, qb, par):
            p.dma(sync, qt[par][:, :],
                  q_scr[h * 128:(h + 1) * 128, qb * 128:(qb + 1) * 128],
                  "S", wait=False)
            return p.cnt["S"]

        items = [(h, qb) for h in range(32) for qb in range(16)]
        sload = {}
        qload = {}
        vev = {}
        aexp = {}
        vcopy = {}
        vpv = {}
        tscore = {}
        ttrans = {}
        estore3 = {}

        sload[0] = strip_loads(0, 0)
        qload[0] = q_load(0, 0, 0)

        with nc.psum_tensor("s3ps0", [128, 512], F32) as sp0, \
             nc.psum_tensor("s3ps1", [128, 512], F32) as sp1, \
             nc.psum_tensor("et3ps", [128, 4096], BF16) as etp, \
             nc.psum_tensor("pv3ps0", [128, 128], F32) as pvp0, \
             nc.psum_tensor("pv3ps1", [128, 128], F32) as pvp1:
            sps = [sp0, sp1]
            pvs = [pvp0, pvp1]
            spst = 0

            def do_s3(j):
                hj, qbj = items[j]
                pj = j % 2
                loj = qbj if qbj < 8 else qbj + 8
                hij = qbj + 8 if qbj < 8 else qbj + 16
                nb = hij + 1
                vAj = vsA[hj % 2]
                vBj = vsB[hj % 2]
                tensor.wait_ge(p.sems["A"], aexp[j])
                if j >= 1:
                    tensor.wait_ge(p.sems["V"], vcopy[j - 1])
                for g2 in range(nb):
                    mm = tensor.transpose(
                        etp[:, g2 * 128:(g2 + 1) * 128],
                        estrip[pj][:, g2 * 128:(g2 + 1) * 128], eyeb[:, :])
                p.inc(mm, "T", 1)
                vector.wait_ge(p.sems["T"], p.cnt["T"])
                v = vector.tensor_copy(etstrip[:, 0:nb * 128],
                                       etp[:, 0:nb * 128])
                p.inc(v, "V", 1)
                vcopy[j] = p.cnt["V"]
                pvp = pvs[j % 2]
                tensor.wait_ge(p.sems["V"], p.cnt["V"])
                if j >= 2:
                    tensor.wait_ge(p.sems["V"], vpv[j - 2])
                for g2 in range(nb):
                    half, lb = _own(g2)
                    vsx = vAj if half == 0 else vBj
                    mm = tensor.matmul(
                        pvp[:, :], etstrip[:, g2 * 128:(g2 + 1) * 128],
                        vsx[:, lb * 128:(lb + 1) * 128],
                        start=(g2 == 0), stop=(g2 == nb - 1))
                p.inc(mm, "T", 1)
                ttrans[j] = p.cnt["T"]
                vector.wait_ge(p.sems["T"], p.cnt["T"])
                if j >= 2:
                    p.wait(vector, "E", estore3.get(j - 2, 0))
                v = vector.tensor_scalar(aev[j % 2][:, :], pvp[:, :],
                                         rsc[pj][:, 0:1], None, ALU.mult)
                p.inc(v, "V", 1)
                vpv[j] = p.cnt["V"]
                scalar.wait_ge(p.sems["V"], p.cnt["V"])
                p.inc(scalar.dma_start(
                    out=att_scr[qbj * 128:(qbj + 1) * 128,
                                hj * 128:(hj + 1) * 128],
                    in_=aev[j % 2][:, :]), "E", 16)
                estore3[j] = p.cnt["E"]
            for idx, (h, qb) in enumerate(items):
                par = idx % 2
                if idx + 1 < len(items):
                    h2, qb2 = items[idx + 1]
                    if qb2 == 0:
                        if h2 >= 2:
                            sync.wait_ge(p.sems["T"],
                                         ttrans[(h2 - 1) * 16 - 1])
                        sload[h2] = strip_loads(h2, h2 % 2)
                    if idx >= 1:
                        sync.wait_ge(p.sems["T"], tscore[idx - 1])
                    qload[idx + 1] = q_load(h2, qb2, (idx + 1) % 2)
                lo = qb if qb < 8 else qb + 8
                hi = qb + 8 if qb < 8 else qb + 16
                nblk = hi + 1
                klen = nblk * 128
                kA, kB = ksA[h % 2], ksB[h % 2]
                vA, vB = vsA[h % 2], vsB[h % 2]

                tensor.wait_ge(p.sems["S"], max(sload[h], qload[idx]))
                if idx >= 2:
                    tensor.wait_ge(p.sems["A"], aexp.get(idx - 2, 0))
                runs = []
                gg = 0
                while gg < nblk:
                    half, lb = _own(gg)
                    run = [gg]
                    while (gg + 1 < nblk and _own(gg + 1)[0] == half and
                           _own(gg + 1)[1] == _own(gg)[1] + 1):
                        gg += 1
                        run.append(gg)
                    runs.append((half, _own(run[0])[1], run))
                    gg += 1
                for half, lb0, run in runs:
                    ks = kA if half == 0 else kB
                    for c0 in range(0, len(run), 4):
                        w = min(4, len(run) - c0) * 128
                        psu = sps[spst % 2]
                        if spst >= 2:
                            vector_tgt = vev.get(("chunk", spst - 2))
                            if vector_tgt:
                                tensor.wait_ge(p.sems["V"], vector_tgt)
                        mm = tensor.matmul(
                            psu[:, 0:w], qt[par][:, :],
                            ks[:, (lb0 + c0) * 128:(lb0 + c0) * 128 + w],
                            start=True, stop=True)
                        p.inc(mm, "T", 1)
                        vector.wait_ge(p.sems["T"], p.cnt["T"])
                        gfirst = run[c0]
                        glast = gfirst + w // 128 - 1
                        dst0 = gfirst * 128
                        if glast < lo:
                            v = vector.tensor_copy(
                                ssb[par][:, dst0:dst0 + w], psu[:, 0:w])
                        elif gfirst >= lo:
                            mi = 9 * qb + (gfirst - lo)
                            v = vector.tensor_tensor(
                                ssb[par][:, dst0:dst0 + w], psu[:, 0:w],
                                mask_sb[:, mi * 128:mi * 128 + w], ALU.add)
                        else:
                            wa = (lo - gfirst) * 128
                            vector.tensor_copy(
                                ssb[par][:, dst0:dst0 + wa], psu[:, 0:wa])
                            mi = 9 * qb
                            v = vector.tensor_tensor(
                                ssb[par][:, dst0 + wa:dst0 + w],
                                psu[:, wa:w],
                                mask_sb[:, mi * 128:mi * 128 + (w - wa)],
                                ALU.add)
                        p.inc(v, "V", 1)
                        vev[("chunk", spst)] = p.cnt["V"]
                        spst += 1
                vev[idx] = p.cnt["V"]
                tscore[idx] = p.cnt["T"]

                if idx >= 1:
                    do_s3(idx - 1)

                vector.wait_ge(p.sems["V"], vev[idx])
                p.inc(vector.reduce_max(mxc[par][:, :], ssb[par][:, 0:klen],
                                        axis=AX.X, negate=True), "V", 1)
                p.wait(vector, "V")
                p.inc(vector.tensor_scalar_mul(mxc[par][:, :],
                                               mxc[par][:, :], SC), "V", 1)
                scalar.wait_ge(p.sems["V"], p.cnt["V"])
                if idx >= 2:
                    scalar.wait_ge(p.sems["T"], ttrans.get(idx - 2, 0))
                p.inc(scalar.activation(out=estrip[par][:, 0:klen],
                                        in_=ssb[par][:, 0:klen], func=AF.Exp,
                                        bias=mxc[par][:, 0:1], scale=SC,
                                        accum_out=rsc[par][:, :]), "A", 1)
                aexp[idx] = p.cnt["A"]
                vector.wait_ge(p.sems["A"], p.cnt["A"])
                p.inc(vector.reciprocal(rsc[par][:, :], rsc[par][:, :]),
                      "V", 1)

            do_s3(len(items) - 1)

        p.wait(scalar, "E")
        p.barrier()

    with ExitStack() as esc:
        attT = p.sb("attT", (128, NK * TOK), BF16, es=esc)
        bufs["strips"] = [p.sb("wosA", (128, NK * 128), BF16, es=esc),
                          p.sb("wosB", (128, NK * 128), BF16, es=esc)]
        yev = [p.sb(f"yo{i}", (128, 512), F32, es=esc) for i in range(2)]
        for c in range(NK):
            for tt in range(NTT):
                p.inc(scalar.dma_start(
                    out=attT[:, c * TOK + tt * 512:c * TOK + tt * 512 + 512],
                    in_=att_scr[tt * 512:(tt + 1) * 512,
                                c * 128:(c + 1) * 128],
                    transpose=True), "E", 16)
        p.wait(scalar, "E")
        p.barrier()
        gstore = {}
        p.sem("EVwo")

        def wo_cb(oc, tt, psu, step):
            evb = yev[step % 2]
            p.wait(vector, "T")
            if step >= 2:
                p.wait(vector, "G", gstore[step - 2])
            v = vector.tensor_copy(evb[:, :], psu[:, :])
            p.inc(v, "EVwo", 1)
            p.wait(g, "EVwo")
            p.inc(g.dma_start(out=x_state[oc * 128:(oc + 1) * 128,
                                          tt * 512:(tt + 1) * 512],
                              in_=evb[:, :], accum_op=ALU.add), "G", 16)
            gstore[step] = p.cnt["G"]

        emit_linear2(p, bufs, attT, wo_in, 32, wo_cb, free_sem="EVwo",
                     tag="wo")
        p.wait(g, "G")
    p.barrier()


def build_L3():
    p = Prog("mk_L3")
    nc = p.nc
    sync, scalar, tensor, vector, g = (nc.sync, nc.scalar, nc.tensor,
                                       nc.vector, nc.gpsimd)

    x_in = p.din("xT3", (D, TOK))
    parts_in = p.din("partials3", (8, 64))
    qs_in = p.din("qs3", (1, TOK))
    dense_in = p.din("dense3", (D, TOK))
    assign_in = p.din("assign", (1, D), I32)
    iota_in = p.din("iota64", (64, 1))
    eye_in = p.din("eye", (128, 128))
    onesrow_in = p.din("onesrow", (1, 128))
    onescol_in = p.din("onescol", (128, 1))
    w12_in = p.din("w12p", (64, 128, NK * 128))
    b12_in = p.din("b12c", (128, 64))
    w3_in = p.din("w3p", (32, 128, NK * 128))
    b3_in = p.din("b3c", (128, 32))
    wq_in = p.din("wqp", (32, 128, NK * 128))
    wk_in = p.din("wkp", (32, 128, NK * 128))
    wv_in = p.din("wvp", (32, 128, NK * 128))
    wo_in = p.din("wop", (32, 128, NK * 128))
    cos_in = p.din("cosb", (128, TOK))
    sin_in = p.din("sinb", (128, TOK))
    mask_in = p.din("maskb", (144, 128, 128), BF16)
    stamp_in = p.din("stamp3", (1, 1), I32)

    x_state = p.dout("xout", (D, TOK))

    x4_scr = p.dscr("x4_scr", (D, TOK), BF16)
    q_scr = p.dscr("q_scr", (D, TOK), BF16)
    myK = p.dscr("myK", (D, TOK), BF16)
    myV = p.dscr("myV", (D, TOK), BF16)
    att_scr = p.dscr("att_scr", (TOK, D), BF16)
    shKA = p.dscr("shKA", (D, TOK), BF16, shared=True)
    shKB = p.dscr("shKB", (D, TOK), BF16, shared=True)
    shVA = p.dscr("shVA", (D, TOK), BF16, shared=True)
    shVB = p.dscr("shVB", (D, TOK), BF16, shared=True)
    flA = p.dscr("flA3", (1, 4), I32, shared=True)
    flB = p.dscr("flB3", (1, 4), I32, shared=True)

    import os
    STAGE = int(os.environ.get("L3STAGE", "4"))

    bufs = {}
    bufs["xf32"] = [p.sb("xf32a", (128, TOK)), p.sb("xf32b", (128, TOK))]
    bufs["sq"] = p.sb("sq", (128, TOK))
    bufs["rsb"] = p.sb("rsb", (128, TOK))
    bufs["bb"] = p.sb("bb", (128, TOK))
    bufs["ones_col"] = p.sb("ones_col", (128, 1))
    bufs["onesrow"] = p.sb("onesrow_sb", (1, 128))
    bufs["rs_half"] = p.sb("rs_half", (128, 32))
    bufs["stamp_sb"] = p.sb("stamp_sb", (1, 1), I32)
    bufs["eye_in_ap"] = eye_in

    with p.es, nc.body("main", valid_engines=ALL_ENG):
        p.dma(sync, bufs["ones_col"][:, :], onescol_in, "S", wait=False)
        p.dma(sync, bufs["onesrow"][:, :], onesrow_in, "S", wait=False)
        p.dma(sync, x_state, x_in, "S", wait=False)
        p.wait(sync, "S")
        p.barrier()

        with ExitStack() as esc:
            for nm, shp, dt in (("parts_sb", (8, 64), F32),
                                ("csel_col", (64, 1), F32),
                                ("ass_i", (1, D), I32),
                                ("ass_f", (1, D), F32),
                                ("iota_sb", (64, 1), F32),
                                ("oh_sb", (64, D), F32),
                                ("qs_row", (1, TOK), F32),
                                ("QS_sb", (128, TOK), F32)):
                bufs[nm] = p.sb(nm, shp, dt, es=esc)
            p.dma(sync, bufs["iota_sb"][:, :], iota_in, "S", wait=False)
            emit_csel(p, bufs, parts_in, qs_in, assign_in, tag="c3")
            emit_maskaccum(p, bufs, dense_in, x_state, tag="m3")

        if STAGE >= 2:
            emit_l1ffn(p, bufs, x_state, w12_in, b12_in, w3_in, b3_in,
                       x4_scr, tag="f4")
        if STAGE >= 3:
            emit_l1attn(p, bufs, x_state, wq_in, wk_in, wv_in, wo_in,
                        cos_in, sin_in, mask_in, stamp_in, shKA, shKB,
                        shVA, shVB, flA, flB, q_scr, myK, myV, att_scr)
        if STAGE >= 4:
            emit_l1ffn(p, bufs, x_state, w12_in, b12_in, w3_in, b3_in,
                       x4_scr, tag="f6")

        p.wait(sync, "E")
        p.wait(sync, "S")
        p.wait(sync, "G")
        p.wait(sync, "V")
        p.wait(sync, "T")
    return p


def _host_prep3(inp, core):
    h = {}
    h["w12p"] = _prep_oc(np.asarray(inp["L1_ffn_w12"], np.float32))
    h["b12c"] = _col_chunks(np.asarray(inp["L1_ffn_b12"], np.float32))
    h["w3p"] = _prep_oc(np.asarray(inp["L1_ffn_w3"], np.float32))
    h["b3c"] = _col_chunks(np.asarray(inp["L1_ffn_b3"], np.float32))
    perm = np.concatenate([hh * 128 + np.r_[np.arange(0, 128, 2),
                                            np.arange(1, 128, 2)]
                           for hh in range(32)])
    wq = np.asarray(inp["L1_attn_wq"], np.float32)[perm]
    wk = np.asarray(inp["L1_attn_wk"], np.float32)[perm]
    h["wqp"] = _prep_oc(wq)
    h["wkp"] = _prep_oc(wk)
    h["wvp"] = _prep_oc(np.asarray(inp["L1_attn_wv"], np.float32))
    h["wop"] = _prep_oc(np.asarray(inp["L1_attn_wo"], np.float32))
    return h


def _host_core3(core):
    import ml_dtypes
    rows = _tok_rows(core)
    t = rows.astype(np.float64)
    cos = np.cos(t).astype(np.float32)
    sin = np.sin(t).astype(np.float32)
    cosb = np.ascontiguousarray(np.broadcast_to(cos[None, :], (128, TOK)))
    sinb = np.ascontiguousarray(np.broadcast_to(sin[None, :], (128, TOK)))
    parity = core % 2
    tril = np.tril(np.ones((128, 128), np.float32))
    m_none = np.zeros((128, 128), np.float32)
    m_tril = np.where(tril > 0, 0.0, -1e9).astype(np.float32)
    m_full = np.full((128, 128), -1e9, np.float32)
    tiles = np.zeros((144, 128, 128), np.float32)
    for qb in range(16):
        lo = qb if qb < 8 else qb + 8
        gq = _gq(parity, qb)
        for i in range(9):
            gcur = lo + i
            if gcur < gq:
                tiles[9 * qb + i] = m_none
            elif gcur == gq:
                tiles[9 * qb + i] = m_tril
            else:
                tiles[9 * qb + i] = m_full
    maskb = tiles.astype(ml_dtypes.bfloat16)
    return cosb, sinb, maskb



_HOST = {}


def _host_prep(inp):
    h = {}
    h["w0f"] = _prep_oc(np.asarray(inp["L0_ffn_weight"], np.float32))
    h["b0f"] = _col_chunks(np.asarray(inp["L0_ffn_bias"], np.float32))
    cent = np.asarray(inp["L0_ffn_centroids"], np.float32)
    h["centT"] = np.ascontiguousarray(cent.T.reshape(NK, 128, 64))
    h["assign"] = np.asarray(inp["L0_ffn_assignments"], np.int32).reshape(1, D)
    h["eye"] = np.eye(128, dtype=np.float32)
    h["onesrow"] = np.ones((1, 128), np.float32)
    h["onescol"] = np.ones((128, 1), np.float32)
    h["iota64"] = np.arange(64, dtype=np.float32).reshape(64, 1)
    cw = np.asarray(inp["L0_attn_conv_w"], np.float32)
    h["cw"] = np.stack([_prep_co(cw[t]) for t in range(3)])
    h["cb"] = _col_chunks(np.asarray(inp["L0_attn_conv_b"], np.float32))
    h["wq0"] = _prep_co(np.asarray(inp["L0_attn_wq"], np.float32).reshape(1024, 1024))
    h["wk0"] = _prep_co(np.asarray(inp["L0_attn_wk"], np.float32).reshape(1024, 1024))
    h["wv0r"] = np.ascontiguousarray(
        np.asarray(inp["L0_attn_wv"], np.float32).reshape(8, 128, 1024))
    h["wo0"] = _prep_co(np.asarray(inp["L0_attn_wo"], np.float32).reshape(1024, 1024))
    h["proj0"] = _prep_co(np.asarray(inp["L0_attn_proj_w"], np.float32))
    h["bq0"] = _col_chunks(np.asarray(inp["L0_attn_bq"], np.float32).reshape(-1))
    h["bk0"] = _col_chunks(np.asarray(inp["L0_attn_bk"], np.float32).reshape(-1))
    h["bv0b"] = np.ascontiguousarray(np.broadcast_to(
        np.asarray(inp["L0_attn_bv"], np.float32).reshape(1, 1024), (128, 1024)))
    h["bo0"] = _col_chunks(np.asarray(inp["L0_attn_bo"], np.float32))
    h["pb0"] = _col_chunks(np.asarray(inp["L0_attn_proj_b"], np.float32))
    tril = np.tril(np.ones((128, 128), bool))
    h["trilm"] = np.where(tril, 0.0, -1e9).astype(np.float32)
    return h


def kernel(**inp):
    x_full = np.asarray(inp["L0_attn_inputs"], np.float32)
    core_rows = [_tok_rows(c) for c in range(NCORES)]
    xT = [np.ascontiguousarray(x_full[c // 2][core_rows[c]].T)
          for c in range(NCORES)]
    h = _host_prep(inp)
    h3 = _host_prep3(inp, 0)
    percore = [_host_core3(c) for c in range(NCORES)]

    p1 = build_L1()
    in_maps = [{"xT": xT[c], "w0f": h["w0f"], "b0f": h["b0f"],
                "centT": h["centT"], "eye": h["eye"],
                "onesrow": h["onesrow"], "onescol": h["onescol"]}
               for c in range(NCORES)]
    r1 = run_bass_kernel_spmd(p1.nc, in_maps, core_ids=list(range(NCORES)))
    l1 = r1.results
    parts1 = np.concatenate([l1[c]["partials"] for c in range(NCORES)],
                            axis=0)

    p2 = build_L2()
    in_maps = [{"xT": xT[c], "partials1": parts1, "qs1": l1[c]["qs"],
                "dense1": l1[c]["dense"], "assign": h["assign"],
                "iota64": h["iota64"], "eye": h["eye"],
                "onesrow": h["onesrow"], "onescol": h["onescol"],
                "cw": h["cw"], "cb": h["cb"], "wq0": h["wq0"],
                "wk0": h["wk0"], "wv0r": h["wv0r"], "wo0": h["wo0"],
                "proj0": h["proj0"], "bq0": h["bq0"], "bk0": h["bk0"],
                "bv0b": h["bv0b"], "bo0": h["bo0"], "pb0": h["pb0"],
                "trilm": h["trilm"], "w0f": h["w0f"], "b0f": h["b0f"],
                "centT": h["centT"], "stamp": np.array([[MAGIC]], np.int32)}
               for c in range(NCORES)]
    r2 = run_bass_kernel_spmd(p2.nc, in_maps, core_ids=list(range(NCORES)))
    l2 = r2.results
    parts3 = np.concatenate([l2[c]["partials3"] for c in range(NCORES)],
                            axis=0)

    p3 = build_L3()
    in_maps = []
    for c in range(NCORES):
        cosb, sinb, maskb = percore[c]
        in_maps.append(
            {"xT3": l2[c]["x2"], "partials3": parts3, "qs3": l2[c]["qs3"],
             "dense3": l2[c]["dense3"], "assign": h["assign"],
             "iota64": h["iota64"], "eye": h["eye"],
             "onesrow": h["onesrow"], "onescol": h["onescol"],
             "w12p": h3["w12p"], "b12c": h3["b12c"], "w3p": h3["w3p"],
             "b3c": h3["b3c"], "wqp": h3["wqp"], "wkp": h3["wkp"],
             "wvp": h3["wvp"], "wop": h3["wop"], "cosb": cosb,
             "sinb": sinb, "maskb": maskb,
             "stamp3": np.array([[MAGIC3]], np.int32)})
    r3 = run_bass_kernel_spmd(p3.nc, in_maps, core_ids=list(range(NCORES)))

    out = np.empty((4, T, D), np.float32)
    for c in range(NCORES):
        out[c // 2][core_rows[c]] = r3.results[c]["xout"].T
    return out
